# version 02
# baseline (speedup 1.0000x reference)
import numpy as np
import ml_dtypes
import concourse.bass as bass
import concourse.mybir as mybir
from concourse.bass_utils import run_bass_kernel_spmd

F32 = mybir.dt.float32
BF16 = mybir.dt.bfloat16
AF = mybir.ActivationFunctionType
OP = mybir.AluOpType
AX = mybir.AxisListType

D = 2048
KC = 16
T = 1024
TP = 1152
TH = 1040
NT = 9
I = 8192
IC = 64
NPAIR = 16
SEQ = 4096
CR = float(1.5 * 2.0 ** 23)
NEG = -1.0e30
SLOTW = 144
TSBW = 9 * SLOTW
QKV_NT = [(0, 352), (352, 352), (704, 336)]
QKV_NT_V = [(0, 384), (384, 384), (768, 384)]


class Sem:
    def __init__(self, nc, name):
        self.h = nc.alloc_semaphore(name)
        self.n = 0


_COMPUTE_OPS = {
    "tensor_tensor", "tensor_scalar", "tensor_copy", "tensor_add", "tensor_sub",
    "tensor_mul", "tensor_max", "reduce_sum", "reduce_max", "tensor_reduce",
    "reciprocal", "activation", "copy", "mul", "add", "square", "sqrt", "sign",
    "memset", "memzero", "iota", "scalar_tensor_tensor", "tensor_single_scalar",
    "tensor_scalar_add", "tensor_scalar_sub", "tensor_scalar_mul",
    "tensor_scalar_max", "tensor_scalar_min", "select", "bn_stats", "bn_aggr",
}


class ES:

    def __init__(self, e, sem):
        self._e = e
        self._s = sem

    def __getattr__(self, name):
        fn = getattr(self._e, name)
        if name in _COMPUTE_OPS:
            s = self._s
            e = self._e

            def wrapped(*a, **k):
                if s.n > 0:
                    e.wait_ge(s.h, s.n)
                ins = fn(*a, **k)
                s.n += 1
                ins.then_inc(s.h, 1)
                return ins
            return wrapped
        if name == "dma_start":
            s = self._s
            e = self._e

            def wrapped_dma(*a, **k):
                if s.n > 0:
                    e.wait_ge(s.h, s.n)
                return fn(*a, **k)
            return wrapped_dma
        return fn


def build_nc():
    nc = bass.Bass("TRN2", target_bir_lowering=False, debug=False)

    xh = nc.dram_tensor("xh", [TP, D], F32, kind="ExternalInput").ap()
    wqkv = nc.dram_tensor("wqkv", [NPAIR, KC, 128, 384], F32, kind="ExternalInput").ap()
    wproj = nc.dram_tensor("wproj", [4, KC, 128, 512], F32, kind="ExternalInput").ap()
    wg_d = nc.dram_tensor("wg", [16, KC, 128, 512], F32, kind="ExternalInput").ap()
    wu_d = nc.dram_tensor("wu", [16, KC, 128, 512], F32, kind="ExternalInput").ap()
    wd_d = nc.dram_tensor("wd", [16, IC, 128, 128], F32, kind="ExternalInput").ap()
    pbh = nc.dram_tensor("pbh", [128, 16], F32, kind="ExternalInput").ap()
    lnwh = nc.dram_tensor("lnwh", [128, 64], F32, kind="ExternalInput").ap()
    mmulh = nc.dram_tensor("mmulh", [128, TSBW], BF16, kind="ExternalInput").ap()
    maddh = nc.dram_tensor("maddh", [2, 128, TSBW], BF16, kind="ExternalInput").ap()

    y = nc.dram_tensor("y", [T, D], F32, kind="ExternalOutput").ap()
    hscr = nc.dram_tensor("hscr", [IC, 128, T], F32, kind="Internal").ap()
    ascr = nc.dram_tensor("ascr", [KC, 128, T], F32, kind="Internal").ap()
    atscr = nc.dram_tensor("atscr", [KC, 128, T], BF16, kind="Internal").ap()

    A = nc.alloc_sbuf_tensor_at

    p = 16512
    def pal(name, shape, dtype):
        nonlocal p
        p = (p + 63) // 64 * 64
        t = A(name, shape, dtype, offset=p)
        esz = {BF16: 2, F32: 4}[dtype]
        p += int(np.prod(shape[1:])) * esz
        return t

    identf = pal("identf", [128, 128], F32)
    iof = pal("iof", [128, 128], F32)
    iop = pal("iop", [128, 1], F32)
    ones_bf = pal("ones_bf", [128, 128], BF16)
    ones_f = pal("ones_f", [128, 128], F32)
    ones1 = pal("ones1", [1, 128], F32)
    tmp128 = pal("tmp128", [128, 128], F32)
    c_round = pal("c_round", [128, 1], F32)
    c_nhalf = pal("c_nhalf", [128, 1], F32)
    mmul = pal("mmul", [128, TSBW], BF16)
    madd = pal("madd", [128, 2 * TSBW], BF16)
    pb_sb = pal("pb_sb", [128, 16], F32)
    lnw_sb = pal("lnw_sb", [128, 64], F32)
    partials = pal("partials", [128, 48], F32)
    swg = pal("swg", [128, 1], F32); iswg = pal("iswg", [128, 1], F32)
    swu = pal("swu", [128, 1], F32); iswu = pal("iswu", [128, 1], F32)
    swd = pal("swd", [128, 1], F32); iswd = pal("iswd", [128, 1], F32)
    ln_mean = pal("ln_mean", [128, NT], F32)
    ln_rstd = pal("ln_rstd", [128, NT], F32)
    ln_srow = pal("ln_srow", [128, NT], F32)
    isx_row = pal("isx_row", [1, TP], F32)
    ISX = pal("ISX", [128, T], F32)
    Mhi = pal("Mhi", [128, T], F32)
    Mlo = pal("Mlo", [128, T], F32)
    r_bc = pal("r_bc", [128, T], F32)
    RS = pal("RS", [128, T], F32)
    INVD = pal("INVD", [128, T], F32)
    sh_row = pal("sh_row", [1, T], F32)
    iv_row = pal("iv_row", [1, T], F32)
    amax_tok = pal("amax_tok", [128, 8], F32)
    rtok = pal("rtok", [128, 8], F32)
    mrms = pal("mrms", [128, 8], F32)
    shtok = pal("shtok", [128, 8], F32)
    ivtok = pal("ivtok", [128, 8], F32)
    negmax = pal("negmax", [128, 1], F32)
    rsum = pal("rsum", [128, 9], F32)
    rinv = pal("rinv", [128, 9], F32)
    tv1 = pal("tv1", [128, 8], F32)
    tv2 = pal("tv2", [128, 8], F32)
    assert p < 66512, p

    xnT = A("xnT", [128, KC * TP], BF16, offset=66560)
    qxT = A("qxT", [128, KC * TP], BF16, offset=66560)
    hq = A("hq", [128, IC * 512], BF16, offset=66560)
    ot3 = A("ot3", [128, D], F32, offset=132096)
    xtok = A("xtok", [128, D], F32, offset=103424)
    xn_t = A("xn_t", [128, D], F32, offset=111744)
    xnb_t = A("xnb_t", [128, D], BF16, offset=119936)
    sq_t = A("sq_t", [128, D], BF16, offset=124032)
    T_sb = A("T_sb", [128, TSBW], F32, offset=103424)
    E_bf = A("E_bf", [128, TSBW], BF16, offset=108672)
    P_bf = A("P_bf", [128, TSBW], BF16, offset=111296)
    PT_a = A("PT_a", [128, 8 * 128], BF16, offset=113920)
    PT_b = A("PT_b", [128, 8 * 128], BF16, offset=115968)
    ath0 = A("ath0", [64, 2 * T], BF16, offset=118016)
    ath1 = A("ath1", [64, 2 * T], BF16, offset=130304)
    wstage = A("wstage", [128, KC * 512], F32, offset=103424)
    wq_bf0 = A("wq_bf0", [128, KC * 384], BF16, offset=140352)
    wq_bf1 = A("wq_bf1", [128, KC * 384], BF16, offset=152640)
    qm0 = A("qm0", [128, T], BF16, offset=164928)
    qm1 = A("qm1", [128, T], BF16, offset=167040)
    kT_p = A("kT_p", [128, TH], BF16, offset=128192)
    vT_p = A("vT_p", [128, TP], BF16, offset=169152)
    atrb = A("atrb", [128, 2 * T], BF16, offset=140352)
    tern_g = A("tern_g", [128, KC * 512], BF16, offset=140352)
    tern_u = A("tern_u", [128, KC * 512], BF16, offset=156736)
    wdstage = A("wdstage", [128, IC * 128], F32, offset=140352)
    v_tok = A("v_tok", [128, NT * 128], BF16, offset=173120)
    f0st = A("f0st", [128, 2 * 2048], F32, offset=175552)
    astg = A("astg", [128, 4 * T], F32, offset=173120)
    xtok2 = A("xtok2", [128, D], F32, offset=173120)
    xn2 = A("xn2", [128, D], F32, offset=181312)
    qx2 = A("qx2", [128, D], BF16, offset=189568)
    hstage = A("hstage", [128, 4 * T], F32, offset=173120)
    sg_t = A("sg_t", [128, 512], F32, offset=189568)
    sgp_t = A("sgp_t", [128, 512], F32, offset=191616)
    hl_t = A("hl_t", [128, 512], F32, offset=193728)
    hsq_t = A("hsq_t", [128, 512], BF16, offset=195840)
    hrb = A("hrb", [128, 2 * 512], F32, offset=189568)
    tern_d = A("tern_d", [128, IC * 128], BF16, offset=173120)
    aprb = A("aprb", [128, 2 * 512], F32, offset=196928)
    fin_t = A("fin_t", [128, 512], F32, offset=201024)
    ot2 = A("ot2", [128, D], F32, offset=203136)
    pw_bf = A("pw_bf", [128, KC * 512], BF16, offset=203136)
    ot0 = A("ot0", [128, D], F32, offset=211328)
    ot1 = A("ot1", [128, D], F32, offset=219520)

    psA = nc.alloc_psum_tensor("psA", [128, 4096], F32)

    def bank(i, w=512):
        return psA[:, i * 512:i * 512 + w]

    sV = Sem(nc, "sV")
    sS = Sem(nc, "sS")
    sM = Sem(nc, "sM")
    sP = Sem(nc, "sP")
    sO = Sem(nc, "sO")
    sF0 = Sem(nc, "sF0")
    sVt = Sem(nc, "sVt")
    sVm = Sem(nc, "sVm")
    sXm = Sem(nc, "sXm")
    sXx = Sem(nc, "sXx")
    sXw = Sem(nc, "sXw")
    sXa = [Sem(nc, "sXa0"), Sem(nc, "sXa1")]
    sTp = Sem(nc, "sTp")
    sTt = Sem(nc, "sTt")
    sTv = Sem(nc, "sTv")
    sTat = [Sem(nc, "sTat0"), Sem(nc, "sTat1")]
    sTa4 = Sem(nc, "sTa4")
    sTh = Sem(nc, "sTh")
    sTr = [Sem(nc, "sTr0"), Sem(nc, "sTr1")]
    sTap = [Sem(nc, "sTap0"), Sem(nc, "sTap1")]
    sGw = [Sem(nc, "sGw0"), Sem(nc, "sGw1")]
    sGf = [Sem(nc, "sGf0"), Sem(nc, "sGf1")]
    sGp = Sem(nc, "sGp")

    ev = {}

    def mark(label, s):
        ev[label] = (s, s.n)

    def wv(eng, label):
        s, n = ev[label]
        if n > 0:
            eng.wait_ge(s.h, n)

    def inc(ins, s, k=1):
        s.n += k
        ins.then_inc(s.h, k)

    f0_list = []
    for wi in range(3):
        for g in range(16):
            for q in range(4):
                f0_list.append((wi, g, q))

    def f0_dma(e, qi):
        wi, g, q = f0_list[qi]
        if qi >= 2:
            e.wait_ge(sF0.h, qi - 1)
        wt = (wg_d, wu_d, wd_d)[wi]
        nk = 4 if wi < 2 else 16
        src = wt[g, q * nk:(q + 1) * nk, :, :]
        dst = f0st[:, (qi % 2) * 2048:(qi % 2) * 2048 + 2048].rearrange("p (k c) -> p k c", k=nk)
        ins = e.dma_start(out=dst, in_=src.rearrange("k p c -> p k c"))
        inc(ins, sGf[qi % 2], 16)
        mark("f0_dma_%d" % qi, sGf[qi % 2])

    def f0_red(e, qi):
        wi, g, q = f0_list[qi]
        wv(e, "f0_dma_%d" % qi)
        col = wi * 16 + g
        if q == 0:
            ins = e.tensor_reduce(out=partials[:, col:col + 1],
                                  in_=f0st[:, (qi % 2) * 2048:(qi % 2) * 2048 + 2048],
                                  axis=AX.X, op=OP.add, apply_absolute_value=True)
        else:
            e.tensor_reduce(out=tv1[:, 7:8], in_=f0st[:, (qi % 2) * 2048:(qi % 2) * 2048 + 2048],
                            axis=AX.X, op=OP.add, apply_absolute_value=True)
            ins = e.tensor_add(out=partials[:, col:col + 1], in0=partials[:, col:col + 1],
                               in1=tv1[:, 7:8])
        ins2 = e.engine_nop()
        inc(ins2, sF0, 1)

    with nc.Block() as blk:
        @blk.sync
        def _(e):
            ins = e.dma_start(out=mmul[:], in_=mmulh[:, :]); inc(ins, sXm, 16)
            ins = e.dma_start(out=madd[:].rearrange("p (a c) -> p a c", a=2),
                              in_=maddh.rearrange("a p c -> p a c")); inc(ins, sXm, 16)
            ins = e.dma_start(out=pb_sb[:], in_=pbh[:, :]); inc(ins, sXm, 16)
            ins = e.dma_start(out=lnw_sb[:], in_=lnwh[:, :]); inc(ins, sXm, 16)
        mark("init_dma", sXm)

        @blk.gpsimd
        def _(e):
            e = ES(e, sP)
            ins = e.iota(iof[:], [[1, 128]], channel_multiplier=0, allow_small_or_imprecise_dtypes=True)
            ins = e.iota(iop[:], [[0, 1]], channel_multiplier=1, allow_small_or_imprecise_dtypes=True)
            e.tensor_scalar(out=identf[:], in0=iof[:], scalar1=iop[:, 0:1], scalar2=None, op0=OP.is_equal)
            e.memset(ones_bf[:], 1.0)
            e.memset(ones_f[:], 1.0)
            e.memset(ones1[:], 1.0)
            e.memset(c_round[:], CR)
            e.memset(c_nhalf[:], -0.5)
            e.memset(qm0[:], 0.0)
            e.memset(qm1[:], 0.0)
            ins = e.memset(v_tok[:], 0.0)
        mark("init_gp", sP)

        for t in range(NT):
            @blk.sync
            def _(e, t=t):
                if t > 0:
                    wv(e, "a1_done_%d" % (t - 1))
                ins = e.dma_start(out=xtok[:], in_=xh[t * 128:(t + 1) * 128, :])
                inc(ins, sXx, 16)
            mark("a1_x_%d" % t, sXx)

            @blk.vector
            def _(e, t=t):
                e = ES(e, sV)
                if t == 0:
                    wv(e, "init_gp")
                wv(e, "a1_x_%d" % t)
                e.reduce_sum(out=tv1[:, 0:1], in_=xtok[:], axis=AX.X)
                ins = e.tensor_scalar(out=ln_mean[:, t:t + 1], in0=tv1[:, 0:1],
                                      scalar1=1.0 / D, scalar2=None, op0=OP.mult)
            mark("a1_mean_%d" % t, sV)

            @blk.scalar
            def _(e, t=t):
                e = ES(e, sS)
                if t == 0:
                    wv(e, "init_gp")
                wv(e, "a1_x_%d" % t)
                ins = e.activation(out=sq_t[:], in_=xtok[:], func=AF.Square, accum_out=tv2[:, 0:1])
            mark("a1_ssq_%d" % t, sS)

            @blk.vector
            def _(e, t=t):
                e = ES(e, sV)
                wv(e, "a1_ssq_%d" % t)
                wv(e, "a1_mean_%d" % t)
                e.tensor_scalar(out=tv2[:, 1:2], in0=tv2[:, 0:1], scalar1=1.0 / D, scalar2=1e-5,
                                op0=OP.mult, op1=OP.add)
                e.tensor_mul(out=tv2[:, 2:3], in0=ln_mean[:, t:t + 1], in1=ln_mean[:, t:t + 1])
                ins = e.tensor_sub(out=tv2[:, 3:4], in0=tv2[:, 1:2], in1=tv2[:, 2:3])
            mark("a1_var_%d" % t, sV)

            @blk.scalar
            def _(e, t=t):
                e = ES(e, sS)
                wv(e, "a1_var_%d" % t)
                e.activation(out=tv2[:, 4:5], in_=tv2[:, 3:4], func=AF.Ln)
                ins = e.activation(out=ln_rstd[:, t:t + 1], in_=tv2[:, 4:5], func=AF.Exp,
                                   scale=c_nhalf[:, 0:1])
            mark("a1_rstd_%d" % t, sS)

            @blk.vector
            def _(e, t=t):
                e = ES(e, sV)
                wv(e, "a1_rstd_%d" % t)
                ins = e.tensor_scalar(out=xn_t[:], in0=xtok[:], scalar1=ln_mean[:, t:t + 1],
                                      scalar2=ln_rstd[:, t:t + 1], op0=OP.subtract, op1=OP.mult)
                e.reduce_max(out=tv1[:, 1:2], in_=xn_t[:], axis=AX.X, apply_absolute_value=True)
                e.tensor_scalar(out=tv1[:, 2:3], in0=tv1[:, 1:2], scalar1=1e-5, scalar2=None, op0=OP.max)
                e.reciprocal(out=tv1[:, 3:4], in_=tv1[:, 2:3])
                e.tensor_scalar(out=ln_srow[:, t:t + 1], in0=tv1[:, 3:4], scalar1=127.0,
                                scalar2=None, op0=OP.mult)
                ins = e.tensor_scalar(out=tmp128[:], in0=ones_f[:],
                                      scalar1=tv1[:, 2:3], scalar2=1.0 / 127.0, op0=OP.mult, op1=OP.mult)
            mark("a1_xn_%d" % t, sV)

            @blk.tensor
            def _(e, t=t):
                wv(e, "a1_xn_%d" % t)
                if t == 0:
                    wv(e, "init_gp")
                if t > 0:
                    wv(e, "a1_isxrow_%d" % (t - 1))
                ins = e.transpose(psA[:, 3584:3584 + 128], tmp128[:], identf[:])
                inc(ins, sM, 1)
            mark("a1_isxT_%d" % t, sM)

            @blk.vector
            def _(e, t=t):
                e = ES(e, sV)
                wv(e, "a1_isxT_%d" % t)
                ins = e.tensor_copy(out=isx_row[0:1, t * 128:(t + 1) * 128],
                                    in_=psA[0:1, 3584:3584 + 128])
            mark("a1_isxrow_%d" % t, sV)

            @blk.scalar
            def _(e, t=t):
                e = ES(e, sS)
                wv(e, "a1_xn_%d" % t)
                if t > 0:
                    wv(e, "a1_done_%d" % (t - 1))
                ins = e.copy(out=xnb_t[:], in_=xn_t[:])
            mark("a1_cast_%d" % t, sS)

            @blk.scalar
            def _(e, t=t):
                e = ES(e, sS)
                wv(e, "a1_cast_%d" % t)
                for kc in range(KC):
                    ins = e.dma_start(out=xnT[:, kc * TP + t * 128: kc * TP + (t + 1) * 128],
                                      in_=xnb_t[:, kc * 128:(kc + 1) * 128], transpose=True)
                    inc(ins, sTt, 16)
            mark("a1_done_%d" % t, sTt)
        mark("a1_all", sTt)

        @blk.tensor
        def _(e):
            wv(e, "a1_isxrow_%d" % (NT - 1))
            e.matmul(psA[:, 3072:3584], ones1[:], isx_row[0:1, 8:520], start=True, stop=True)
            ins = e.matmul(psA[:, 3584:4096], ones1[:], isx_row[0:1, 520:1032], start=True, stop=True)
            inc(ins, sM, 1)
        mark("isx_mm", sM)

        @blk.vector
        def _(e):
            e = ES(e, sV)
            wv(e, "isx_mm")
            e.tensor_copy(out=ISX[:, 0:512], in_=psA[:, 3072:3584])
            ins = e.tensor_copy(out=ISX[:, 512:1024], in_=psA[:, 3584:4096])
        mark("isx_done", sV)

        f0_emitted = 0

        def emit_wq(pr):
            wbuf = (wq_bf0, wq_bf1)[pr % 2]

            @blk.gpsimd
            def _(e, pr=pr, wbuf=wbuf):
                e = ES(e, sP)
                if pr >= 2:
                    wv(e, "qkv_mm_%d" % (pr - 2))
                ins = e.dma_start(out=wbuf[:].rearrange("p (k c) -> p k c", k=KC),
                                  in_=wqkv[pr].rearrange("k p c -> p k c"))
                inc(ins, sGw[pr % 2], 16)
            mark("wq_%d" % pr, sGw[pr % 2])

        emit_wq(0)
        for pr in range(NPAIR):
            wbuf = (wq_bf0, wq_bf1)[pr % 2]
            if pr + 1 < NPAIR:
                emit_wq(pr + 1)

            @blk.gpsimd
            def _(e, pr=pr):
                e = ES(e, sP)
                for k in range(12):
                    qi = pr * 12 + k
                    if qi < 192:
                        f0_dma(e, qi)

            for mi in range(3):
                for ti, (n0, nn) in enumerate(QKV_NT if mi < 2 else QKV_NT_V):
                    u = mi * 3 + ti

                    @blk.tensor
                    def _(e, pr=pr, mi=mi, ti=ti, n0=n0, nn=nn, u=u, wbuf=wbuf):
                        if u == 0:
                            wv(e, "wq_%d" % pr)
                            if pr == 0:
                                wv(e, "a1_all")
                                wv(e, "isx_done")
                        gu = pr * 9 + u
                        if gu >= 2:
                            wv(e, "qkv_ev_%d" % (gu - 2))
                        for kc in range(KC):
                            ins = e.matmul(bank(gu % 2, nn),
                                           wbuf[:, kc * 384 + mi * 128: kc * 384 + (mi + 1) * 128],
                                           xnT[:, kc * TP + n0: kc * TP + n0 + nn],
                                           start=(kc == 0), stop=(kc == KC - 1))
                        inc(ins, sM, 1)
                    mark("qkv_ps_%d_%d" % (pr, u), sM)

                    @blk.vector
                    def _(e, pr=pr, mi=mi, n0=n0, nn=nn, u=u):
                        e = ES(e, sV)
                        gu = pr * 9 + u
                        wv(e, "qkv_ps_%d_%d" % (pr, u))
                        if mi == 0:
                            s0 = max(n0, 8) - n0
                            s1 = min(n0 + nn, 8 + T) - n0
                            d0 = n0 + s0 - 8
                            ins = e.tensor_scalar(out=qm0[0:64, d0:d0 + (s1 - s0)],
                                                  in0=bank(gu % 2, nn)[0:64, s0:s1],
                                                  scalar1=0.125, scalar2=None, op0=OP.mult)
                            ins = e.tensor_scalar(out=qm1[64:128, d0:d0 + (s1 - s0)],
                                                  in0=bank(gu % 2, nn)[64:128, s0:s1],
                                                  scalar1=0.125, scalar2=None, op0=OP.mult)
                        else:
                            dst = (None, kT_p, vT_p)[mi]
                            ins = e.tensor_copy(out=dst[:, n0:n0 + nn], in_=bank(gu % 2, nn))
                    mark("qkv_ev_%d" % (pr * 9 + u), sV)
            mark("qkv_mm_%d" % pr, sM)
            mark("qkv_ev_done_%d" % pr, sV)

            @blk.scalar
            def _(e, pr=pr):
                e = ES(e, sS)
                wv(e, "qkv_ev_done_%d" % pr)
                if pr > 0:
                    wv(e, "a3_pv_%d_1" % (pr - 1))
                for tt in range(NT):
                    ins = e.dma_start(out=v_tok[:, tt * 128: tt * 128 + 128],
                                      in_=vT_p[:, tt * 128: tt * 128 + 128], transpose=True)
                    inc(ins, sTv, 16)
            mark("vtok_%d" % pr, sTv)

            for half in range(2):
                hh = pr * 2 + half

                @blk.tensor
                def _(e, pr=pr, half=half, hh=hh):
                    if hh > 0:
                        wv(e, "a3_Tsb_%d" % (hh - 1))
                    for jj in range(4):
                        j = half * 4 + jj
                        for h in range(2):
                            slot = jj * 2 + h
                            off = 1024 + (slot // 3) * 512 + (slot % 3) * 144
                            qm = (qm0, qm1)[h]
                            ins = e.matmul(psA[:, off:off + 144],
                                           qm[:, j * 128: (j + 1) * 128],
                                           kT_p[:, j * 128: j * 128 + 144],
                                           start=True, stop=True)
                    off8 = 1024 + 2 * 512 + 2 * 144
                    ins = e.matmul(psA[:, off8:off8 + 144],
                                   qm0[:, half * 512: half * 512 + 128],
                                   kT_p[:, half * 512: half * 512 + 144],
                                   start=True, stop=True)
                    inc(ins, sM, 1)
                mark("a3_sc_%d" % hh, sM)

                @blk.vector
                def _(e, pr=pr, half=half, hh=hh):
                    e = ES(e, sV)
                    if hh == 0:
                        wv(e, "init_dma")
                    else:
                        wv(e, "a3_exp_%d" % (hh - 1))
                    wv(e, "a3_sc_%d" % hh)
                    psc = psA[:, 1024:1024 + 1536].rearrange("p (b s) -> p b s", b=3)[:, :, 0:432]
                    ins = e.tensor_tensor(out=T_sb[:].rearrange("p (b c) -> p b c", b=3), in0=psc,
                                          in1=mmul[:].rearrange("p (b c) -> p b c", b=3), op=OP.mult)
                    mark("a3_Tsb_%d" % hh, sV)
                    e.tensor_tensor(out=T_sb[:], in0=T_sb[:],
                                    in1=madd[:, half * TSBW:(half + 1) * TSBW], op=OP.add)
                    ins = e.reduce_max(out=negmax[:], in_=T_sb[:], axis=AX.X, negate=True)
                mark("a3_mx_%d" % hh, sV)

                @blk.scalar
                def _(e, hh=hh):
                    e = ES(e, sS)
                    wv(e, "a3_mx_%d" % hh)
                    ins = e.activation(out=E_bf[:], in_=T_sb[:], func=AF.Exp, bias=negmax[:, 0:1])
                mark("a3_exp_%d" % hh, sS)

                @blk.vector
                def _(e, hh=hh):
                    e = ES(e, sV)
                    wv(e, "a3_exp_%d" % hh)
                    if hh > 0:
                        wv(e, "a3_pt_%d" % (hh - 1))
                    e.reduce_sum(out=rsum[:], in_=E_bf[:].rearrange("p (s c) -> p s c", s=9), axis=AX.X)
                    e.reciprocal(out=rinv[:], in_=rsum[:])
                    for slot in range(9):
                        ins = e.tensor_scalar(out=P_bf[:, slot * 144:(slot + 1) * 144],
                                              in0=E_bf[:, slot * 144:(slot + 1) * 144],
                                              scalar1=rinv[:, slot:slot + 1], scalar2=None, op0=OP.mult)
                mark("a3_P_%d" % hh, sV)

                @blk.sync
                def _(e, hh=hh):
                    wv(e, "a3_P_%d" % hh)
                    if hh > 0:
                        wv(e, "a3_pv_%d_%d" % ((hh - 1) // 2, (hh - 1) % 2))
                    for slot in range(8):
                        ins = e.dma_start(out=PT_a[:, slot * 128:(slot + 1) * 128],
                                          in_=P_bf[:, slot * 144: slot * 144 + 128], transpose=True)
                        inc(ins, sTp, 16)
                        ins = e.dma_start(out=PT_b[:, slot * 128:(slot + 1) * 128],
                                          in_=P_bf[:, slot * 144 + 128: slot * 144 + 256], transpose=True)
                        inc(ins, sTp, 16)
                mark("a3_pt_%d" % hh, sTp)

                @blk.tensor
                def _(e, pr=pr, half=half, hh=hh):
                    wv(e, "a3_pt_%d" % hh)
                    wv(e, "vtok_%d" % pr)
                    if hh > 0:
                        wv(e, "a3_oev_%d" % (hh - 1))
                    for jj in range(4):
                        j = half * 4 + jj
                        for h in range(2):
                            slot = jj * 2 + h
                            out = psA[0:64, 2560 + slot * 128: 2560 + (slot + 1) * 128]
                            e.matmul(out, v_tok[:, j * 128 + 64 * h: j * 128 + 64 * h + 64],
                                     PT_a[:, slot * 128:(slot + 1) * 128], start=True, stop=False)
                            ins = e.matmul(out,
                                           v_tok[0:16, (j + 1) * 128 + 64 * h: (j + 1) * 128 + 64 * h + 64],
                                           PT_b[0:16, slot * 128:(slot + 1) * 128], start=False, stop=True)
                    inc(ins, sM, 1)
                mark("a3_pv_%d_%d" % (pr, half), sM)

                @blk.vector
                def _(e, pr=pr, half=half):
                    e = ES(e, sV)
                    wv(e, "a3_pv_%d_%d" % (pr, half))
                    if pr >= 2 and half == 0:
                        wv(e, "at_spill_%d" % (pr - 2))
                    for jj in range(4):
                        for h in range(2):
                            slot = jj * 2 + h
                            dst = (ath0, ath1)[h]
                            ins = e.tensor_copy(out=dst[:, (pr % 2) * T + half * 512 + jj * 128:
                                                        (pr % 2) * T + half * 512 + (jj + 1) * 128],
                                                in_=psA[0:64, 2560 + slot * 128: 2560 + (slot + 1) * 128])
                mark("a3_oev_%d" % (pr * 2 + half), sV)

            @blk.scalar
            def _(e, pr=pr):
                e = ES(e, sS)
                wv(e, "a3_oev_%d" % (pr * 2 + 1))
                ins = e.dma_start(out=atscr[pr][0:64, :], in_=ath0[:, (pr % 2) * T:(pr % 2) * T + T])
                inc(ins, sTat[pr % 2], 16)
                ins = e.dma_start(out=atscr[pr][64:128, :], in_=ath1[:, (pr % 2) * T:(pr % 2) * T + T])
                inc(ins, sTat[pr % 2], 16)
            mark("at_spill_%d" % pr, sTat[pr % 2])

            @blk.vector
            def _(e, pr=pr):
                e = ES(e, sV)
                nonlocal f0_emitted
                hi = min((pr + 1) * 12, 192)
                while f0_emitted < hi:
                    f0_red(e, f0_emitted)
                    f0_emitted += 1
        @blk.vector
        def _(e):
            e = ES(e, sV)
            nonlocal f0_emitted
            while f0_emitted < 192:
                f0_red(e, f0_emitted)
                f0_emitted += 1
        mark("f0_red_all", sV)

        @blk.vector
        def _(e):
            e = ES(e, sV)
            wv(e, "f0_red_all")
            e.reduce_sum(out=tv1[:, 0:1], in_=partials[:, 0:16], axis=AX.X)
            e.reduce_sum(out=tv1[:, 1:2], in_=partials[:, 16:32], axis=AX.X)
            ins = e.reduce_sum(out=tv1[:, 2:3], in_=partials[:, 32:48], axis=AX.X)
        mark("f0_sums", sV)

        @blk.tensor
        def _(e):
            wv(e, "f0_sums")
            wv(e, "a3_oev_%d" % (NPAIR * 2 - 1))
            e.matmul(psA[:, 3072:3073], ones_f[:], tv1[:, 0:1], start=True, stop=True)
            e.matmul(psA[:, 3073:3074], ones_f[:], tv1[:, 1:2], start=True, stop=True)
            ins = e.matmul(psA[:, 3074:3075], ones_f[:], tv1[:, 2:3], start=True, stop=True)
            inc(ins, sM, 1)
        mark("f0_bc", sM)

        @blk.vector
        def _(e):
            e = ES(e, sV)
            wv(e, "f0_bc")
            nel = float(I * D)
            for wi, (ssb, isb) in enumerate(((swg, iswg), (swu, iswu), (swd, iswd))):
                e.tensor_scalar(out=isb[:], in0=psA[:, 3072 + wi:3073 + wi],
                                scalar1=1.0 / nel, scalar2=1e-5, op0=OP.mult, op1=OP.max)
                ins = e.reciprocal(out=ssb[:], in_=isb[:])
        mark("f0_scales", sV)

        for g in range(4):
            @blk.gpsimd
            def _(e, g=g):
                e = ES(e, sP)
                if g == 0:
                    wv(e, "f0_red_all")
                else:
                    wv(e, "a4_mm_%d" % (g - 1))
                ins = e.dma_start(out=pw_bf[:].rearrange("p (k c) -> p k c", k=KC),
                                  in_=wproj[g].rearrange("k p c -> p k c"))
                inc(ins, sGp, 16)
            mark("a4_w_%d" % g, sGp)

            for kc in range(KC):
                @blk.sync
                def _(e, g=g, kc=kc):
                    wv(e, "at_spill_%d" % kc)
                    gk = g * KC + kc
                    if gk >= 2:
                        wv(e, "a4_mmk_%d" % (gk - 2))
                    ins = e.dma_start(out=atrb[:, (kc % 2) * T:(kc % 2) * T + T], in_=atscr[kc])
                    inc(ins, sXa[kc % 2], 16)
                mark("a4_rb_%d_%d" % (g, kc), sXa[kc % 2])

                @blk.tensor
                def _(e, g=g, kc=kc):
                    if kc == 0:
                        wv(e, "a4_w_%d" % g)
                        if g == 0:
                            wv(e, "f0_scales")
                        else:
                            wv(e, "a4_ev_%d" % (g - 1))
                    wv(e, "a4_rb_%d_%d" % (g, kc))
                    for mi in range(4):
                        for nt in range(2):
                            ins = e.matmul(bank(mi * 2 + nt),
                                           pw_bf[:, kc * 512 + mi * 128: kc * 512 + (mi + 1) * 128],
                                           atrb[:, (kc % 2) * T + nt * 512:
                                                (kc % 2) * T + (nt + 1) * 512],
                                           start=(kc == 0), stop=(kc == KC - 1))
                    inc(ins, sM, 1)
                mark("a4_mmk_%d" % (g * KC + kc), sM)
            mark("a4_mm_%d" % g, sM)

            @blk.vector
            def _(e, g=g):
                e = ES(e, sV)
                wv(e, "a4_mm_%d" % g)
                if g >= 1:
                    wv(e, "a4_spill_%d" % (g - 1))
                for mi in range(4):
                    for nt in range(2):
                        m = g * 4 + mi
                        ins = e.tensor_scalar(out=astg[:, mi * T + nt * 512: mi * T + (nt + 1) * 512],
                                              in0=bank(mi * 2 + nt), scalar1=pb_sb[:, m:m + 1],
                                              scalar2=None, op0=OP.add)
            mark("a4_ev_%d" % g, sV)

            @blk.scalar
            def _(e, g=g):
                e = ES(e, sS)
                wv(e, "a4_ev_%d" % g)
                ins = e.dma_start(out=ascr[g * 4:(g + 1) * 4].rearrange("k p c -> p k c"),
                                  in_=astg[:].rearrange("p (k c) -> p k c", k=4))
                inc(ins, sTa4, 16)
            mark("a4_spill_%d" % g, sTa4)

        for t in range(NT):
            @blk.sync
            def _(e, t=t):
                if t == 0:
                    wv(e, "a4_spill_3")
                else:
                    wv(e, "f1r_t_%d" % (t - 1))
                ins = e.dma_start(out=xtok2[:], in_=xh[t * 128:(t + 1) * 128, :])
                inc(ins, sXx, 16)
            mark("f1r_x_%d" % t, sXx)

            @blk.vector
            def _(e, t=t):
                e = ES(e, sV)
                wv(e, "f1r_x_%d" % t)
                ins = e.tensor_scalar(out=xn2[:], in0=xtok2[:], scalar1=ln_mean[:, t:t + 1],
                                      scalar2=ln_rstd[:, t:t + 1], op0=OP.subtract, op1=OP.mult)
            mark("f1r_xn_%d" % t, sV)

            @blk.scalar
            def _(e, t=t):
                e = ES(e, sS)
                wv(e, "f1r_xn_%d" % t)
                ins = e.activation(out=xn2[:], in_=xn2[:], func=AF.Identity,
                                   scale=ln_srow[:, t:t + 1], bias=c_round[:, 0:1])
            mark("f1r_y_%d" % t, sS)

            @blk.vector
            def _(e, t=t):
                e = ES(e, sV)
                wv(e, "f1r_y_%d" % t)
                ins = e.tensor_scalar(out=qx2[:], in0=xn2[:], scalar1=CR, scalar2=None, op0=OP.subtract)
            mark("f1r_qx_%d" % t, sV)

            @blk.scalar
            def _(e, t=t):
                e = ES(e, sS)
                wv(e, "f1r_qx_%d" % t)
                for kc in range(KC):
                    ins = e.dma_start(out=qxT[:, kc * TP + t * 128: kc * TP + (t + 1) * 128],
                                      in_=qx2[:, kc * 128:(kc + 1) * 128], transpose=True)
                    inc(ins, sTt, 16)
            mark("f1r_t_%d" % t, sTt)
        mark("f1r_all", sTt)

        for g in range(16):
            for which in range(2):
                lab = "%d_%d" % (g, which)
                warr = (wg_d, wu_d)[which]
                tdst = (tern_g, tern_u)[which]
                ssc = (swg, swu)[which]

                @blk.sync
                def _(e, g=g, which=which, warr=warr):
                    if which == 1:
                        wv(e, "f1_tern_%d_0" % g)
                    elif g > 0:
                        wv(e, "f1_tern_%d_1" % (g - 1))
                    else:
                        wv(e, "f1r_all")
                        wv(e, "at_spill_%d" % (NPAIR - 1))
                        wv(e, "a4_mm_3")
                    ins = e.dma_start(out=wstage[:].rearrange("p (k c) -> p k c", k=KC),
                                      in_=warr[g].rearrange("k p c -> p k c"))
                    inc(ins, sXw, 16)
                mark("f1_wdma_%s" % lab, sXw)

                @blk.scalar
                def _(e, g=g, which=which, lab=lab, ssc=ssc):
                    e = ES(e, sS)
                    wv(e, "f1_wdma_%s" % lab)
                    if g == 0 and which == 0:
                        wv(e, "f0_scales")
                    ins = e.activation(out=wstage[:], in_=wstage[:], func=AF.Identity,
                                       scale=ssc[:, 0:1], bias=c_round[:, 0:1])
                mark("f1_tern_s_%s" % lab, sS)

                @blk.vector
                def _(e, g=g, which=which, lab=lab, tdst=tdst):
                    e = ES(e, sV)
                    wv(e, "f1_tern_s_%s" % lab)
                    if g > 0:
                        wv(e, "f1_mm_%d_%d" % (g - 1, which))
                    e.tensor_scalar(out=wstage[:], in0=wstage[:], scalar1=CR, scalar2=1.0,
                                    op0=OP.subtract, op1=OP.min)
                    ins = e.tensor_scalar(out=tdst[:], in0=wstage[:], scalar1=-1.0,
                                          scalar2=None, op0=OP.max)
                mark("f1_tern_%s" % lab, sV)

            for mi in range(4):
                c = g * 4 + mi
                for nt in range(2):
                    u = mi * 2 + nt
                    gu = g * 8 + u
                    lab = "%d_%d" % (c, nt)

                    @blk.tensor
                    def _(e, g=g, mi=mi, nt=nt, u=u, gu=gu):
                        if u == 0:
                            wv(e, "f1_tern_%d_0" % g)
                            if g == 0:
                                wv(e, "f1r_all")
                                wv(e, "f1s_prev_none" if False else "f0_scales")
                        if gu >= 2:
                            wv(e, "f1_gg_ev_%d" % (gu - 2))
                        for kc in range(KC):
                            ins = e.matmul(bank(u % 2),
                                           tern_g[:, kc * 512 + mi * 128: kc * 512 + (mi + 1) * 128],
                                           qxT[:, kc * TP + 8 + nt * 512: kc * TP + 8 + (nt + 1) * 512],
                                           start=(kc == 0), stop=(kc == KC - 1))
                        inc(ins, sM, 1)
                    mark("f1_gp_%d" % gu, sM)

                    @blk.tensor
                    def _(e, g=g, mi=mi, nt=nt, u=u, gu=gu):
                        if u == 0:
                            wv(e, "f1_tern_%d_1" % g)
                        if gu >= 2:
                            wv(e, "f1_h_ev_%d" % (gu - 2))
                        for kc in range(KC):
                            ins = e.matmul(bank(2 + u % 2),
                                           tern_u[:, kc * 512 + mi * 128: kc * 512 + (mi + 1) * 128],
                                           qxT[:, kc * TP + 8 + nt * 512: kc * TP + 8 + (nt + 1) * 512],
                                           start=(kc == 0), stop=(kc == KC - 1))
                        inc(ins, sM, 1)
                    mark("f1_up_%d" % gu, sM)

                    @blk.vector
                    def _(e, u=u, nt=nt, gu=gu):
                        e = ES(e, sV)
                        wv(e, "f1_gp_%d" % gu)
                        ins = e.scalar_tensor_tensor(out=sg_t[:], in0=bank(u % 2), scalar=iswg[:, 0:1],
                                                     in1=ISX[:, nt * 512:(nt + 1) * 512],
                                                     op0=OP.mult, op1=OP.mult)
                    mark("f1_gg_%s" % lab, sV)
                    mark("f1_gg_ev_%d" % gu, sV)

                    @blk.scalar
                    def _(e, lab=lab):
                        e = ES(e, sS)
                        wv(e, "f1_gg_%s" % lab)
                        ins = e.activation(out=sg_t[:], in_=sg_t[:], func=AF.Silu)
                    mark("f1_sil_%s" % lab, sS)

                    @blk.gpsimd
                    def _(e, nt=nt, lab=lab):
                        e = ES(e, sP)
                        wv(e, "f1_sil_%s" % lab)
                        ins = e.tensor_scalar(out=sgp_t[:], in0=sg_t[:], scalar1=iswu[:, 0:1],
                                              scalar2=None, op0=OP.mult)
                        ins = e.tensor_tensor(out=sgp_t[:], in0=sgp_t[:],
                                              in1=ISX[:, nt * 512:(nt + 1) * 512], op=OP.mult)
                    mark("f1_sgp_%s" % lab, sP)

                    @blk.vector
                    def _(e, g=g, mi=mi, nt=nt, u=u, gu=gu, lab=lab):
                        e = ES(e, sV)
                        wv(e, "f1_sgp_%s" % lab)
                        wv(e, "f1_up_%d" % gu)
                        if g > 0 and u <= 1:
                            wv(e, "f1_spill_%d" % (g - 1))
                        ins = e.tensor_tensor(out=hstage[:, mi * T + nt * 512: mi * T + (nt + 1) * 512],
                                              in0=bank(2 + u % 2), in1=sgp_t[:], op=OP.mult)
                    mark("f1_h_%s" % lab, sV)
                    mark("f1_h_ev_%d" % gu, sV)

                    @blk.scalar
                    def _(e, g=g, mi=mi, nt=nt, gu=gu, lab=lab):
                        e = ES(e, sS)
                        wv(e, "f1_h_%s" % lab)
                        if gu >= 1:
                            wv(e, "f1_ssq_%d_%d" % ((gu - 1) // 2, (gu - 1) % 2))
                        ins = e.activation(out=hsq_t[:],
                                           in_=hstage[:, mi * T + nt * 512: mi * T + (nt + 1) * 512],
                                           func=AF.Square)
                    mark("f1_hsq_%s" % lab, sS)

                    @blk.tensor
                    def _(e, c=c, nt=nt, lab=lab):
                        wv(e, "f1_hsq_%s" % lab)
                        ins = e.matmul(psA[:, 3072 + nt * 512: 3072 + (nt + 1) * 512],
                                       ones_bf[:], hsq_t[:],
                                       start=(c == 0), stop=(c == IC - 1))
                        inc(ins, sM, 1)
                    mark("f1_ssq_%s" % lab, sM)

                    @blk.gpsimd
                    def _(e, c=c, mi=mi, nt=nt, gu=gu, lab=lab):
                        e = ES(e, sP)
                        wv(e, "f1_h_%s" % lab)
                        if gu >= 1:
                            wv(e, "f1_M_%d_%d" % ((gu - 1) // 2, (gu - 1) % 2))
                        ins = e.tensor_scalar(out=hl_t[:],
                                              in0=hstage[:, mi * T + nt * 512: mi * T + (nt + 1) * 512],
                                              scalar1=lnw_sb[:, c:c + 1], scalar2=None, op0=OP.mult)
                    mark("f1_hl_%s" % lab, sP)

                    @blk.vector
                    def _(e, c=c, nt=nt, lab=lab):
                        e = ES(e, sV)
                        wv(e, "f1_hl_%s" % lab)
                        if c == 0:
                            e.tensor_copy(out=Mhi[:, nt * 512:(nt + 1) * 512], in_=hl_t[:])
                            ins = e.tensor_copy(out=Mlo[:, nt * 512:(nt + 1) * 512], in_=hl_t[:])
                        else:
                            e.tensor_tensor(out=Mhi[:, nt * 512:(nt + 1) * 512],
                                            in0=Mhi[:, nt * 512:(nt + 1) * 512], in1=hl_t[:], op=OP.max)
                            ins = e.tensor_tensor(out=Mlo[:, nt * 512:(nt + 1) * 512],
                                                  in0=Mlo[:, nt * 512:(nt + 1) * 512], in1=hl_t[:], op=OP.min)
                    mark("f1_M_%s" % lab, sV)
            mark("f1_mm_%d_0" % g, sM)
            mark("f1_mm_%d_1" % g, sM)

            @blk.scalar
            def _(e, g=g):
                e = ES(e, sS)
                for mi in range(4):
                    for nt in range(2):
                        wv(e, "f1_hsq_%d_%d" % (g * 4 + mi, nt))
                        wv(e, "f1_M_%d_%d" % (g * 4 + mi, nt))
                ins = e.dma_start(out=hscr[g * 4:(g + 1) * 4].rearrange("k p c -> p k c"),
                                  in_=hstage[:].rearrange("p (k c) -> p k c", k=4))
                inc(ins, sTh, 16)
            mark("f1_spill_%d" % g, sTh)

        @blk.vector
        def _(e):
            e = ES(e, sV)
            wv(e, "f1_ssq_63_1")
            for mi in range(4):
                wv(e, "f1_M_%d_1" % (60 + mi))
            e.tensor_scalar(out=r_bc[:, 0:512], in0=psA[:, 3072:3584], scalar1=1.0 / I,
                            scalar2=1e-6, op0=OP.mult, op1=OP.add)
            ins = e.tensor_scalar(out=r_bc[:, 512:1024], in0=psA[:, 3584:4096], scalar1=1.0 / I,
                                  scalar2=1e-6, op0=OP.mult, op1=OP.add)
            e.tensor_scalar(out=Mlo[:], in0=Mlo[:], scalar1=-1.0, scalar2=None, op0=OP.mult)
            ins = e.tensor_tensor(out=Mhi[:], in0=Mhi[:], in1=Mlo[:], op=OP.max)
        mark("f1s_pre", sV)

        @blk.scalar
        def _(e):
            e = ES(e, sS)
            wv(e, "f1s_pre")
            e.activation(out=r_bc[:], in_=r_bc[:], func=AF.Ln)
            ins = e.activation(out=r_bc[:], in_=r_bc[:], func=AF.Exp, scale=c_nhalf[:, 0:1])
        mark("f1s_rbc", sS)

        @blk.tensor
        def _(e):
            wv(e, "f1s_pre")
            for b in range(8):
                if b >= 2:
                    e.wait_ge(sVm.h, b - 1)
                ins = e.transpose(psA[:, 2048 + (b % 2) * 512: 2048 + (b % 2) * 512 + 128],
                                  Mhi[:, b * 128:(b + 1) * 128], identf[:])
                inc(ins, sM, 1)
                mark("f1s_mt_%d" % b, sM)

        @blk.vector
        def _(e):
            e = ES(e, sV)
            for b in range(8):
                wv(e, "f1s_mt_%d" % b)
                ins = e.reduce_max(out=amax_tok[:, b:b + 1],
                                   in_=psA[:, 2048 + (b % 2) * 512: 2048 + (b % 2) * 512 + 128],
                                   axis=AX.X)
                ins2 = e.engine_nop()
                inc(ins2, sVm, 1)
        mark("f1s_am", sV)

        @blk.tensor
        def _(e):
            wv(e, "f1s_rbc")
            wv(e, "f1s_am")
            for b in range(8):
                ins = e.transpose(psA[:, 2048 + b * 128: 2048 + b * 128 + 128],
                                  r_bc[:, b * 128:(b + 1) * 128], identf[:])
                inc(ins, sM, 1)
                mark("f1s_rt_%d" % b, sM)

        @blk.vector
        def _(e):
            e = ES(e, sV)
            wv(e, "f1s_rt_%d" % 7)
            for b in range(8):
                ins = e.tensor_copy(out=rtok[:, b:b + 1],
                                    in_=psA[:, 2048 + b * 128: 2048 + b * 128 + 1])
            mark("f1s_rtok", sV)
            e.tensor_tensor(out=mrms[:], in0=rtok[:], in1=amax_tok[:], op=OP.mult)
            e.tensor_scalar(out=mrms[:], in0=mrms[:], scalar1=1e-5, scalar2=None, op0=OP.max)
            e.reciprocal(out=shtok[:], in_=mrms[:])
            e.tensor_scalar(out=shtok[:], in0=shtok[:], scalar1=127.0, scalar2=None, op0=OP.mult)
            e.tensor_scalar(out=ivtok[:], in0=mrms[:], scalar1=1.0 / 127.0,
                            scalar2=None, op0=OP.mult)
            ins = e.tensor_scalar(out=ivtok[:], in0=ivtok[:], scalar1=iswd[:, 0:1],
                                  scalar2=None, op0=OP.mult)
        mark("f1s_tok", sV)

        for b in range(8):
            for which, (colsrc, rowdst) in enumerate(((shtok, sh_row), (ivtok, iv_row))):
                lab = "%d_%d" % (b, which)

                @blk.vector
                def _(e, b=b, which=which, colsrc=colsrc, lab=lab):
                    e = ES(e, sV)
                    wv(e, "f1s_tok")
                    if b * 2 + which >= 2:
                        wv(e, "f1s_row_%d_%d" % ((b * 2 + which - 2) // 2, (b * 2 + which - 2) % 2))
                    ins = e.tensor_scalar(out=tmp128[:], in0=ones_f[:], scalar1=colsrc[:, b:b + 1],
                                          scalar2=None, op0=OP.mult)
                mark("f1s_bcc_%s" % lab, sV)

                @blk.tensor
                def _(e, b=b, which=which, lab=lab):
                    wv(e, "f1s_bcc_%s" % lab)
                    ins = e.transpose(psA[:, 2048 + ((b * 2 + which) % 2) * 512:
                                          2048 + ((b * 2 + which) % 2) * 512 + 128],
                                      tmp128[:], identf[:])
                    inc(ins, sM, 1)
                mark("f1s_ct_%s" % lab, sM)

                @blk.vector
                def _(e, b=b, which=which, rowdst=rowdst, lab=lab):
                    e = ES(e, sV)
                    wv(e, "f1s_ct_%s" % lab)
                    ins = e.tensor_copy(out=rowdst[0:1, b * 128:(b + 1) * 128],
                                        in_=psA[0:1, 2048 + ((b * 2 + which) % 2) * 512:
                                                2048 + ((b * 2 + which) % 2) * 512 + 128])
                mark("f1s_row_%s" % lab, sV)
        mark("f1s_rows", sV)

        @blk.tensor
        def _(e):
            wv(e, "f1s_rows")
            e.matmul(psA[:, 0:512], ones1[:], sh_row[0:1, 0:512], start=True, stop=True)
            e.matmul(psA[:, 512:1024], ones1[:], sh_row[0:1, 512:1024], start=True, stop=True)
            e.matmul(psA[:, 1024:1536], ones1[:], iv_row[0:1, 0:512], start=True, stop=True)
            ins = e.matmul(psA[:, 1536:2048], ones1[:], iv_row[0:1, 512:1024], start=True, stop=True)
            inc(ins, sM, 1)
        mark("f1s_bc", sM)

        @blk.vector
        def _(e):
            e = ES(e, sV)
            wv(e, "f1s_bc")
            e.tensor_tensor(out=RS[:, 0:512], in0=r_bc[:, 0:512], in1=psA[:, 0:512], op=OP.mult)
            e.tensor_tensor(out=RS[:, 512:1024], in0=r_bc[:, 512:1024], in1=psA[:, 512:1024], op=OP.mult)
            e.tensor_copy(out=INVD[:, 0:512], in_=psA[:, 1024:1536])
            ins = e.tensor_copy(out=INVD[:, 512:1024], in_=psA[:, 1536:2048])
        mark("f1s_done", sV)

        for half in range(2):
            c0 = half * 512
            for c in range(IC):
                @blk.scalar
                def _(e, half=half, c=c, c0=c0):
                    e = ES(e, sS)
                    if half == 0 and c < 2:
                        wv(e, "f1_spill_15")
                        wv(e, "f1s_done")
                    else:
                        lab = ("f2_q_%d_%d" % (half, c - 2)) if c >= 2 else \
                              ("f2_q_%d_%d" % (half - 1, IC - 2 + c))
                        wv(e, lab)
                        if half == 1 and c < 2:
                            wv(e, "f3_mm_0_%d" % (KC - 1))
                    ins = e.dma_start(out=hrb[:, (c % 2) * 512:(c % 2) * 512 + 512],
                                      in_=hscr[c][:, c0:c0 + 512])
                    inc(ins, sTr[c % 2], 16)
                mark("f2_rb_%d_%d" % (half, c), sTr[c % 2])

                @blk.gpsimd
                def _(e, half=half, c=c, c0=c0):
                    e = ES(e, sP)
                    wv(e, "f2_rb_%d_%d" % (half, c))
                    hc = half * IC + c
                    if hc >= 2:
                        wv(e, "f2_q_%d_%d" % ((hc - 2) // IC, (hc - 2) % IC))
                    buf = (hl_t, sgp_t)[c % 2]
                    ins = e.tensor_scalar(out=buf[:], in0=hrb[:, (c % 2) * 512:(c % 2) * 512 + 512],
                                          scalar1=lnw_sb[:, c:c + 1], scalar2=None, op0=OP.mult)
                    ins = e.tensor_tensor(out=buf[:], in0=buf[:], in1=RS[:, c0:c0 + 512], op=OP.mult)
                mark("f2_hl_%d_%d" % (half, c), sP)

                @blk.vector
                def _(e, half=half, c=c):
                    e = ES(e, sV)
                    wv(e, "f2_hl_%d_%d" % (half, c))
                    buf = (hl_t, sgp_t)[c % 2]
                    ins = e.tensor_scalar(out=hq[:, c * 512:(c + 1) * 512], in0=buf[:],
                                          scalar1=CR, scalar2=CR, op0=OP.add, op1=OP.subtract)
                mark("f2_q_%d_%d" % (half, c), sV)
            mark("f2_done_%d" % half, sV)

            for b in range(4):
                @blk.sync
                def _(e, half=half, b=b):
                    if half > 0:
                        wv(e, "out_all_%d" % (half - 1))
                    else:
                        wv(e, "f1_tern_15_1")
                    tb = half * 4 + b
                    ins = e.dma_start(out=(ot0, ot1, ot2, ot3)[b][:],
                                      in_=xh[8 + tb * 128: 8 + (tb + 1) * 128, :])
                    inc(ins, sXx, 16)
                mark("f3_x_%d_%d" % (half, b), sXx)

            for m in range(KC):
                lab = "%d_%d" % (half, m)
                hm = half * KC + m

                @blk.sync
                def _(e, half=half, m=m, hm=hm):
                    if hm > 0:
                        wv(e, "f3_tern_%d_%d" % ((hm - 1) // KC, (hm - 1) % KC))
                    else:
                        wv(e, "f2_done_0")
                    ins = e.dma_start(out=wdstage[:].rearrange("p (k c) -> p k c", k=IC),
                                      in_=wd_d[m].rearrange("k p c -> p k c"))
                    inc(ins, sXw, 16)
                mark("f3_wdma_%s" % lab, sXw)

                @blk.scalar
                def _(e, lab=lab, hm=hm):
                    e = ES(e, sS)
                    wv(e, "f3_wdma_%s" % lab)
                    ins = e.activation(out=wdstage[:], in_=wdstage[:], func=AF.Identity,
                                       scale=swd[:, 0:1], bias=c_round[:, 0:1])
                mark("f3_tern_s_%s" % lab, sS)

                @blk.vector
                def _(e, lab=lab, hm=hm):
                    e = ES(e, sV)
                    wv(e, "f3_tern_s_%s" % lab)
                    if hm > 0:
                        wv(e, "f3_mm_%d_%d" % ((hm - 1) // KC, (hm - 1) % KC))
                    e.tensor_scalar(out=wdstage[:], in0=wdstage[:], scalar1=CR, scalar2=1.0,
                                    op0=OP.subtract, op1=OP.min)
                    ins = e.tensor_scalar(out=tern_d[:], in0=wdstage[:], scalar1=-1.0,
                                          scalar2=None, op0=OP.max)
                mark("f3_tern_%s" % lab, sV)

                @blk.scalar
                def _(e, half=half, m=m, hm=hm, c0=c0):
                    e = ES(e, sS)
                    if hm == 0:
                        wv(e, "a4_spill_3")
                    if hm >= 2:
                        wv(e, "f3_fin_%d_%d" % ((hm - 2) // KC, (hm - 2) % KC))
                    ins = e.dma_start(out=aprb[:, (m % 2) * 512:(m % 2) * 512 + 512],
                                      in_=ascr[m][:, c0:c0 + 512])
                    inc(ins, sTap[m % 2], 16)
                mark("f3_ap_%s" % lab, sTap[m % 2])

                @blk.tensor
                def _(e, half=half, m=m, lab=lab, hm=hm):
                    wv(e, "f3_tern_%s" % lab)
                    wv(e, "f2_done_%d" % half)
                    if hm >= 2:
                        wv(e, "f3_ev_%d_%d" % ((hm - 2) // KC, (hm - 2) % KC))
                    for kc in range(IC):
                        ins = e.matmul(bank(m % 2), tern_d[:, kc * 128:(kc + 1) * 128],
                                       hq[:, kc * 512:(kc + 1) * 512],
                                       start=(kc == 0), stop=(kc == IC - 1))
                    inc(ins, sM, 1)
                mark("f3_mm_%s" % lab, sM)

                @blk.vector
                def _(e, half=half, m=m, lab=lab, c0=c0):
                    e = ES(e, sV)
                    wv(e, "f3_mm_%s" % lab)
                    wv(e, "f3_ap_%s" % lab)
                    e.tensor_tensor(out=fin_t[:], in0=bank(m % 2), in1=INVD[:, c0:c0 + 512], op=OP.mult)
                    ins = e.tensor_tensor(out=fin_t[:], in0=fin_t[:],
                                          in1=aprb[:, (m % 2) * 512:(m % 2) * 512 + 512], op=OP.add)
                mark("f3_fin_%s" % lab, sV)
                mark("f3_ev_%s" % lab, sV)

                @blk.tensor
                def _(e, half=half, m=m, lab=lab, hm=hm):
                    wv(e, "f3_fin_%s" % lab)
                    for b in range(4):
                        gidx = hm * 4 + b
                        if gidx >= 2:
                            e.wait_ge(sVt.h, gidx - 1)
                        ins = e.transpose(psA[:, 2048 + (gidx % 2) * 512: 2048 + (gidx % 2) * 512 + 128],
                                          fin_t[:, b * 128:(b + 1) * 128], identf[:])
                        inc(ins, sM, 1)
                        mark("f3_tt_%d_%d_%d" % (half, m, b), sM)

                @blk.vector
                def _(e, half=half, m=m, hm=hm):
                    e = ES(e, sV)
                    for b in range(4):
                        gidx = hm * 4 + b
                        wv(e, "f3_tt_%d_%d_%d" % (half, m, b))
                        wv(e, "f3_x_%d_%d" % (half, 3))
                        ot = (ot0, ot1, ot2, ot3)[b]
                        ins = e.tensor_tensor(out=ot[:, m * 128:(m + 1) * 128],
                                              in0=psA[:, 2048 + (gidx % 2) * 512:
                                                      2048 + (gidx % 2) * 512 + 128],
                                              in1=ot[:, m * 128:(m + 1) * 128], op=OP.add)
                        ins2 = e.engine_nop()
                        inc(ins2, sVt, 1)
                        mark("f3_ta_%d_%d_%d" % (half, m, b), sV)

            for b in range(4):
                @blk.scalar
                def _(e, half=half, b=b):
                    e = ES(e, sS)
                    for m in range(KC):
                        wv(e, "f3_ta_%d_%d_%d" % (half, m, b))
                    tb = half * 4 + b
                    ins = e.dma_start(out=y[tb * 128:(tb + 1) * 128, :],
                                      in_=(ot0, ot1, ot2, ot3)[b][:])
                    inc(ins, sO, 16)
                mark("out_%d_%d" % (half, b), sO)
            mark("out_all_%d" % half, sO)

        @blk.scalar
        def _(e):
            e = ES(e, sS)
            e.wait_ge(sO.h, sO.n)

    return nc


_NC_CACHE = None


def _get_nc():
    global _NC_CACHE
    if _NC_CACHE is None:
        _NC_CACHE = build_nc()
    return _NC_CACHE


def _host_prep(inputs):
    x = np.ascontiguousarray(np.asarray(inputs["L0_attn_x"], dtype=np.float32))
    qkv_w = np.asarray(inputs["L0_attn_qkv_w"], dtype=np.float32)
    proj_w = np.asarray(inputs["L0_attn_proj_w"], dtype=np.float32)
    proj_b = np.asarray(inputs["L0_attn_proj_b"], dtype=np.float32)
    w_gate = np.asarray(inputs["L0_ffn_w_gate"], dtype=np.float32)
    w_up = np.asarray(inputs["L0_ffn_w_up"], dtype=np.float32)
    w_down = np.asarray(inputs["L0_ffn_w_down"], dtype=np.float32)
    lnw = np.asarray(inputs["L0_ffn_ln_weight"], dtype=np.float32)

    wqkv = np.ascontiguousarray(
        qkv_w.reshape(3, 16, 128, D).transpose(1, 3, 0, 2).reshape(NPAIR, D, 384)
        .reshape(NPAIR, KC, 128, 384))
    wproj = np.ascontiguousarray(proj_w.T.reshape(KC, 128, 4, 512).transpose(2, 0, 1, 3))
    wg_a = np.ascontiguousarray(w_gate.T.reshape(KC, 128, 16, 512).transpose(2, 0, 1, 3))
    wu_a = np.ascontiguousarray(w_up.T.reshape(KC, 128, 16, 512).transpose(2, 0, 1, 3))
    wd_a = np.ascontiguousarray(w_down.T.reshape(IC, 128, 16, 128).transpose(2, 0, 1, 3))
    pbh = np.ascontiguousarray(proj_b.reshape(16, 128).T)
    lnwh = np.ascontiguousarray(lnw.reshape(64, 128).T)

    pp = np.arange(128)[:, None]
    ff = np.arange(SLOTW)[None, :]
    rel = ff - pp - 8
    band = (rel >= -8) & (rel < 8)
    mband = np.where(band, (8.0 - np.abs(rel) / 2.0) / 8.0, 0.0).astype(np.float32)
    mmulh = np.zeros((128, TSBW), np.float32)
    for s in range(8):
        mmulh[:, s * SLOTW:(s + 1) * SLOTW] = mband
    mmulh = mmulh.astype(ml_dtypes.bfloat16)

    in_maps = []
    for cidx in range(8):
        b, q0 = cidx // 4, (cidx % 4) * T
        xhc = np.zeros((TP, D), np.float32)
        klo = q0 - 8
        lo = max(klo, 0)
        hi = min(klo + TP, SEQ)
        xhc[lo - klo: hi - klo] = x[b, lo:hi]

        maddh = np.full((2, 128, TSBW), NEG, np.float32)
        for half in range(2):
            for jj in range(4):
                j = half * 4 + jj
                ig = q0 + j * 128 + pp
                jg = klo + j * 128 + ff
                valid = band & (jg >= 0) & (jg < SEQ)
                Aa = np.where(valid, 0.0, NEG).astype(np.float32)
                for h in range(2):
                    s = jj * 2 + h
                    maddh[half, :, s * SLOTW:(s + 1) * SLOTW] = Aa
            maddh[half, :, 8 * SLOTW] = 0.0
        maddh = maddh.astype(ml_dtypes.bfloat16)

        in_maps.append({
            "xh": xhc, "wqkv": wqkv, "wproj": wproj, "wg": wg_a, "wu": wu_a,
            "wd": wd_a, "pbh": pbh, "lnwh": lnwh, "mmulh": mmulh, "maddh": maddh,
        })
    return in_maps


def kernel(**inputs):
    in_maps = _host_prep(inputs)
    nc = _get_nc()
    res = run_bass_kernel_spmd(nc, in_maps, core_ids=list(range(8)))
    out = np.stack([r["y"] for r in res.results], axis=0)
    return out.reshape(2, SEQ, D)


# version 03
# speedup vs baseline: 1.2734x; 1.2734x over previous
import numpy as np
import ml_dtypes
import concourse.bass as bass
import concourse.mybir as mybir
from concourse.bass_utils import run_bass_kernel_spmd

F32 = mybir.dt.float32
BF16 = mybir.dt.bfloat16
AF = mybir.ActivationFunctionType
OP = mybir.AluOpType
AX = mybir.AxisListType

D = 2048
KC = 16
T = 1024
TP = 1152
TH = 1040
NT = 9
I = 8192
IC = 64
NPAIR = 16
SEQ = 4096
CR = float(1.5 * 2.0 ** 23)
NEG = -1.0e30
SLOTW = 144
TSBW = 9 * SLOTW
QKV_NT = [(0, 352), (352, 352), (704, 336)]
QKV_NT_V = [(0, 384), (384, 384), (768, 384)]


class Sem:
    def __init__(self, nc, name):
        self.h = nc.alloc_semaphore(name)
        self.n = 0


_COMPUTE_OPS = {
    "tensor_tensor", "tensor_scalar", "tensor_copy", "tensor_add", "tensor_sub",
    "tensor_mul", "tensor_max", "reduce_sum", "reduce_max", "tensor_reduce",
    "reciprocal", "activation", "copy", "mul", "add", "square", "sqrt", "sign",
    "memset", "memzero", "iota", "scalar_tensor_tensor", "tensor_single_scalar",
    "tensor_scalar_add", "tensor_scalar_sub", "tensor_scalar_mul",
    "tensor_scalar_max", "tensor_scalar_min", "select", "bn_stats", "bn_aggr",
}


class ES:

    def __init__(self, e, sem):
        self._e = e
        self._s = sem

    def __getattr__(self, name):
        fn = getattr(self._e, name)
        if name in _COMPUTE_OPS:
            s = self._s
            e = self._e

            def wrapped(*a, **k):
                if s.n > 0:
                    e.wait_ge(s.h, s.n)
                ins = fn(*a, **k)
                s.n += 1
                ins.then_inc(s.h, 1)
                return ins
            return wrapped
        if name == "dma_start":
            s = self._s
            e = self._e

            def wrapped_dma(*a, **k):
                if s.n > 0:
                    e.wait_ge(s.h, s.n)
                return fn(*a, **k)
            return wrapped_dma
        return fn


def build_nc():
    nc = bass.Bass("TRN2", target_bir_lowering=False, debug=False)

    xh = nc.dram_tensor("xh", [TP, D], F32, kind="ExternalInput").ap()
    wqkv = nc.dram_tensor("wqkv", [NPAIR, 128, KC * 384], F32, kind="ExternalInput").ap()
    wproj = nc.dram_tensor("wproj", [4, 128, KC * 512], F32, kind="ExternalInput").ap()
    wg_d = nc.dram_tensor("wg", [16, KC, 128, 512], F32, kind="ExternalInput").ap()
    wu_d = nc.dram_tensor("wu", [16, KC, 128, 512], F32, kind="ExternalInput").ap()
    wd_d = nc.dram_tensor("wd", [16, IC, 128, 128], F32, kind="ExternalInput").ap()
    pbh = nc.dram_tensor("pbh", [128, 16], F32, kind="ExternalInput").ap()
    lnwh = nc.dram_tensor("lnwh", [128, 64], F32, kind="ExternalInput").ap()
    mmulh = nc.dram_tensor("mmulh", [128, TSBW], BF16, kind="ExternalInput").ap()
    maddh = nc.dram_tensor("maddh", [2, 128, TSBW], BF16, kind="ExternalInput").ap()

    y = nc.dram_tensor("y", [T, D], F32, kind="ExternalOutput").ap()
    hscr = nc.dram_tensor("hscr", [IC, 128, T], F32, kind="Internal").ap()
    ascr = nc.dram_tensor("ascr", [KC, 128, T], F32, kind="Internal").ap()
    atscr = nc.dram_tensor("atscr", [KC, 128, T], BF16, kind="Internal").ap()

    A = nc.alloc_sbuf_tensor_at

    p = 16512
    def pal(name, shape, dtype):
        nonlocal p
        p = (p + 63) // 64 * 64
        t = A(name, shape, dtype, offset=p)
        esz = {BF16: 2, F32: 4}[dtype]
        p += int(np.prod(shape[1:])) * esz
        return t

    identf = pal("identf", [128, 128], F32)
    iof = pal("iof", [128, 128], F32)
    iop = pal("iop", [128, 1], F32)
    ones_bf = pal("ones_bf", [128, 128], BF16)
    ones_f = pal("ones_f", [128, 128], F32)
    ones1 = pal("ones1", [1, 128], F32)
    tmp128 = pal("tmp128", [128, 128], F32)
    c_round = pal("c_round", [128, 1], F32)
    c_nhalf = pal("c_nhalf", [128, 1], F32)
    mmul = pal("mmul", [128, TSBW], BF16)
    madd = pal("madd", [128, 2 * TSBW], BF16)
    pb_sb = pal("pb_sb", [128, 16], F32)
    lnw_sb = pal("lnw_sb", [128, 64], F32)
    partials = pal("partials", [128, 48], F32)
    swg = pal("swg", [128, 1], F32); iswg = pal("iswg", [128, 1], F32)
    swu = pal("swu", [128, 1], F32); iswu = pal("iswu", [128, 1], F32)
    swd = pal("swd", [128, 1], F32); iswd = pal("iswd", [128, 1], F32)
    ln_mean = pal("ln_mean", [128, NT], F32)
    ln_rstd = pal("ln_rstd", [128, NT], F32)
    ln_srow = pal("ln_srow", [128, NT], F32)
    isx_row = pal("isx_row", [1, TP], F32)
    ISX = pal("ISX", [128, T], F32)
    Mhi = pal("Mhi", [128, T], F32)
    Mlo = pal("Mlo", [128, T], F32)
    r_bc = pal("r_bc", [128, T], F32)
    RS = pal("RS", [128, T], F32)
    INVD = pal("INVD", [128, T], F32)
    sh_row = pal("sh_row", [1, T], F32)
    iv_row = pal("iv_row", [1, T], F32)
    amax_tok = pal("amax_tok", [128, 8], F32)
    rtok = pal("rtok", [128, 8], F32)
    mrms = pal("mrms", [128, 8], F32)
    shtok = pal("shtok", [128, 8], F32)
    ivtok = pal("ivtok", [128, 8], F32)
    negmax = pal("negmax", [128, 1], F32)
    rsum = pal("rsum", [128, 9], F32)
    rinv = pal("rinv", [128, 9], F32)
    tv1 = pal("tv1", [128, 8], F32)
    tv2 = pal("tv2", [128, 8], F32)
    assert p < 66512, p

    xnT = A("xnT", [128, KC * TP], BF16, offset=66560)
    qxT = A("qxT", [128, KC * TP], BF16, offset=66560)
    hq = A("hq", [128, IC * 512], BF16, offset=66560)
    ot3 = A("ot3", [128, D], F32, offset=132096)
    xtok = A("xtok", [128, D], F32, offset=103424)
    xn_t = A("xn_t", [128, D], F32, offset=111744)
    xnb_t = A("xnb_t", [128, D], BF16, offset=119936)
    sq_t = A("sq_t", [128, D], BF16, offset=124032)
    T_sb = A("T_sb", [128, TSBW], F32, offset=103424)
    E_bf = A("E_bf", [128, TSBW], BF16, offset=108672)
    P_bf = A("P_bf", [128, TSBW], BF16, offset=111296)
    PT_a = A("PT_a", [128, 8 * 128], BF16, offset=113920)
    PT_b = A("PT_b", [128, 8 * 128], BF16, offset=115968)
    ath0 = A("ath0", [64, 2 * T], BF16, offset=118016)
    ath1 = A("ath1", [64, 2 * T], BF16, offset=130304)
    wstage = A("wstage", [128, KC * 512], F32, offset=103424)
    wq_bf0 = A("wq_bf0", [128, KC * 384], BF16, offset=140352)
    wq_bf1 = A("wq_bf1", [128, KC * 384], BF16, offset=152640)
    qm0 = A("qm0", [128, T], BF16, offset=164928)
    qm1 = A("qm1", [128, T], BF16, offset=167040)
    kT_p = A("kT_p", [128, TH], BF16, offset=128192)
    vT_p = A("vT_p", [128, TP], BF16, offset=169152)
    atrb = A("atrb", [128, 2 * T], BF16, offset=140352)
    tern_g = A("tern_g", [128, KC * 512], BF16, offset=140352)
    tern_u = A("tern_u", [128, KC * 512], BF16, offset=156736)
    wdstage = A("wdstage", [128, IC * 128], F32, offset=140352)
    v_tok = A("v_tok", [128, NT * 128], BF16, offset=173120)
    f0st = A("f0st", [128, 2 * 2048], F32, offset=175552)
    astg = A("astg", [128, 4 * T], F32, offset=173120)
    xtok2 = A("xtok2", [128, D], F32, offset=173120)
    xn2 = A("xn2", [128, D], F32, offset=181312)
    qx2 = A("qx2", [128, D], BF16, offset=189568)
    hstage = A("hstage", [128, 4 * T], F32, offset=173120)
    sg_t = A("sg_t", [128, 512], F32, offset=189568)
    sgp_t = A("sgp_t", [128, 512], F32, offset=191616)
    hl_t = A("hl_t", [128, 512], F32, offset=193728)
    hsq_t = A("hsq_t", [128, 512], BF16, offset=195840)
    hrb = A("hrb", [128, 2 * 512], F32, offset=189568)
    tern_d = A("tern_d", [128, IC * 128], BF16, offset=173120)
    aprb = A("aprb", [128, 2 * 512], F32, offset=196928)
    fin_t = A("fin_t", [128, 512], F32, offset=201024)
    ot2 = A("ot2", [128, D], F32, offset=203136)
    pw_bf = A("pw_bf", [128, KC * 512], BF16, offset=203136)
    ot0 = A("ot0", [128, D], F32, offset=211328)
    ot1 = A("ot1", [128, D], F32, offset=219520)

    psA = nc.alloc_psum_tensor("psA", [128, 4096], F32)

    def bank(i, w=512):
        return psA[:, i * 512:i * 512 + w]

    sV = Sem(nc, "sV")
    sS = Sem(nc, "sS")
    sM = Sem(nc, "sM")
    sP = Sem(nc, "sP")
    sO = Sem(nc, "sO")
    sF0 = Sem(nc, "sF0")
    sVt = Sem(nc, "sVt")
    sVm = Sem(nc, "sVm")
    sXm = Sem(nc, "sXm")
    sXx = Sem(nc, "sXx")
    sXw = Sem(nc, "sXw")
    sXa = [Sem(nc, "sXa0"), Sem(nc, "sXa1")]
    sTp = Sem(nc, "sTp")
    sTt = Sem(nc, "sTt")
    sTv = Sem(nc, "sTv")
    sTat = [Sem(nc, "sTat0"), Sem(nc, "sTat1")]
    sTa4 = Sem(nc, "sTa4")
    sTh = Sem(nc, "sTh")
    sTr = [Sem(nc, "sTr0"), Sem(nc, "sTr1")]
    sTap = [Sem(nc, "sTap0"), Sem(nc, "sTap1")]
    sGw = [Sem(nc, "sGw0"), Sem(nc, "sGw1")]
    sGf = [Sem(nc, "sGf0"), Sem(nc, "sGf1")]
    sGp = Sem(nc, "sGp")

    ev = {}

    def mark(label, s):
        ev[label] = (s, s.n)

    def wv(eng, label):
        s, n = ev[label]
        if n > 0:
            eng.wait_ge(s.h, n)

    def inc(ins, s, k=1):
        s.n += k
        ins.then_inc(s.h, k)

    f0_list = []
    for wi in range(3):
        for g in range(16):
            for q in range(4):
                f0_list.append((wi, g, q))

    def f0_dma(e, qi):
        wi, g, q = f0_list[qi]
        if qi >= 2:
            e.wait_ge(sF0.h, qi - 1)
        wt = (wg_d, wu_d, wd_d)[wi]
        nk = 4 if wi < 2 else 16
        src = wt[g, q * nk:(q + 1) * nk, :, :]
        dst = f0st[:, (qi % 2) * 2048:(qi % 2) * 2048 + 2048].rearrange("p (k c) -> p k c", k=nk)
        ins = e.dma_start(out=dst, in_=src.rearrange("k p c -> p k c"))
        inc(ins, sGf[qi % 2], 16)
        mark("f0_dma_%d" % qi, sGf[qi % 2])

    def f0_red(e, qi):
        wi, g, q = f0_list[qi]
        wv(e, "f0_dma_%d" % qi)
        col = wi * 16 + g
        if q == 0:
            ins = e.tensor_reduce(out=partials[:, col:col + 1],
                                  in_=f0st[:, (qi % 2) * 2048:(qi % 2) * 2048 + 2048],
                                  axis=AX.X, op=OP.add, apply_absolute_value=True)
        else:
            e.tensor_reduce(out=tv1[:, 7:8], in_=f0st[:, (qi % 2) * 2048:(qi % 2) * 2048 + 2048],
                            axis=AX.X, op=OP.add, apply_absolute_value=True)
            ins = e.tensor_add(out=partials[:, col:col + 1], in0=partials[:, col:col + 1],
                               in1=tv1[:, 7:8])
        ins2 = e.engine_nop()
        inc(ins2, sF0, 1)

    with nc.Block() as blk:
        @blk.sync
        def _(e):
            ins = e.dma_start(out=mmul[:], in_=mmulh[:, :]); inc(ins, sXm, 16)
            ins = e.dma_start(out=madd[:].rearrange("p (a c) -> p a c", a=2),
                              in_=maddh.rearrange("a p c -> p a c")); inc(ins, sXm, 16)
            ins = e.dma_start(out=pb_sb[:], in_=pbh[:, :]); inc(ins, sXm, 16)
            ins = e.dma_start(out=lnw_sb[:], in_=lnwh[:, :]); inc(ins, sXm, 16)
        mark("init_dma", sXm)

        @blk.gpsimd
        def _(e):
            e = ES(e, sP)
            ins = e.iota(iof[:], [[1, 128]], channel_multiplier=0, allow_small_or_imprecise_dtypes=True)
            ins = e.iota(iop[:], [[0, 1]], channel_multiplier=1, allow_small_or_imprecise_dtypes=True)
            e.tensor_scalar(out=identf[:], in0=iof[:], scalar1=iop[:, 0:1], scalar2=None, op0=OP.is_equal)
            e.memset(ones_bf[:], 1.0)
            e.memset(ones_f[:], 1.0)
            e.memset(ones1[:], 1.0)
            e.memset(c_round[:], CR)
            e.memset(c_nhalf[:], -0.5)
            e.memset(qm0[:], 0.0)
            e.memset(qm1[:], 0.0)
            ins = e.memset(v_tok[:], 0.0)
        mark("init_gp", sP)

        for t in range(NT):
            @blk.sync
            def _(e, t=t):
                if t > 0:
                    wv(e, "a1_done_%d" % (t - 1))
                ins = e.dma_start(out=xtok[:], in_=xh[t * 128:(t + 1) * 128, :])
                inc(ins, sXx, 16)
            mark("a1_x_%d" % t, sXx)

            @blk.vector
            def _(e, t=t):
                e = ES(e, sV)
                if t == 0:
                    wv(e, "init_gp")
                wv(e, "a1_x_%d" % t)
                e.reduce_sum(out=tv1[:, 0:1], in_=xtok[:], axis=AX.X)
                ins = e.tensor_scalar(out=ln_mean[:, t:t + 1], in0=tv1[:, 0:1],
                                      scalar1=1.0 / D, scalar2=None, op0=OP.mult)
            mark("a1_mean_%d" % t, sV)

            @blk.scalar
            def _(e, t=t):
                e = ES(e, sS)
                if t == 0:
                    wv(e, "init_gp")
                wv(e, "a1_x_%d" % t)
                ins = e.activation(out=sq_t[:], in_=xtok[:], func=AF.Square, accum_out=tv2[:, 0:1])
            mark("a1_ssq_%d" % t, sS)

            @blk.vector
            def _(e, t=t):
                e = ES(e, sV)
                wv(e, "a1_ssq_%d" % t)
                wv(e, "a1_mean_%d" % t)
                e.tensor_scalar(out=tv2[:, 1:2], in0=tv2[:, 0:1], scalar1=1.0 / D, scalar2=1e-5,
                                op0=OP.mult, op1=OP.add)
                e.tensor_mul(out=tv2[:, 2:3], in0=ln_mean[:, t:t + 1], in1=ln_mean[:, t:t + 1])
                ins = e.tensor_sub(out=tv2[:, 3:4], in0=tv2[:, 1:2], in1=tv2[:, 2:3])
            mark("a1_var_%d" % t, sV)

            @blk.scalar
            def _(e, t=t):
                e = ES(e, sS)
                wv(e, "a1_var_%d" % t)
                e.activation(out=tv2[:, 4:5], in_=tv2[:, 3:4], func=AF.Ln)
                ins = e.activation(out=ln_rstd[:, t:t + 1], in_=tv2[:, 4:5], func=AF.Exp,
                                   scale=c_nhalf[:, 0:1])
            mark("a1_rstd_%d" % t, sS)

            @blk.vector
            def _(e, t=t):
                e = ES(e, sV)
                wv(e, "a1_rstd_%d" % t)
                ins = e.tensor_scalar(out=xn_t[:], in0=xtok[:], scalar1=ln_mean[:, t:t + 1],
                                      scalar2=ln_rstd[:, t:t + 1], op0=OP.subtract, op1=OP.mult)
                e.reduce_max(out=tv1[:, 1:2], in_=xn_t[:], axis=AX.X, apply_absolute_value=True)
                e.tensor_scalar(out=tv1[:, 2:3], in0=tv1[:, 1:2], scalar1=1e-5, scalar2=None, op0=OP.max)
                e.reciprocal(out=tv1[:, 3:4], in_=tv1[:, 2:3])
                e.tensor_scalar(out=ln_srow[:, t:t + 1], in0=tv1[:, 3:4], scalar1=127.0,
                                scalar2=None, op0=OP.mult)
                ins = e.tensor_scalar(out=tmp128[:], in0=ones_f[:],
                                      scalar1=tv1[:, 2:3], scalar2=1.0 / 127.0, op0=OP.mult, op1=OP.mult)
            mark("a1_xn_%d" % t, sV)

            @blk.tensor
            def _(e, t=t):
                wv(e, "a1_xn_%d" % t)
                if t == 0:
                    wv(e, "init_gp")
                if t > 0:
                    wv(e, "a1_isxrow_%d" % (t - 1))
                ins = e.transpose(psA[:, 3584:3584 + 128], tmp128[:], identf[:])
                inc(ins, sM, 1)
            mark("a1_isxT_%d" % t, sM)

            @blk.vector
            def _(e, t=t):
                e = ES(e, sV)
                wv(e, "a1_isxT_%d" % t)
                ins = e.tensor_copy(out=isx_row[0:1, t * 128:(t + 1) * 128],
                                    in_=psA[0:1, 3584:3584 + 128])
            mark("a1_isxrow_%d" % t, sV)

            @blk.scalar
            def _(e, t=t):
                e = ES(e, sS)
                wv(e, "a1_xn_%d" % t)
                if t > 0:
                    wv(e, "a1_done_%d" % (t - 1))
                ins = e.copy(out=xnb_t[:], in_=xn_t[:])
            mark("a1_cast_%d" % t, sS)

            @blk.scalar
            def _(e, t=t):
                e = ES(e, sS)
                wv(e, "a1_cast_%d" % t)
                for kc in range(KC):
                    ins = e.dma_start(out=xnT[:, kc * TP + t * 128: kc * TP + (t + 1) * 128],
                                      in_=xnb_t[:, kc * 128:(kc + 1) * 128], transpose=True)
                    inc(ins, sTt, 16)
            mark("a1_done_%d" % t, sTt)
        mark("a1_all", sTt)

        @blk.tensor
        def _(e):
            wv(e, "a1_isxrow_%d" % (NT - 1))
            e.matmul(psA[:, 3072:3584], ones1[:], isx_row[0:1, 8:520], start=True, stop=True)
            ins = e.matmul(psA[:, 3584:4096], ones1[:], isx_row[0:1, 520:1032], start=True, stop=True)
            inc(ins, sM, 1)
        mark("isx_mm", sM)

        @blk.vector
        def _(e):
            e = ES(e, sV)
            wv(e, "isx_mm")
            e.tensor_copy(out=ISX[:, 0:512], in_=psA[:, 3072:3584])
            ins = e.tensor_copy(out=ISX[:, 512:1024], in_=psA[:, 3584:4096])
        mark("isx_done", sV)

        f0_emitted = 0

        def emit_wq(pr):
            wbuf = (wq_bf0, wq_bf1)[pr % 2]

            @blk.gpsimd
            def _(e, pr=pr, wbuf=wbuf):
                e = ES(e, sP)
                if pr >= 2:
                    wv(e, "qkv_mm_%d" % (pr - 2))
                ins = e.dma_start(out=wbuf[:], in_=wqkv[pr])
                inc(ins, sGw[pr % 2], 16)
            mark("wq_%d" % pr, sGw[pr % 2])

        emit_wq(0)
        for pr in range(NPAIR):
            wbuf = (wq_bf0, wq_bf1)[pr % 2]
            if pr + 1 < NPAIR:
                emit_wq(pr + 1)

            @blk.gpsimd
            def _(e, pr=pr):
                e = ES(e, sP)
                for k in range(12):
                    qi = pr * 12 + k
                    if qi < 192:
                        f0_dma(e, qi)

            for mi in range(3):
                for ti, (n0, nn) in enumerate(QKV_NT if mi < 2 else QKV_NT_V):
                    u = mi * 3 + ti

                    @blk.tensor
                    def _(e, pr=pr, mi=mi, ti=ti, n0=n0, nn=nn, u=u, wbuf=wbuf):
                        if u == 0:
                            wv(e, "wq_%d" % pr)
                            if pr == 0:
                                wv(e, "a1_all")
                                wv(e, "isx_done")
                        gu = pr * 9 + u
                        if gu >= 2:
                            wv(e, "qkv_ev_%d" % (gu - 2))
                        for kc in range(KC):
                            ins = e.matmul(bank(gu % 2, nn),
                                           wbuf[:, kc * 384 + mi * 128: kc * 384 + (mi + 1) * 128],
                                           xnT[:, kc * TP + n0: kc * TP + n0 + nn],
                                           start=(kc == 0), stop=(kc == KC - 1))
                        inc(ins, sM, 1)
                    mark("qkv_ps_%d_%d" % (pr, u), sM)

                    @blk.vector
                    def _(e, pr=pr, mi=mi, n0=n0, nn=nn, u=u):
                        e = ES(e, sV)
                        gu = pr * 9 + u
                        wv(e, "qkv_ps_%d_%d" % (pr, u))
                        if mi == 0:
                            s0 = max(n0, 8) - n0
                            s1 = min(n0 + nn, 8 + T) - n0
                            d0 = n0 + s0 - 8
                            ins = e.tensor_scalar(out=qm0[0:64, d0:d0 + (s1 - s0)],
                                                  in0=bank(gu % 2, nn)[0:64, s0:s1],
                                                  scalar1=0.125, scalar2=None, op0=OP.mult)
                            ins = e.tensor_scalar(out=qm1[64:128, d0:d0 + (s1 - s0)],
                                                  in0=bank(gu % 2, nn)[64:128, s0:s1],
                                                  scalar1=0.125, scalar2=None, op0=OP.mult)
                        else:
                            dst = (None, kT_p, vT_p)[mi]
                            ins = e.tensor_copy(out=dst[:, n0:n0 + nn], in_=bank(gu % 2, nn))
                    mark("qkv_ev_%d" % (pr * 9 + u), sV)
            mark("qkv_mm_%d" % pr, sM)
            mark("qkv_ev_done_%d" % pr, sV)

            @blk.scalar
            def _(e, pr=pr):
                e = ES(e, sS)
                wv(e, "qkv_ev_done_%d" % pr)
                if pr > 0:
                    wv(e, "a3_pv_%d_1" % (pr - 1))
                for tt in range(NT):
                    ins = e.dma_start(out=v_tok[:, tt * 128: tt * 128 + 128],
                                      in_=vT_p[:, tt * 128: tt * 128 + 128], transpose=True)
                    inc(ins, sTv, 16)
            mark("vtok_%d" % pr, sTv)

            for half in range(2):
                hh = pr * 2 + half

                @blk.tensor
                def _(e, pr=pr, half=half, hh=hh):
                    if hh > 0:
                        wv(e, "a3_Tsb_%d" % (hh - 1))
                    for jj in range(4):
                        j = half * 4 + jj
                        for h in range(2):
                            slot = jj * 2 + h
                            off = 1024 + (slot // 3) * 512 + (slot % 3) * 144
                            qm = (qm0, qm1)[h]
                            ins = e.matmul(psA[:, off:off + 144],
                                           qm[:, j * 128: (j + 1) * 128],
                                           kT_p[:, j * 128: j * 128 + 144],
                                           start=True, stop=True)
                    off8 = 1024 + 2 * 512 + 2 * 144
                    ins = e.matmul(psA[:, off8:off8 + 144],
                                   qm0[:, half * 512: half * 512 + 128],
                                   kT_p[:, half * 512: half * 512 + 144],
                                   start=True, stop=True)
                    inc(ins, sM, 1)
                mark("a3_sc_%d" % hh, sM)

                @blk.vector
                def _(e, pr=pr, half=half, hh=hh):
                    e = ES(e, sV)
                    if hh == 0:
                        wv(e, "init_dma")
                    else:
                        wv(e, "a3_exp_%d" % (hh - 1))
                    wv(e, "a3_sc_%d" % hh)
                    psc = psA[:, 1024:1024 + 1536].rearrange("p (b s) -> p b s", b=3)[:, :, 0:432]
                    ins = e.tensor_tensor(out=T_sb[:].rearrange("p (b c) -> p b c", b=3), in0=psc,
                                          in1=mmul[:].rearrange("p (b c) -> p b c", b=3), op=OP.mult)
                    mark("a3_Tsb_%d" % hh, sV)
                    e.tensor_tensor(out=T_sb[:], in0=T_sb[:],
                                    in1=madd[:, half * TSBW:(half + 1) * TSBW], op=OP.add)
                    ins = e.reduce_max(out=negmax[:], in_=T_sb[:], axis=AX.X, negate=True)
                mark("a3_mx_%d" % hh, sV)

                @blk.scalar
                def _(e, hh=hh):
                    e = ES(e, sS)
                    wv(e, "a3_mx_%d" % hh)
                    ins = e.activation(out=E_bf[:], in_=T_sb[:], func=AF.Exp, bias=negmax[:, 0:1])
                mark("a3_exp_%d" % hh, sS)

                @blk.vector
                def _(e, hh=hh):
                    e = ES(e, sV)
                    wv(e, "a3_exp_%d" % hh)
                    if hh > 0:
                        wv(e, "a3_pt_%d" % (hh - 1))
                    e.reduce_sum(out=rsum[:], in_=E_bf[:].rearrange("p (s c) -> p s c", s=9), axis=AX.X)
                    e.reciprocal(out=rinv[:], in_=rsum[:])
                    for slot in range(9):
                        ins = e.tensor_scalar(out=P_bf[:, slot * 144:(slot + 1) * 144],
                                              in0=E_bf[:, slot * 144:(slot + 1) * 144],
                                              scalar1=rinv[:, slot:slot + 1], scalar2=None, op0=OP.mult)
                mark("a3_P_%d" % hh, sV)

                @blk.sync
                def _(e, hh=hh):
                    wv(e, "a3_P_%d" % hh)
                    if hh > 0:
                        wv(e, "a3_pv_%d_%d" % ((hh - 1) // 2, (hh - 1) % 2))
                    for slot in range(8):
                        ins = e.dma_start(out=PT_a[:, slot * 128:(slot + 1) * 128],
                                          in_=P_bf[:, slot * 144: slot * 144 + 128], transpose=True)
                        inc(ins, sTp, 16)
                        ins = e.dma_start(out=PT_b[:, slot * 128:(slot + 1) * 128],
                                          in_=P_bf[:, slot * 144 + 128: slot * 144 + 256], transpose=True)
                        inc(ins, sTp, 16)
                mark("a3_pt_%d" % hh, sTp)

                @blk.tensor
                def _(e, pr=pr, half=half, hh=hh):
                    wv(e, "a3_pt_%d" % hh)
                    wv(e, "vtok_%d" % pr)
                    if hh > 0:
                        wv(e, "a3_oev_%d" % (hh - 1))
                    for jj in range(4):
                        j = half * 4 + jj
                        for h in range(2):
                            slot = jj * 2 + h
                            out = psA[0:64, 2560 + slot * 128: 2560 + (slot + 1) * 128]
                            e.matmul(out, v_tok[:, j * 128 + 64 * h: j * 128 + 64 * h + 64],
                                     PT_a[:, slot * 128:(slot + 1) * 128], start=True, stop=False)
                            ins = e.matmul(out,
                                           v_tok[0:16, (j + 1) * 128 + 64 * h: (j + 1) * 128 + 64 * h + 64],
                                           PT_b[0:16, slot * 128:(slot + 1) * 128], start=False, stop=True)
                    inc(ins, sM, 1)
                mark("a3_pv_%d_%d" % (pr, half), sM)

                @blk.vector
                def _(e, pr=pr, half=half):
                    e = ES(e, sV)
                    wv(e, "a3_pv_%d_%d" % (pr, half))
                    if pr >= 2 and half == 0:
                        wv(e, "at_spill_%d" % (pr - 2))
                    for jj in range(4):
                        for h in range(2):
                            slot = jj * 2 + h
                            dst = (ath0, ath1)[h]
                            ins = e.tensor_copy(out=dst[:, (pr % 2) * T + half * 512 + jj * 128:
                                                        (pr % 2) * T + half * 512 + (jj + 1) * 128],
                                                in_=psA[0:64, 2560 + slot * 128: 2560 + (slot + 1) * 128])
                mark("a3_oev_%d" % (pr * 2 + half), sV)

            @blk.scalar
            def _(e, pr=pr):
                e = ES(e, sS)
                wv(e, "a3_oev_%d" % (pr * 2 + 1))
                ins = e.dma_start(out=atscr[pr][0:64, :], in_=ath0[:, (pr % 2) * T:(pr % 2) * T + T])
                inc(ins, sTat[pr % 2], 16)
                ins = e.dma_start(out=atscr[pr][64:128, :], in_=ath1[:, (pr % 2) * T:(pr % 2) * T + T])
                inc(ins, sTat[pr % 2], 16)
            mark("at_spill_%d" % pr, sTat[pr % 2])

            @blk.vector
            def _(e, pr=pr):
                e = ES(e, sV)
                nonlocal f0_emitted
                hi = min((pr + 1) * 12, 192)
                while f0_emitted < hi:
                    f0_red(e, f0_emitted)
                    f0_emitted += 1
        @blk.vector
        def _(e):
            e = ES(e, sV)
            nonlocal f0_emitted
            while f0_emitted < 192:
                f0_red(e, f0_emitted)
                f0_emitted += 1
        mark("f0_red_all", sV)

        @blk.vector
        def _(e):
            e = ES(e, sV)
            wv(e, "f0_red_all")
            e.reduce_sum(out=tv1[:, 0:1], in_=partials[:, 0:16], axis=AX.X)
            e.reduce_sum(out=tv1[:, 1:2], in_=partials[:, 16:32], axis=AX.X)
            ins = e.reduce_sum(out=tv1[:, 2:3], in_=partials[:, 32:48], axis=AX.X)
        mark("f0_sums", sV)

        @blk.tensor
        def _(e):
            wv(e, "f0_sums")
            wv(e, "a3_oev_%d" % (NPAIR * 2 - 1))
            e.matmul(psA[:, 3072:3073], ones_f[:], tv1[:, 0:1], start=True, stop=True)
            e.matmul(psA[:, 3073:3074], ones_f[:], tv1[:, 1:2], start=True, stop=True)
            ins = e.matmul(psA[:, 3074:3075], ones_f[:], tv1[:, 2:3], start=True, stop=True)
            inc(ins, sM, 1)
        mark("f0_bc", sM)

        @blk.vector
        def _(e):
            e = ES(e, sV)
            wv(e, "f0_bc")
            nel = float(I * D)
            for wi, (ssb, isb) in enumerate(((swg, iswg), (swu, iswu), (swd, iswd))):
                e.tensor_scalar(out=isb[:], in0=psA[:, 3072 + wi:3073 + wi],
                                scalar1=1.0 / nel, scalar2=1e-5, op0=OP.mult, op1=OP.max)
                ins = e.reciprocal(out=ssb[:], in_=isb[:])
        mark("f0_scales", sV)

        for g in range(4):
            @blk.gpsimd
            def _(e, g=g):
                e = ES(e, sP)
                if g == 0:
                    wv(e, "f0_red_all")
                else:
                    wv(e, "a4_mm_%d" % (g - 1))
                ins = e.dma_start(out=pw_bf[:], in_=wproj[g])
                inc(ins, sGp, 16)
            mark("a4_w_%d" % g, sGp)

            for kc in range(KC):
                @blk.sync
                def _(e, g=g, kc=kc):
                    wv(e, "at_spill_%d" % kc)
                    gk = g * KC + kc
                    if gk >= 2:
                        wv(e, "a4_mmk_%d" % (gk - 2))
                    ins = e.dma_start(out=atrb[:, (kc % 2) * T:(kc % 2) * T + T], in_=atscr[kc])
                    inc(ins, sXa[kc % 2], 16)
                mark("a4_rb_%d_%d" % (g, kc), sXa[kc % 2])

                @blk.tensor
                def _(e, g=g, kc=kc):
                    if kc == 0:
                        wv(e, "a4_w_%d" % g)
                        if g == 0:
                            wv(e, "f0_scales")
                        else:
                            wv(e, "a4_ev_%d" % (g - 1))
                    wv(e, "a4_rb_%d_%d" % (g, kc))
                    for mi in range(4):
                        for nt in range(2):
                            ins = e.matmul(bank(mi * 2 + nt),
                                           pw_bf[:, kc * 512 + mi * 128: kc * 512 + (mi + 1) * 128],
                                           atrb[:, (kc % 2) * T + nt * 512:
                                                (kc % 2) * T + (nt + 1) * 512],
                                           start=(kc == 0), stop=(kc == KC - 1))
                    inc(ins, sM, 1)
                mark("a4_mmk_%d" % (g * KC + kc), sM)
            mark("a4_mm_%d" % g, sM)

            @blk.vector
            def _(e, g=g):
                e = ES(e, sV)
                wv(e, "a4_mm_%d" % g)
                if g >= 1:
                    wv(e, "a4_spill_%d" % (g - 1))
                for mi in range(4):
                    for nt in range(2):
                        m = g * 4 + mi
                        ins = e.tensor_scalar(out=astg[:, mi * T + nt * 512: mi * T + (nt + 1) * 512],
                                              in0=bank(mi * 2 + nt), scalar1=pb_sb[:, m:m + 1],
                                              scalar2=None, op0=OP.add)
            mark("a4_ev_%d" % g, sV)

            @blk.scalar
            def _(e, g=g):
                e = ES(e, sS)
                wv(e, "a4_ev_%d" % g)
                ins = e.dma_start(out=ascr[g * 4:(g + 1) * 4].rearrange("k p c -> p k c"),
                                  in_=astg[:].rearrange("p (k c) -> p k c", k=4))
                inc(ins, sTa4, 16)
            mark("a4_spill_%d" % g, sTa4)

        for t in range(NT):
            @blk.sync
            def _(e, t=t):
                if t == 0:
                    wv(e, "a4_spill_3")
                else:
                    wv(e, "f1r_t_%d" % (t - 1))
                ins = e.dma_start(out=xtok2[:], in_=xh[t * 128:(t + 1) * 128, :])
                inc(ins, sXx, 16)
            mark("f1r_x_%d" % t, sXx)

            @blk.vector
            def _(e, t=t):
                e = ES(e, sV)
                wv(e, "f1r_x_%d" % t)
                ins = e.tensor_scalar(out=xn2[:], in0=xtok2[:], scalar1=ln_mean[:, t:t + 1],
                                      scalar2=ln_rstd[:, t:t + 1], op0=OP.subtract, op1=OP.mult)
            mark("f1r_xn_%d" % t, sV)

            @blk.scalar
            def _(e, t=t):
                e = ES(e, sS)
                wv(e, "f1r_xn_%d" % t)
                ins = e.activation(out=xn2[:], in_=xn2[:], func=AF.Identity,
                                   scale=ln_srow[:, t:t + 1], bias=c_round[:, 0:1])
            mark("f1r_y_%d" % t, sS)

            @blk.vector
            def _(e, t=t):
                e = ES(e, sV)
                wv(e, "f1r_y_%d" % t)
                ins = e.tensor_scalar(out=qx2[:], in0=xn2[:], scalar1=CR, scalar2=None, op0=OP.subtract)
            mark("f1r_qx_%d" % t, sV)

            @blk.scalar
            def _(e, t=t):
                e = ES(e, sS)
                wv(e, "f1r_qx_%d" % t)
                for kc in range(KC):
                    ins = e.dma_start(out=qxT[:, kc * TP + t * 128: kc * TP + (t + 1) * 128],
                                      in_=qx2[:, kc * 128:(kc + 1) * 128], transpose=True)
                    inc(ins, sTt, 16)
            mark("f1r_t_%d" % t, sTt)
        mark("f1r_all", sTt)

        for g in range(16):
            for which in range(2):
                lab = "%d_%d" % (g, which)
                warr = (wg_d, wu_d)[which]
                tdst = (tern_g, tern_u)[which]
                ssc = (swg, swu)[which]

                @blk.sync
                def _(e, g=g, which=which, warr=warr):
                    if which == 1:
                        wv(e, "f1_tern_%d_0" % g)
                    elif g > 0:
                        wv(e, "f1_tern_%d_1" % (g - 1))
                    else:
                        wv(e, "f1r_all")
                        wv(e, "at_spill_%d" % (NPAIR - 1))
                        wv(e, "a4_mm_3")
                    ins = e.dma_start(out=wstage[:].rearrange("p (k c) -> p k c", k=KC),
                                      in_=warr[g].rearrange("k p c -> p k c"))
                    inc(ins, sXw, 16)
                mark("f1_wdma_%s" % lab, sXw)

                @blk.scalar
                def _(e, g=g, which=which, lab=lab, ssc=ssc):
                    e = ES(e, sS)
                    wv(e, "f1_wdma_%s" % lab)
                    if g == 0 and which == 0:
                        wv(e, "f0_scales")
                    ins = e.activation(out=wstage[:], in_=wstage[:], func=AF.Identity,
                                       scale=ssc[:, 0:1], bias=c_round[:, 0:1])
                mark("f1_tern_s_%s" % lab, sS)

                @blk.vector
                def _(e, g=g, which=which, lab=lab, tdst=tdst):
                    e = ES(e, sV)
                    wv(e, "f1_tern_s_%s" % lab)
                    if g > 0:
                        wv(e, "f1_mm_%d_%d" % (g - 1, which))
                    e.tensor_scalar(out=wstage[:], in0=wstage[:], scalar1=CR, scalar2=1.0,
                                    op0=OP.subtract, op1=OP.min)
                    ins = e.tensor_scalar(out=tdst[:], in0=wstage[:], scalar1=-1.0,
                                          scalar2=None, op0=OP.max)
                mark("f1_tern_%s" % lab, sV)

            for mi in range(4):
                c = g * 4 + mi
                for nt in range(2):
                    u = mi * 2 + nt
                    gu = g * 8 + u
                    lab = "%d_%d" % (c, nt)

                    @blk.tensor
                    def _(e, g=g, mi=mi, nt=nt, u=u, gu=gu):
                        if u == 0:
                            wv(e, "f1_tern_%d_0" % g)
                            if g == 0:
                                wv(e, "f1r_all")
                                wv(e, "f1s_prev_none" if False else "f0_scales")
                        if gu >= 2:
                            wv(e, "f1_gg_ev_%d" % (gu - 2))
                        for kc in range(KC):
                            ins = e.matmul(bank(u % 2),
                                           tern_g[:, kc * 512 + mi * 128: kc * 512 + (mi + 1) * 128],
                                           qxT[:, kc * TP + 8 + nt * 512: kc * TP + 8 + (nt + 1) * 512],
                                           start=(kc == 0), stop=(kc == KC - 1))
                        inc(ins, sM, 1)
                    mark("f1_gp_%d" % gu, sM)

                    @blk.tensor
                    def _(e, g=g, mi=mi, nt=nt, u=u, gu=gu):
                        if u == 0:
                            wv(e, "f1_tern_%d_1" % g)
                        if gu >= 2:
                            wv(e, "f1_h_ev_%d" % (gu - 2))
                        for kc in range(KC):
                            ins = e.matmul(bank(2 + u % 2),
                                           tern_u[:, kc * 512 + mi * 128: kc * 512 + (mi + 1) * 128],
                                           qxT[:, kc * TP + 8 + nt * 512: kc * TP + 8 + (nt + 1) * 512],
                                           start=(kc == 0), stop=(kc == KC - 1))
                        inc(ins, sM, 1)
                    mark("f1_up_%d" % gu, sM)

                    @blk.vector
                    def _(e, u=u, nt=nt, gu=gu):
                        e = ES(e, sV)
                        wv(e, "f1_gp_%d" % gu)
                        ins = e.scalar_tensor_tensor(out=sg_t[:], in0=bank(u % 2), scalar=iswg[:, 0:1],
                                                     in1=ISX[:, nt * 512:(nt + 1) * 512],
                                                     op0=OP.mult, op1=OP.mult)
                    mark("f1_gg_%s" % lab, sV)
                    mark("f1_gg_ev_%d" % gu, sV)

                    @blk.scalar
                    def _(e, lab=lab):
                        e = ES(e, sS)
                        wv(e, "f1_gg_%s" % lab)
                        ins = e.activation(out=sg_t[:], in_=sg_t[:], func=AF.Silu)
                    mark("f1_sil_%s" % lab, sS)

                    @blk.vector
                    def _(e, nt=nt, lab=lab):
                        e = ES(e, sV)
                        wv(e, "f1_sil_%s" % lab)
                        ins = e.tensor_scalar(out=sgp_t[:], in0=sg_t[:], scalar1=iswu[:, 0:1],
                                              scalar2=None, op0=OP.mult)
                        ins = e.tensor_tensor(out=sgp_t[:], in0=sgp_t[:],
                                              in1=ISX[:, nt * 512:(nt + 1) * 512], op=OP.mult)
                    mark("f1_sgp_%s" % lab, sV)

                    @blk.vector
                    def _(e, g=g, mi=mi, nt=nt, u=u, gu=gu, lab=lab):
                        e = ES(e, sV)
                        wv(e, "f1_sgp_%s" % lab)
                        wv(e, "f1_up_%d" % gu)
                        if g > 0 and u <= 1:
                            wv(e, "f1_spill_%d" % (g - 1))
                        ins = e.tensor_tensor(out=hstage[:, mi * T + nt * 512: mi * T + (nt + 1) * 512],
                                              in0=bank(2 + u % 2), in1=sgp_t[:], op=OP.mult)
                    mark("f1_h_%s" % lab, sV)
                    mark("f1_h_ev_%d" % gu, sV)

                    @blk.scalar
                    def _(e, g=g, mi=mi, nt=nt, gu=gu, lab=lab):
                        e = ES(e, sS)
                        wv(e, "f1_h_%s" % lab)
                        if gu >= 1:
                            wv(e, "f1_ssq_%d_%d" % ((gu - 1) // 2, (gu - 1) % 2))
                        ins = e.activation(out=hsq_t[:],
                                           in_=hstage[:, mi * T + nt * 512: mi * T + (nt + 1) * 512],
                                           func=AF.Square)
                    mark("f1_hsq_%s" % lab, sS)

                    @blk.tensor
                    def _(e, c=c, nt=nt, lab=lab):
                        wv(e, "f1_hsq_%s" % lab)
                        ins = e.matmul(psA[:, 3072 + nt * 512: 3072 + (nt + 1) * 512],
                                       ones_bf[:], hsq_t[:],
                                       start=(c == 0), stop=(c == IC - 1))
                        inc(ins, sM, 1)
                    mark("f1_ssq_%s" % lab, sM)

                    @blk.vector
                    def _(e, c=c, mi=mi, nt=nt, gu=gu, lab=lab):
                        e = ES(e, sV)
                        wv(e, "f1_h_%s" % lab)
                        ins = e.tensor_scalar(out=hl_t[:],
                                              in0=hstage[:, mi * T + nt * 512: mi * T + (nt + 1) * 512],
                                              scalar1=lnw_sb[:, c:c + 1], scalar2=None, op0=OP.mult)
                    mark("f1_hl_%s" % lab, sV)

                    @blk.vector
                    def _(e, c=c, nt=nt, lab=lab):
                        e = ES(e, sV)
                        wv(e, "f1_hl_%s" % lab)
                        if c == 0:
                            e.tensor_copy(out=Mhi[:, nt * 512:(nt + 1) * 512], in_=hl_t[:])
                            ins = e.tensor_copy(out=Mlo[:, nt * 512:(nt + 1) * 512], in_=hl_t[:])
                        else:
                            e.tensor_tensor(out=Mhi[:, nt * 512:(nt + 1) * 512],
                                            in0=Mhi[:, nt * 512:(nt + 1) * 512], in1=hl_t[:], op=OP.max)
                            ins = e.tensor_tensor(out=Mlo[:, nt * 512:(nt + 1) * 512],
                                                  in0=Mlo[:, nt * 512:(nt + 1) * 512], in1=hl_t[:], op=OP.min)
                    mark("f1_M_%s" % lab, sV)
            mark("f1_mm_%d_0" % g, sM)
            mark("f1_mm_%d_1" % g, sM)

            @blk.scalar
            def _(e, g=g):
                e = ES(e, sS)
                for mi in range(4):
                    for nt in range(2):
                        wv(e, "f1_hsq_%d_%d" % (g * 4 + mi, nt))
                        wv(e, "f1_M_%d_%d" % (g * 4 + mi, nt))
                ins = e.dma_start(out=hscr[g * 4:(g + 1) * 4].rearrange("k p c -> p k c"),
                                  in_=hstage[:].rearrange("p (k c) -> p k c", k=4))
                inc(ins, sTh, 16)
            mark("f1_spill_%d" % g, sTh)

        @blk.vector
        def _(e):
            e = ES(e, sV)
            wv(e, "f1_ssq_63_1")
            for mi in range(4):
                wv(e, "f1_M_%d_1" % (60 + mi))
            e.tensor_scalar(out=r_bc[:, 0:512], in0=psA[:, 3072:3584], scalar1=1.0 / I,
                            scalar2=1e-6, op0=OP.mult, op1=OP.add)
            ins = e.tensor_scalar(out=r_bc[:, 512:1024], in0=psA[:, 3584:4096], scalar1=1.0 / I,
                                  scalar2=1e-6, op0=OP.mult, op1=OP.add)
            e.tensor_scalar(out=Mlo[:], in0=Mlo[:], scalar1=-1.0, scalar2=None, op0=OP.mult)
            ins = e.tensor_tensor(out=Mhi[:], in0=Mhi[:], in1=Mlo[:], op=OP.max)
        mark("f1s_pre", sV)

        @blk.scalar
        def _(e):
            e = ES(e, sS)
            wv(e, "f1s_pre")
            e.activation(out=r_bc[:], in_=r_bc[:], func=AF.Ln)
            ins = e.activation(out=r_bc[:], in_=r_bc[:], func=AF.Exp, scale=c_nhalf[:, 0:1])
        mark("f1s_rbc", sS)

        @blk.tensor
        def _(e):
            wv(e, "f1s_pre")
            for b in range(8):
                if b >= 2:
                    e.wait_ge(sVm.h, b - 1)
                ins = e.transpose(psA[:, 2048 + (b % 2) * 512: 2048 + (b % 2) * 512 + 128],
                                  Mhi[:, b * 128:(b + 1) * 128], identf[:])
                inc(ins, sM, 1)
                mark("f1s_mt_%d" % b, sM)

        @blk.vector
        def _(e):
            e = ES(e, sV)
            for b in range(8):
                wv(e, "f1s_mt_%d" % b)
                ins = e.reduce_max(out=amax_tok[:, b:b + 1],
                                   in_=psA[:, 2048 + (b % 2) * 512: 2048 + (b % 2) * 512 + 128],
                                   axis=AX.X)
                ins2 = e.engine_nop()
                inc(ins2, sVm, 1)
        mark("f1s_am", sV)

        @blk.tensor
        def _(e):
            wv(e, "f1s_rbc")
            wv(e, "f1s_am")
            for b in range(8):
                ins = e.transpose(psA[:, 2048 + b * 128: 2048 + b * 128 + 128],
                                  r_bc[:, b * 128:(b + 1) * 128], identf[:])
                inc(ins, sM, 1)
                mark("f1s_rt_%d" % b, sM)

        @blk.vector
        def _(e):
            e = ES(e, sV)
            wv(e, "f1s_rt_%d" % 7)
            for b in range(8):
                ins = e.tensor_copy(out=rtok[:, b:b + 1],
                                    in_=psA[:, 2048 + b * 128: 2048 + b * 128 + 1])
            mark("f1s_rtok", sV)
            e.tensor_tensor(out=mrms[:], in0=rtok[:], in1=amax_tok[:], op=OP.mult)
            e.tensor_scalar(out=mrms[:], in0=mrms[:], scalar1=1e-5, scalar2=None, op0=OP.max)
            e.reciprocal(out=shtok[:], in_=mrms[:])
            e.tensor_scalar(out=shtok[:], in0=shtok[:], scalar1=127.0, scalar2=None, op0=OP.mult)
            e.tensor_scalar(out=ivtok[:], in0=mrms[:], scalar1=1.0 / 127.0,
                            scalar2=None, op0=OP.mult)
            ins = e.tensor_scalar(out=ivtok[:], in0=ivtok[:], scalar1=iswd[:, 0:1],
                                  scalar2=None, op0=OP.mult)
        mark("f1s_tok", sV)

        for b in range(8):
            for which, (colsrc, rowdst) in enumerate(((shtok, sh_row), (ivtok, iv_row))):
                lab = "%d_%d" % (b, which)

                @blk.vector
                def _(e, b=b, which=which, colsrc=colsrc, lab=lab):
                    e = ES(e, sV)
                    wv(e, "f1s_tok")
                    if b * 2 + which >= 2:
                        wv(e, "f1s_row_%d_%d" % ((b * 2 + which - 2) // 2, (b * 2 + which - 2) % 2))
                    ins = e.tensor_scalar(out=tmp128[:], in0=ones_f[:], scalar1=colsrc[:, b:b + 1],
                                          scalar2=None, op0=OP.mult)
                mark("f1s_bcc_%s" % lab, sV)

                @blk.tensor
                def _(e, b=b, which=which, lab=lab):
                    wv(e, "f1s_bcc_%s" % lab)
                    ins = e.transpose(psA[:, 2048 + ((b * 2 + which) % 2) * 512:
                                          2048 + ((b * 2 + which) % 2) * 512 + 128],
                                      tmp128[:], identf[:])
                    inc(ins, sM, 1)
                mark("f1s_ct_%s" % lab, sM)

                @blk.vector
                def _(e, b=b, which=which, rowdst=rowdst, lab=lab):
                    e = ES(e, sV)
                    wv(e, "f1s_ct_%s" % lab)
                    ins = e.tensor_copy(out=rowdst[0:1, b * 128:(b + 1) * 128],
                                        in_=psA[0:1, 2048 + ((b * 2 + which) % 2) * 512:
                                                2048 + ((b * 2 + which) % 2) * 512 + 128])
                mark("f1s_row_%s" % lab, sV)
        mark("f1s_rows", sV)

        @blk.tensor
        def _(e):
            wv(e, "f1s_rows")
            e.matmul(psA[:, 0:512], ones1[:], sh_row[0:1, 0:512], start=True, stop=True)
            e.matmul(psA[:, 512:1024], ones1[:], sh_row[0:1, 512:1024], start=True, stop=True)
            e.matmul(psA[:, 1024:1536], ones1[:], iv_row[0:1, 0:512], start=True, stop=True)
            ins = e.matmul(psA[:, 1536:2048], ones1[:], iv_row[0:1, 512:1024], start=True, stop=True)
            inc(ins, sM, 1)
        mark("f1s_bc", sM)

        @blk.vector
        def _(e):
            e = ES(e, sV)
            wv(e, "f1s_bc")
            e.tensor_tensor(out=RS[:, 0:512], in0=r_bc[:, 0:512], in1=psA[:, 0:512], op=OP.mult)
            e.tensor_tensor(out=RS[:, 512:1024], in0=r_bc[:, 512:1024], in1=psA[:, 512:1024], op=OP.mult)
            e.tensor_copy(out=INVD[:, 0:512], in_=psA[:, 1024:1536])
            ins = e.tensor_copy(out=INVD[:, 512:1024], in_=psA[:, 1536:2048])
        mark("f1s_done", sV)

        for half in range(2):
            c0 = half * 512
            for c in range(IC):
                @blk.scalar
                def _(e, half=half, c=c, c0=c0):
                    e = ES(e, sS)
                    if half == 0 and c < 2:
                        wv(e, "f1_spill_15")
                        wv(e, "f1s_done")
                    else:
                        lab = ("f2_q_%d_%d" % (half, c - 2)) if c >= 2 else \
                              ("f2_q_%d_%d" % (half - 1, IC - 2 + c))
                        wv(e, lab)
                        if half == 1 and c < 2:
                            wv(e, "f3_mm_0_%d" % (KC - 1))
                    ins = e.dma_start(out=hrb[:, (c % 2) * 512:(c % 2) * 512 + 512],
                                      in_=hscr[c][:, c0:c0 + 512])
                    inc(ins, sTr[c % 2], 16)
                mark("f2_rb_%d_%d" % (half, c), sTr[c % 2])

                @blk.vector
                def _(e, half=half, c=c, c0=c0):
                    e = ES(e, sV)
                    wv(e, "f2_rb_%d_%d" % (half, c))
                    buf = (hl_t, sgp_t)[c % 2]
                    ins = e.tensor_scalar(out=buf[:], in0=hrb[:, (c % 2) * 512:(c % 2) * 512 + 512],
                                          scalar1=lnw_sb[:, c:c + 1], scalar2=None, op0=OP.mult)
                    ins = e.tensor_tensor(out=buf[:], in0=buf[:], in1=RS[:, c0:c0 + 512], op=OP.mult)
                    ins = e.tensor_scalar(out=hq[:, c * 512:(c + 1) * 512], in0=buf[:],
                                          scalar1=CR, scalar2=CR, op0=OP.add, op1=OP.subtract)
                mark("f2_q_%d_%d" % (half, c), sV)
            mark("f2_done_%d" % half, sV)

            for b in range(4):
                @blk.sync
                def _(e, half=half, b=b):
                    if half > 0:
                        wv(e, "out_all_%d" % (half - 1))
                    else:
                        wv(e, "f1_tern_15_1")
                    tb = half * 4 + b
                    ins = e.dma_start(out=(ot0, ot1, ot2, ot3)[b][:],
                                      in_=xh[8 + tb * 128: 8 + (tb + 1) * 128, :])
                    inc(ins, sXx, 16)
                mark("f3_x_%d_%d" % (half, b), sXx)

            for m in range(KC):
                lab = "%d_%d" % (half, m)
                hm = half * KC + m

                @blk.sync
                def _(e, half=half, m=m, hm=hm):
                    if hm > 0:
                        wv(e, "f3_tern_%d_%d" % ((hm - 1) // KC, (hm - 1) % KC))
                    else:
                        wv(e, "f2_done_0")
                    ins = e.dma_start(out=wdstage[:].rearrange("p (k c) -> p k c", k=IC),
                                      in_=wd_d[m].rearrange("k p c -> p k c"))
                    inc(ins, sXw, 16)
                mark("f3_wdma_%s" % lab, sXw)

                @blk.scalar
                def _(e, lab=lab, hm=hm):
                    e = ES(e, sS)
                    wv(e, "f3_wdma_%s" % lab)
                    ins = e.activation(out=wdstage[:], in_=wdstage[:], func=AF.Identity,
                                       scale=swd[:, 0:1], bias=c_round[:, 0:1])
                mark("f3_tern_s_%s" % lab, sS)

                @blk.vector
                def _(e, lab=lab, hm=hm):
                    e = ES(e, sV)
                    wv(e, "f3_tern_s_%s" % lab)
                    if hm > 0:
                        wv(e, "f3_mm_%d_%d" % ((hm - 1) // KC, (hm - 1) % KC))
                    e.tensor_scalar(out=wdstage[:], in0=wdstage[:], scalar1=CR, scalar2=1.0,
                                    op0=OP.subtract, op1=OP.min)
                    ins = e.tensor_scalar(out=tern_d[:], in0=wdstage[:], scalar1=-1.0,
                                          scalar2=None, op0=OP.max)
                mark("f3_tern_%s" % lab, sV)

                @blk.scalar
                def _(e, half=half, m=m, hm=hm, c0=c0):
                    e = ES(e, sS)
                    if hm == 0:
                        wv(e, "a4_spill_3")
                    if hm >= 2:
                        wv(e, "f3_fin_%d_%d" % ((hm - 2) // KC, (hm - 2) % KC))
                    ins = e.dma_start(out=aprb[:, (m % 2) * 512:(m % 2) * 512 + 512],
                                      in_=ascr[m][:, c0:c0 + 512])
                    inc(ins, sTap[m % 2], 16)
                mark("f3_ap_%s" % lab, sTap[m % 2])

                @blk.tensor
                def _(e, half=half, m=m, lab=lab, hm=hm):
                    wv(e, "f3_tern_%s" % lab)
                    wv(e, "f2_done_%d" % half)
                    if hm >= 2:
                        wv(e, "f3_ev_%d_%d" % ((hm - 2) // KC, (hm - 2) % KC))
                    for kc in range(IC):
                        ins = e.matmul(bank(m % 2), tern_d[:, kc * 128:(kc + 1) * 128],
                                       hq[:, kc * 512:(kc + 1) * 512],
                                       start=(kc == 0), stop=(kc == IC - 1))
                    inc(ins, sM, 1)
                mark("f3_mm_%s" % lab, sM)

                @blk.vector
                def _(e, half=half, m=m, lab=lab, c0=c0):
                    e = ES(e, sV)
                    wv(e, "f3_mm_%s" % lab)
                    wv(e, "f3_ap_%s" % lab)
                    e.tensor_tensor(out=fin_t[:], in0=bank(m % 2), in1=INVD[:, c0:c0 + 512], op=OP.mult)
                    ins = e.tensor_tensor(out=fin_t[:], in0=fin_t[:],
                                          in1=aprb[:, (m % 2) * 512:(m % 2) * 512 + 512], op=OP.add)
                mark("f3_fin_%s" % lab, sV)
                mark("f3_ev_%s" % lab, sV)

                @blk.tensor
                def _(e, half=half, m=m, lab=lab, hm=hm):
                    wv(e, "f3_fin_%s" % lab)
                    for b in range(4):
                        gidx = hm * 4 + b
                        if gidx >= 2:
                            e.wait_ge(sVt.h, gidx - 1)
                        ins = e.transpose(psA[:, 2048 + (gidx % 2) * 512: 2048 + (gidx % 2) * 512 + 128],
                                          fin_t[:, b * 128:(b + 1) * 128], identf[:])
                        inc(ins, sM, 1)
                        mark("f3_tt_%d_%d_%d" % (half, m, b), sM)

                @blk.vector
                def _(e, half=half, m=m, hm=hm):
                    e = ES(e, sV)
                    for b in range(4):
                        gidx = hm * 4 + b
                        wv(e, "f3_tt_%d_%d_%d" % (half, m, b))
                        wv(e, "f3_x_%d_%d" % (half, 3))
                        ot = (ot0, ot1, ot2, ot3)[b]
                        ins = e.tensor_tensor(out=ot[:, m * 128:(m + 1) * 128],
                                              in0=psA[:, 2048 + (gidx % 2) * 512:
                                                      2048 + (gidx % 2) * 512 + 128],
                                              in1=ot[:, m * 128:(m + 1) * 128], op=OP.add)
                        ins2 = e.engine_nop()
                        inc(ins2, sVt, 1)
                        mark("f3_ta_%d_%d_%d" % (half, m, b), sV)

            for b in range(4):
                @blk.scalar
                def _(e, half=half, b=b):
                    e = ES(e, sS)
                    for m in range(KC):
                        wv(e, "f3_ta_%d_%d_%d" % (half, m, b))
                    tb = half * 4 + b
                    ins = e.dma_start(out=y[tb * 128:(tb + 1) * 128, :],
                                      in_=(ot0, ot1, ot2, ot3)[b][:])
                    inc(ins, sO, 16)
                mark("out_%d_%d" % (half, b), sO)
            mark("out_all_%d" % half, sO)

        @blk.scalar
        def _(e):
            e = ES(e, sS)
            e.wait_ge(sO.h, sO.n)

    return nc


_NC_CACHE = None


def _get_nc():
    global _NC_CACHE
    if _NC_CACHE is None:
        _NC_CACHE = build_nc()
    return _NC_CACHE


def _host_prep(inputs):
    x = np.ascontiguousarray(np.asarray(inputs["L0_attn_x"], dtype=np.float32))
    qkv_w = np.asarray(inputs["L0_attn_qkv_w"], dtype=np.float32)
    proj_w = np.asarray(inputs["L0_attn_proj_w"], dtype=np.float32)
    proj_b = np.asarray(inputs["L0_attn_proj_b"], dtype=np.float32)
    w_gate = np.asarray(inputs["L0_ffn_w_gate"], dtype=np.float32)
    w_up = np.asarray(inputs["L0_ffn_w_up"], dtype=np.float32)
    w_down = np.asarray(inputs["L0_ffn_w_down"], dtype=np.float32)
    lnw = np.asarray(inputs["L0_ffn_ln_weight"], dtype=np.float32)

    wqkv = np.ascontiguousarray(
        qkv_w.reshape(3, 16, 128, D).transpose(1, 3, 0, 2).reshape(NPAIR, KC, 128, 384)
        .transpose(0, 2, 1, 3).reshape(NPAIR, 128, KC * 384))
    wproj = np.ascontiguousarray(
        proj_w.T.reshape(KC, 128, 4, 512).transpose(2, 0, 1, 3)
        .transpose(0, 2, 1, 3).reshape(4, 128, KC * 512))
    wg_a = np.ascontiguousarray(w_gate.T.reshape(KC, 128, 16, 512).transpose(2, 0, 1, 3))
    wu_a = np.ascontiguousarray(w_up.T.reshape(KC, 128, 16, 512).transpose(2, 0, 1, 3))
    wd_a = np.ascontiguousarray(w_down.T.reshape(IC, 128, 16, 128).transpose(2, 0, 1, 3))
    pbh = np.ascontiguousarray(proj_b.reshape(16, 128).T)
    lnwh = np.ascontiguousarray(lnw.reshape(64, 128).T)

    pp = np.arange(128)[:, None]
    ff = np.arange(SLOTW)[None, :]
    rel = ff - pp - 8
    band = (rel >= -8) & (rel < 8)
    mband = np.where(band, (8.0 - np.abs(rel) / 2.0) / 8.0, 0.0).astype(np.float32)
    mmulh = np.zeros((128, TSBW), np.float32)
    for s in range(8):
        mmulh[:, s * SLOTW:(s + 1) * SLOTW] = mband
    mmulh = mmulh.astype(ml_dtypes.bfloat16)

    in_maps = []
    for cidx in range(8):
        b, q0 = cidx // 4, (cidx % 4) * T
        xhc = np.zeros((TP, D), np.float32)
        klo = q0 - 8
        lo = max(klo, 0)
        hi = min(klo + TP, SEQ)
        xhc[lo - klo: hi - klo] = x[b, lo:hi]

        maddh = np.full((2, 128, TSBW), NEG, np.float32)
        for half in range(2):
            for jj in range(4):
                j = half * 4 + jj
                ig = q0 + j * 128 + pp
                jg = klo + j * 128 + ff
                valid = band & (jg >= 0) & (jg < SEQ)
                Aa = np.where(valid, 0.0, NEG).astype(np.float32)
                for h in range(2):
                    s = jj * 2 + h
                    maddh[half, :, s * SLOTW:(s + 1) * SLOTW] = Aa
            maddh[half, :, 8 * SLOTW] = 0.0
        maddh = maddh.astype(ml_dtypes.bfloat16)

        in_maps.append({
            "xh": xhc, "wqkv": wqkv, "wproj": wproj, "wg": wg_a, "wu": wu_a,
            "wd": wd_a, "pbh": pbh, "lnwh": lnwh, "mmulh": mmulh, "maddh": maddh,
        })
    return in_maps


def kernel(**inputs):
    in_maps = _host_prep(inputs)
    nc = _get_nc()
    res = run_bass_kernel_spmd(nc, in_maps, core_ids=list(range(8)))
    out = np.stack([r["y"] for r in res.results], axis=0)
    return out.reshape(2, SEQ, D)


# version 04
# speedup vs baseline: 1.3164x; 1.0338x over previous
import numpy as np
import ml_dtypes
import concourse.bass as bass
import concourse.mybir as mybir
from concourse.bass_utils import run_bass_kernel_spmd

F32 = mybir.dt.float32
BF16 = mybir.dt.bfloat16
AF = mybir.ActivationFunctionType
OP = mybir.AluOpType
AX = mybir.AxisListType

D = 2048
KC = 16
T = 1024
TP = 1152
TH = 1040
NT = 9
I = 8192
IC = 64
NPAIR = 16
SEQ = 4096
CR = float(1.5 * 2.0 ** 23)
NEG = -1.0e30
SLOTW = 144
TSBW = 9 * SLOTW
QKV_NT = [(0, 352), (352, 352), (704, 336)]
QKV_NT_V = [(0, 384), (384, 384), (768, 384)]


class Sem:
    def __init__(self, nc, name):
        self.h = nc.alloc_semaphore(name)
        self.n = 0


_COMPUTE_OPS = {
    "tensor_tensor", "tensor_scalar", "tensor_copy", "tensor_add", "tensor_sub",
    "tensor_mul", "tensor_max", "reduce_sum", "reduce_max", "tensor_reduce",
    "reciprocal", "activation", "copy", "mul", "add", "square", "sqrt", "sign",
    "memset", "memzero", "iota", "scalar_tensor_tensor", "tensor_single_scalar",
    "tensor_scalar_add", "tensor_scalar_sub", "tensor_scalar_mul",
    "tensor_scalar_max", "tensor_scalar_min", "select", "bn_stats", "bn_aggr",
}


class ES:

    def __init__(self, e, sem):
        self._e = e
        self._s = sem

    def __getattr__(self, name):
        fn = getattr(self._e, name)
        if name in _COMPUTE_OPS:
            s = self._s
            e = self._e

            def wrapped(*a, **k):
                if s.n > 0:
                    e.wait_ge(s.h, s.n)
                ins = fn(*a, **k)
                s.n += 1
                ins.then_inc(s.h, 1)
                return ins
            return wrapped
        if name == "dma_start":
            s = self._s
            e = self._e

            def wrapped_dma(*a, **k):
                if s.n > 0:
                    e.wait_ge(s.h, s.n)
                return fn(*a, **k)
            return wrapped_dma
        return fn


def build_nc():
    nc = bass.Bass("TRN2", target_bir_lowering=False, debug=False)

    xh = nc.dram_tensor("xh", [TP, D], F32, kind="ExternalInput").ap()
    wqkv = nc.dram_tensor("wqkv", [NPAIR, 128, KC * 384], F32, kind="ExternalInput").ap()
    wproj = nc.dram_tensor("wproj", [4, 128, KC * 512], F32, kind="ExternalInput").ap()
    wg_d = nc.dram_tensor("wg", [16, KC, 128, 512], F32, kind="ExternalInput").ap()
    wu_d = nc.dram_tensor("wu", [16, KC, 128, 512], F32, kind="ExternalInput").ap()
    wd_d = nc.dram_tensor("wd", [16, IC, 128, 128], F32, kind="ExternalInput").ap()
    pbh = nc.dram_tensor("pbh", [128, 16], F32, kind="ExternalInput").ap()
    lnwh = nc.dram_tensor("lnwh", [128, 64], F32, kind="ExternalInput").ap()
    mmulh = nc.dram_tensor("mmulh", [128, TSBW], BF16, kind="ExternalInput").ap()
    maddh = nc.dram_tensor("maddh", [2, 128, TSBW], BF16, kind="ExternalInput").ap()

    y = nc.dram_tensor("y", [T, D], F32, kind="ExternalOutput").ap()
    hscr = nc.dram_tensor("hscr", [IC, 128, T], F32, kind="Internal").ap()
    ascr = nc.dram_tensor("ascr", [KC, 128, T], F32, kind="Internal").ap()
    atscr = nc.dram_tensor("atscr", [KC, 128, T], BF16, kind="Internal").ap()

    A = nc.alloc_sbuf_tensor_at

    p = 16512
    def pal(name, shape, dtype):
        nonlocal p
        p = (p + 63) // 64 * 64
        t = A(name, shape, dtype, offset=p)
        esz = {BF16: 2, F32: 4}[dtype]
        p += int(np.prod(shape[1:])) * esz
        return t

    identf = pal("identf", [128, 128], F32)
    iof = pal("iof", [128, 128], F32)
    iop = pal("iop", [128, 1], F32)
    ones_bf = pal("ones_bf", [128, 128], BF16)
    ones_f = pal("ones_f", [128, 128], F32)
    ones1 = pal("ones1", [1, 128], F32)
    tmp128 = pal("tmp128", [128, 128], F32)
    c_round = pal("c_round", [128, 1], F32)
    c_nhalf = pal("c_nhalf", [128, 1], F32)
    mmul = pal("mmul", [128, TSBW], BF16)
    madd = pal("madd", [128, 2 * TSBW], BF16)
    pb_sb = pal("pb_sb", [128, 16], F32)
    lnw_sb = pal("lnw_sb", [128, 64], F32)
    partials = pal("partials", [128, 48], F32)
    swg = pal("swg", [128, 1], F32); iswg = pal("iswg", [128, 1], F32)
    swu = pal("swu", [128, 1], F32); iswu = pal("iswu", [128, 1], F32)
    swd = pal("swd", [128, 1], F32); iswd = pal("iswd", [128, 1], F32)
    ln_mean = pal("ln_mean", [128, NT], F32)
    ln_rstd = pal("ln_rstd", [128, NT], F32)
    ln_srow = pal("ln_srow", [128, NT], F32)
    isx_row = pal("isx_row", [1, TP], F32)
    ISX = pal("ISX", [128, T], F32)
    Mhi = pal("Mhi", [128, T], F32)
    Mlo = pal("Mlo", [128, T], F32)
    r_bc = pal("r_bc", [128, T], F32)
    RS = pal("RS", [128, T], F32)
    INVD = pal("INVD", [128, T], F32)
    sh_row = pal("sh_row", [1, T], F32)
    iv_row = pal("iv_row", [1, T], F32)
    amax_tok = pal("amax_tok", [128, 8], F32)
    rtok = pal("rtok", [128, 8], F32)
    mrms = pal("mrms", [128, 8], F32)
    shtok = pal("shtok", [128, 8], F32)
    ivtok = pal("ivtok", [128, 8], F32)
    negmax = pal("negmax", [128, 1], F32)
    rsum = pal("rsum", [128, 9], F32)
    rinv = pal("rinv", [128, 9], F32)
    tv1 = pal("tv1", [128, 8], F32)
    tv2 = pal("tv2", [128, 8], F32)
    assert p < 66512, p

    xnT = A("xnT", [128, KC * TP], BF16, offset=66560)
    qxT = A("qxT", [128, KC * TP], BF16, offset=66560)
    hq = A("hq", [128, IC * 512], BF16, offset=66560)
    ot3 = A("ot3", [128, D], F32, offset=132096)
    xtok = A("xtok", [128, D], F32, offset=103424)
    xn_t = A("xn_t", [128, D], F32, offset=111744)
    xnb_t = A("xnb_t", [128, D], BF16, offset=119936)
    sq_t = A("sq_t", [128, D], BF16, offset=124032)
    T_sb = A("T_sb", [128, TSBW], F32, offset=103424)
    E_bf = A("E_bf", [128, TSBW], BF16, offset=108672)
    P_bf = A("P_bf", [128, TSBW], BF16, offset=111296)
    PT_a = A("PT_a", [128, 8 * 128], BF16, offset=113920)
    PT_b = A("PT_b", [128, 8 * 128], BF16, offset=115968)
    ath0 = A("ath0", [64, 2 * T], BF16, offset=118016)
    ath1 = A("ath1", [64, 2 * T], BF16, offset=130304)
    wstage = A("wstage", [128, KC * 512], F32, offset=103424)
    wq_bf0 = A("wq_bf0", [128, KC * 384], BF16, offset=140352)
    wq_bf1 = A("wq_bf1", [128, KC * 384], BF16, offset=152640)
    qm0 = A("qm0", [128, T], BF16, offset=164928)
    qm1 = A("qm1", [128, T], BF16, offset=167040)
    kT_p = A("kT_p", [128, TH], BF16, offset=128192)
    vT_p = A("vT_p", [128, TP], BF16, offset=169152)
    atrb = A("atrb", [128, 2 * T], BF16, offset=140352)
    tern_g = A("tern_g", [128, KC * 512], BF16, offset=140352)
    tern_u = A("tern_u", [128, KC * 512], BF16, offset=156736)
    wdstage = A("wdstage", [128, IC * 128], F32, offset=140352)
    v_tok = [A("v_tok0", [128, NT * 128], BF16, offset=173120),
             A("v_tok1", [128, NT * 128], BF16, offset=124032)]
    f0st = A("f0st", [128, 2 * 2048], F32, offset=175552)
    astg = A("astg", [128, 4 * T], F32, offset=173120)
    xtok2 = A("xtok2", [128, D], F32, offset=173120)
    xn2 = A("xn2", [128, D], F32, offset=181312)
    qx2 = A("qx2", [128, D], BF16, offset=189568)
    hstage = A("hstage", [128, 4 * T], F32, offset=173120)
    sg_t = A("sg_t", [128, 512], F32, offset=189568)
    sgp_t = A("sgp_t", [128, 512], F32, offset=191616)
    hl_t = A("hl_t", [128, 512], F32, offset=193728)
    hsq_t = A("hsq_t", [128, 512], BF16, offset=195840)
    hrb = A("hrb", [128, 2 * 512], F32, offset=189568)
    tern_d = A("tern_d", [128, IC * 128], BF16, offset=173120)
    aprb = A("aprb", [128, 2 * 512], F32, offset=196928)
    ISXU = A("ISXU", [128, T], F32, offset=196928)
    fin_t = A("fin_t", [128, 512], F32, offset=201024)
    ot2 = A("ot2", [128, D], F32, offset=203136)
    pw_bf = A("pw_bf", [128, KC * 512], BF16, offset=203136)
    ot0 = A("ot0", [128, D], F32, offset=211328)
    ot1 = A("ot1", [128, D], F32, offset=219520)

    psA = nc.alloc_psum_tensor("psA", [128, 4096], F32)

    def bank(i, w=512):
        return psA[:, i * 512:i * 512 + w]

    sV = Sem(nc, "sV")
    sS = Sem(nc, "sS")
    sM = Sem(nc, "sM")
    sP = Sem(nc, "sP")
    sO = Sem(nc, "sO")
    sF0 = Sem(nc, "sF0")
    sVt = Sem(nc, "sVt")
    sVm = Sem(nc, "sVm")
    sXm = Sem(nc, "sXm")
    sXx = Sem(nc, "sXx")
    sXw = Sem(nc, "sXw")
    sXa = [Sem(nc, "sXa0"), Sem(nc, "sXa1")]
    sTp = Sem(nc, "sTp")
    sTt = Sem(nc, "sTt")
    sTv = Sem(nc, "sTv")
    sTat = [Sem(nc, "sTat0"), Sem(nc, "sTat1")]
    sTa4 = Sem(nc, "sTa4")
    sTh = Sem(nc, "sTh")
    sTr = [Sem(nc, "sTr0"), Sem(nc, "sTr1")]
    sTap = [Sem(nc, "sTap0"), Sem(nc, "sTap1")]
    sGw = [Sem(nc, "sGw0"), Sem(nc, "sGw1")]
    sGf = [Sem(nc, "sGf0"), Sem(nc, "sGf1")]
    sGp = Sem(nc, "sGp")

    ev = {}

    def mark(label, s):
        ev[label] = (s, s.n)

    def wv(eng, label):
        s, n = ev[label]
        if n > 0:
            eng.wait_ge(s.h, n)

    def inc(ins, s, k=1):
        s.n += k
        ins.then_inc(s.h, k)

    f0_list = []
    for wi in range(3):
        for g in range(16):
            for q in range(4):
                f0_list.append((wi, g, q))

    def f0_dma(e, qi):
        wi, g, q = f0_list[qi]
        if qi >= 2:
            e.wait_ge(sF0.h, qi - 1)
        wt = (wg_d, wu_d, wd_d)[wi]
        nk = 4 if wi < 2 else 16
        src = wt[g, q * nk:(q + 1) * nk, :, :]
        dst = f0st[:, (qi % 2) * 2048:(qi % 2) * 2048 + 2048].rearrange("p (k c) -> p k c", k=nk)
        ins = e.dma_start(out=dst, in_=src.rearrange("k p c -> p k c"))
        inc(ins, sGf[qi % 2], 16)
        mark("f0_dma_%d" % qi, sGf[qi % 2])

    def f0_red(e, qi):
        wi, g, q = f0_list[qi]
        wv(e, "f0_dma_%d" % qi)
        col = wi * 16 + g
        if q == 0:
            ins = e.tensor_reduce(out=partials[:, col:col + 1],
                                  in_=f0st[:, (qi % 2) * 2048:(qi % 2) * 2048 + 2048],
                                  axis=AX.X, op=OP.add, apply_absolute_value=True)
        else:
            e.tensor_reduce(out=tv1[:, 7:8], in_=f0st[:, (qi % 2) * 2048:(qi % 2) * 2048 + 2048],
                            axis=AX.X, op=OP.add, apply_absolute_value=True)
            ins = e.tensor_add(out=partials[:, col:col + 1], in0=partials[:, col:col + 1],
                               in1=tv1[:, 7:8])
        ins2 = e.engine_nop()
        inc(ins2, sF0, 1)

    with nc.Block() as blk:
        @blk.sync
        def _(e):
            ins = e.dma_start(out=mmul[:], in_=mmulh[:, :]); inc(ins, sXm, 16)
            ins = e.dma_start(out=madd[:].rearrange("p (a c) -> p a c", a=2),
                              in_=maddh.rearrange("a p c -> p a c")); inc(ins, sXm, 16)
            ins = e.dma_start(out=pb_sb[:], in_=pbh[:, :]); inc(ins, sXm, 16)
            ins = e.dma_start(out=lnw_sb[:], in_=lnwh[:, :]); inc(ins, sXm, 16)
        mark("init_dma", sXm)

        @blk.gpsimd
        def _(e):
            e = ES(e, sP)
            ins = e.iota(iof[:], [[1, 128]], channel_multiplier=0, allow_small_or_imprecise_dtypes=True)
            ins = e.iota(iop[:], [[0, 1]], channel_multiplier=1, allow_small_or_imprecise_dtypes=True)
            e.tensor_scalar(out=identf[:], in0=iof[:], scalar1=iop[:, 0:1], scalar2=None, op0=OP.is_equal)
            e.memset(ones_bf[:], 1.0)
            e.memset(ones_f[:], 1.0)
            e.memset(ones1[:], 1.0)
            e.memset(c_round[:], CR)
            e.memset(c_nhalf[:], -0.5)
            e.memset(qm0[:], 0.0)
            e.memset(qm1[:], 0.0)
            e.memset(Mhi[:], -1.0e30)
            e.memset(Mlo[:], 1.0e30)
            e.memset(v_tok[0][:], 0.0)
            ins = e.memset(v_tok[1][:], 0.0)
        mark("init_gp", sP)

        for t in range(NT):
            @blk.sync
            def _(e, t=t):
                if t > 0:
                    wv(e, "a1_done_%d" % (t - 1))
                ins = e.dma_start(out=xtok[:], in_=xh[t * 128:(t + 1) * 128, :])
                inc(ins, sXx, 16)
            mark("a1_x_%d" % t, sXx)

            @blk.vector
            def _(e, t=t):
                e = ES(e, sV)
                if t == 0:
                    wv(e, "init_gp")
                wv(e, "a1_x_%d" % t)
                e.reduce_sum(out=tv1[:, 0:1], in_=xtok[:], axis=AX.X)
                ins = e.tensor_scalar(out=ln_mean[:, t:t + 1], in0=tv1[:, 0:1],
                                      scalar1=1.0 / D, scalar2=None, op0=OP.mult)
            mark("a1_mean_%d" % t, sV)

            @blk.scalar
            def _(e, t=t):
                e = ES(e, sS)
                if t == 0:
                    wv(e, "init_gp")
                wv(e, "a1_x_%d" % t)
                ins = e.activation(out=sq_t[:], in_=xtok[:], func=AF.Square, accum_out=tv2[:, 0:1])
            mark("a1_ssq_%d" % t, sS)

            @blk.vector
            def _(e, t=t):
                e = ES(e, sV)
                wv(e, "a1_ssq_%d" % t)
                wv(e, "a1_mean_%d" % t)
                e.tensor_scalar(out=tv2[:, 1:2], in0=tv2[:, 0:1], scalar1=1.0 / D, scalar2=1e-5,
                                op0=OP.mult, op1=OP.add)
                e.tensor_mul(out=tv2[:, 2:3], in0=ln_mean[:, t:t + 1], in1=ln_mean[:, t:t + 1])
                ins = e.tensor_sub(out=tv2[:, 3:4], in0=tv2[:, 1:2], in1=tv2[:, 2:3])
            mark("a1_var_%d" % t, sV)

            @blk.scalar
            def _(e, t=t):
                e = ES(e, sS)
                wv(e, "a1_var_%d" % t)
                e.activation(out=tv2[:, 4:5], in_=tv2[:, 3:4], func=AF.Ln)
                ins = e.activation(out=ln_rstd[:, t:t + 1], in_=tv2[:, 4:5], func=AF.Exp,
                                   scale=c_nhalf[:, 0:1])
            mark("a1_rstd_%d" % t, sS)

            @blk.vector
            def _(e, t=t):
                e = ES(e, sV)
                wv(e, "a1_rstd_%d" % t)
                ins = e.tensor_scalar(out=xn_t[:], in0=xtok[:], scalar1=ln_mean[:, t:t + 1],
                                      scalar2=ln_rstd[:, t:t + 1], op0=OP.subtract, op1=OP.mult)
                e.reduce_max(out=tv1[:, 1:2], in_=xn_t[:], axis=AX.X, apply_absolute_value=True)
                e.tensor_scalar(out=tv1[:, 2:3], in0=tv1[:, 1:2], scalar1=1e-5, scalar2=None, op0=OP.max)
                e.reciprocal(out=tv1[:, 3:4], in_=tv1[:, 2:3])
                e.tensor_scalar(out=ln_srow[:, t:t + 1], in0=tv1[:, 3:4], scalar1=127.0,
                                scalar2=None, op0=OP.mult)
                ins = e.tensor_scalar(out=tmp128[:], in0=ones_f[:],
                                      scalar1=tv1[:, 2:3], scalar2=1.0 / 127.0, op0=OP.mult, op1=OP.mult)
            mark("a1_xn_%d" % t, sV)

            @blk.tensor
            def _(e, t=t):
                wv(e, "a1_xn_%d" % t)
                if t == 0:
                    wv(e, "init_gp")
                if t > 0:
                    wv(e, "a1_isxrow_%d" % (t - 1))
                ins = e.transpose(psA[:, 3584:3584 + 128], tmp128[:], identf[:])
                inc(ins, sM, 1)
            mark("a1_isxT_%d" % t, sM)

            @blk.vector
            def _(e, t=t):
                e = ES(e, sV)
                wv(e, "a1_isxT_%d" % t)
                ins = e.tensor_copy(out=isx_row[0:1, t * 128:(t + 1) * 128],
                                    in_=psA[0:1, 3584:3584 + 128])
            mark("a1_isxrow_%d" % t, sV)

            @blk.scalar
            def _(e, t=t):
                e = ES(e, sS)
                wv(e, "a1_xn_%d" % t)
                if t > 0:
                    wv(e, "a1_done_%d" % (t - 1))
                ins = e.copy(out=xnb_t[:], in_=xn_t[:])
            mark("a1_cast_%d" % t, sS)

            @blk.scalar
            def _(e, t=t):
                e = ES(e, sS)
                wv(e, "a1_cast_%d" % t)
                for kc in range(KC):
                    ins = e.dma_start(out=xnT[:, kc * TP + t * 128: kc * TP + (t + 1) * 128],
                                      in_=xnb_t[:, kc * 128:(kc + 1) * 128], transpose=True)
                    inc(ins, sTt, 16)
            mark("a1_done_%d" % t, sTt)
        mark("a1_all", sTt)

        @blk.tensor
        def _(e):
            wv(e, "a1_isxrow_%d" % (NT - 1))
            e.matmul(psA[:, 3072:3584], ones1[:], isx_row[0:1, 8:520], start=True, stop=True)
            ins = e.matmul(psA[:, 3584:4096], ones1[:], isx_row[0:1, 520:1032], start=True, stop=True)
            inc(ins, sM, 1)
        mark("isx_mm", sM)

        @blk.vector
        def _(e):
            e = ES(e, sV)
            wv(e, "isx_mm")
            e.tensor_copy(out=ISX[:, 0:512], in_=psA[:, 3072:3584])
            ins = e.tensor_copy(out=ISX[:, 512:1024], in_=psA[:, 3584:4096])
        mark("isx_done", sV)

        f0_emitted = 0

        def emit_wq(pr):
            wbuf = (wq_bf0, wq_bf1)[pr % 2]

            @blk.gpsimd
            def _(e, pr=pr, wbuf=wbuf):
                e = ES(e, sP)
                if pr >= 2:
                    wv(e, "qkv_mm_%d" % (pr - 2))
                ins = e.dma_start(out=wbuf[:], in_=wqkv[pr])
                inc(ins, sGw[pr % 2], 16)
            mark("wq_%d" % pr, sGw[pr % 2])

        emit_wq(0)
        def emit_qkv_pair(pr):
            wbuf = (wq_bf0, wq_bf1)[pr % 2]
            if pr + 1 < NPAIR:
                emit_wq(pr + 1)

            @blk.gpsimd
            def _(e, pr=pr):
                e = ES(e, sP)
                for k in range(12):
                    qi = pr * 12 + k
                    if qi < 192:
                        f0_dma(e, qi)

            for mi in range(3):
                for ti, (n0, nn) in enumerate(QKV_NT if mi < 2 else QKV_NT_V):
                    u = mi * 3 + ti

                    @blk.tensor
                    def _(e, pr=pr, mi=mi, ti=ti, n0=n0, nn=nn, u=u, wbuf=wbuf):
                        if u == 0:
                            wv(e, "wq_%d" % pr)
                            if pr == 0:
                                wv(e, "a1_all")
                                wv(e, "isx_done")
                        gu = pr * 9 + u
                        if gu >= 2:
                            wv(e, "qkv_ev_%d" % (gu - 2))
                        for kc in range(KC):
                            ins = e.matmul(bank(gu % 2, nn),
                                           wbuf[:, kc * 384 + mi * 128: kc * 384 + (mi + 1) * 128],
                                           xnT[:, kc * TP + n0: kc * TP + n0 + nn],
                                           start=(kc == 0), stop=(kc == KC - 1))
                        inc(ins, sM, 1)
                    mark("qkv_ps_%d_%d" % (pr, u), sM)

                    @blk.vector
                    def _(e, pr=pr, mi=mi, n0=n0, nn=nn, u=u):
                        e = ES(e, sV)
                        gu = pr * 9 + u
                        wv(e, "qkv_ps_%d_%d" % (pr, u))
                        if mi == 0:
                            s0 = max(n0, 8) - n0
                            s1 = min(n0 + nn, 8 + T) - n0
                            d0 = n0 + s0 - 8
                            ins = e.tensor_scalar(out=qm0[0:64, d0:d0 + (s1 - s0)],
                                                  in0=bank(gu % 2, nn)[0:64, s0:s1],
                                                  scalar1=0.125, scalar2=None, op0=OP.mult)
                            ins = e.tensor_scalar(out=qm1[64:128, d0:d0 + (s1 - s0)],
                                                  in0=bank(gu % 2, nn)[64:128, s0:s1],
                                                  scalar1=0.125, scalar2=None, op0=OP.mult)
                        else:
                            if mi == 2 and ti == 0 and pr > 0:
                                wv(e, "vtok_%d" % (pr - 1))
                            dst = (None, kT_p, vT_p)[mi]
                            ins = e.tensor_copy(out=dst[:, n0:n0 + nn], in_=bank(gu % 2, nn))
                    mark("qkv_ev_%d" % (pr * 9 + u), sV)
            mark("qkv_mm_%d" % pr, sM)
            mark("qkv_ev_done_%d" % pr, sV)

            @blk.scalar
            def _(e, pr=pr):
                e = ES(e, sS)
                wv(e, "qkv_ev_done_%d" % pr)
                if pr >= 2:
                    wv(e, "a3_pv_%d_1" % (pr - 2))
                for tt in range(NT):
                    ins = e.dma_start(out=v_tok[pr % 2][:, tt * 128: tt * 128 + 128],
                                      in_=vT_p[:, tt * 128: tt * 128 + 128], transpose=True)
                    inc(ins, sTv, 16)
            mark("vtok_%d" % pr, sTv)


        def emit_scores_half(pr, half):
            if True:
                hh = pr * 2 + half

                @blk.tensor
                def _(e, pr=pr, half=half, hh=hh):
                    if hh > 0:
                        wv(e, "a3_Tsb_%d" % (hh - 1))
                    for jj in range(4):
                        j = half * 4 + jj
                        for h in range(2):
                            slot = jj * 2 + h
                            off = 1024 + (slot // 3) * 512 + (slot % 3) * 144
                            qm = (qm0, qm1)[h]
                            ins = e.matmul(psA[:, off:off + 144],
                                           qm[:, j * 128: (j + 1) * 128],
                                           kT_p[:, j * 128: j * 128 + 144],
                                           start=True, stop=True)
                    off8 = 1024 + 2 * 512 + 2 * 144
                    ins = e.matmul(psA[:, off8:off8 + 144],
                                   qm0[:, half * 512: half * 512 + 128],
                                   kT_p[:, half * 512: half * 512 + 144],
                                   start=True, stop=True)
                    inc(ins, sM, 1)
                mark("a3_sc_%d" % hh, sM)

                @blk.vector
                def _(e, pr=pr, half=half, hh=hh):
                    e = ES(e, sV)
                    if hh == 0:
                        wv(e, "init_dma")
                    else:
                        wv(e, "a3_exp_%d" % (hh - 1))
                    wv(e, "a3_sc_%d" % hh)
                    psc = psA[:, 1024:1024 + 1536].rearrange("p (b s) -> p b s", b=3)[:, :, 0:432]
                    ins = e.tensor_tensor(out=T_sb[:].rearrange("p (b c) -> p b c", b=3), in0=psc,
                                          in1=mmul[:].rearrange("p (b c) -> p b c", b=3), op=OP.mult)
                    mark("a3_Tsb_%d" % hh, sV)
                    e.tensor_tensor(out=T_sb[:], in0=T_sb[:],
                                    in1=madd[:, half * TSBW:(half + 1) * TSBW], op=OP.add)
                    ins = e.reduce_max(out=negmax[:], in_=T_sb[:], axis=AX.X, negate=True)
                mark("a3_mx_%d" % hh, sV)

                @blk.scalar
                def _(e, hh=hh):
                    e = ES(e, sS)
                    wv(e, "a3_mx_%d" % hh)
                    ins = e.activation(out=E_bf[:], in_=T_sb[:], func=AF.Exp, bias=negmax[:, 0:1])
                mark("a3_exp_%d" % hh, sS)

                @blk.vector
                def _(e, hh=hh):
                    e = ES(e, sV)
                    wv(e, "a3_exp_%d" % hh)
                    if hh > 0:
                        wv(e, "a3_pt_%d" % (hh - 1))
                    e.reduce_sum(out=rsum[:], in_=E_bf[:].rearrange("p (s c) -> p s c", s=9), axis=AX.X)
                    e.reciprocal(out=rinv[:], in_=rsum[:])
                    for slot in range(9):
                        ins = e.tensor_scalar(out=P_bf[:, slot * 144:(slot + 1) * 144],
                                              in0=E_bf[:, slot * 144:(slot + 1) * 144],
                                              scalar1=rinv[:, slot:slot + 1], scalar2=None, op0=OP.mult)
                mark("a3_P_%d" % hh, sV)

                @blk.sync
                def _(e, hh=hh):
                    wv(e, "a3_P_%d" % hh)
                    if hh > 0:
                        wv(e, "a3_pv_%d_%d" % ((hh - 1) // 2, (hh - 1) % 2))
                    for slot in range(8):
                        ins = e.dma_start(out=PT_a[:, slot * 128:(slot + 1) * 128],
                                          in_=P_bf[:, slot * 144: slot * 144 + 128], transpose=True)
                        inc(ins, sTp, 16)
                        ins = e.dma_start(out=PT_b[:, slot * 128:(slot + 1) * 128],
                                          in_=P_bf[:, slot * 144 + 128: slot * 144 + 256], transpose=True)
                        inc(ins, sTp, 16)
                mark("a3_pt_%d" % hh, sTp)


        def emit_pv_half(pr, half):
            if True:
                hh = pr * 2 + half
                @blk.tensor
                def _(e, pr=pr, half=half, hh=hh):
                    wv(e, "a3_pt_%d" % hh)
                    wv(e, "vtok_%d" % pr)
                    if hh > 0:
                        wv(e, "a3_oev_%d" % (hh - 1))
                    for jj in range(4):
                        j = half * 4 + jj
                        for h in range(2):
                            slot = jj * 2 + h
                            out = psA[0:64, 2560 + slot * 128: 2560 + (slot + 1) * 128]
                            vt = v_tok[pr % 2]
                            e.matmul(out, vt[:, j * 128 + 64 * h: j * 128 + 64 * h + 64],
                                     PT_a[:, slot * 128:(slot + 1) * 128], start=True, stop=False)
                            ins = e.matmul(out,
                                           vt[0:16, (j + 1) * 128 + 64 * h: (j + 1) * 128 + 64 * h + 64],
                                           PT_b[0:16, slot * 128:(slot + 1) * 128], start=False, stop=True)
                    inc(ins, sM, 1)
                mark("a3_pv_%d_%d" % (pr, half), sM)

                @blk.vector
                def _(e, pr=pr, half=half):
                    e = ES(e, sV)
                    wv(e, "a3_pv_%d_%d" % (pr, half))
                    if pr >= 2 and half == 0:
                        wv(e, "at_spill_%d" % (pr - 2))
                    for jj in range(4):
                        for h in range(2):
                            slot = jj * 2 + h
                            dst = (ath0, ath1)[h]
                            ins = e.tensor_copy(out=dst[:, (pr % 2) * T + half * 512 + jj * 128:
                                                        (pr % 2) * T + half * 512 + (jj + 1) * 128],
                                                in_=psA[0:64, 2560 + slot * 128: 2560 + (slot + 1) * 128])
                mark("a3_oev_%d" % (pr * 2 + half), sV)

            if half == 1:
                @blk.scalar
                def _(e, pr=pr):
                    e = ES(e, sS)
                    wv(e, "a3_oev_%d" % (pr * 2 + 1))
                    ins = e.dma_start(out=atscr[pr][0:64, :], in_=ath0[:, (pr % 2) * T:(pr % 2) * T + T])
                    inc(ins, sTat[pr % 2], 16)
                    ins = e.dma_start(out=atscr[pr][64:128, :], in_=ath1[:, (pr % 2) * T:(pr % 2) * T + T])
                    inc(ins, sTat[pr % 2], 16)
                mark("at_spill_%d" % pr, sTat[pr % 2])


        for pr in range(NPAIR):
            emit_qkv_pair(pr)
            for half in range(2):
                hh = pr * 2 + half
                if hh >= 1:
                    emit_pv_half((hh - 1) // 2, (hh - 1) % 2)
                emit_scores_half(pr, half)

            @blk.vector
            def _(e, pr=pr):
                e = ES(e, sV)
                nonlocal f0_emitted
                hi = min((pr + 1) * 12, 192)
                while f0_emitted < hi:
                    f0_red(e, f0_emitted)
                    f0_emitted += 1
        emit_pv_half(NPAIR - 1, 1)
        @blk.vector
        def _(e):
            e = ES(e, sV)
            nonlocal f0_emitted
            while f0_emitted < 192:
                f0_red(e, f0_emitted)
                f0_emitted += 1
        mark("f0_red_all", sV)

        @blk.vector
        def _(e):
            e = ES(e, sV)
            wv(e, "f0_red_all")
            e.reduce_sum(out=tv1[:, 0:1], in_=partials[:, 0:16], axis=AX.X)
            e.reduce_sum(out=tv1[:, 1:2], in_=partials[:, 16:32], axis=AX.X)
            ins = e.reduce_sum(out=tv1[:, 2:3], in_=partials[:, 32:48], axis=AX.X)
        mark("f0_sums", sV)

        @blk.tensor
        def _(e):
            wv(e, "f0_sums")
            wv(e, "a3_oev_%d" % (NPAIR * 2 - 1))
            e.matmul(psA[:, 3072:3073], ones_f[:], tv1[:, 0:1], start=True, stop=True)
            e.matmul(psA[:, 3073:3074], ones_f[:], tv1[:, 1:2], start=True, stop=True)
            ins = e.matmul(psA[:, 3074:3075], ones_f[:], tv1[:, 2:3], start=True, stop=True)
            inc(ins, sM, 1)
        mark("f0_bc", sM)

        @blk.vector
        def _(e):
            e = ES(e, sV)
            wv(e, "f0_bc")
            nel = float(I * D)
            for wi, (ssb, isb) in enumerate(((swg, iswg), (swu, iswu), (swd, iswd))):
                e.tensor_scalar(out=isb[:], in0=psA[:, 3072 + wi:3073 + wi],
                                scalar1=1.0 / nel, scalar2=1e-5, op0=OP.mult, op1=OP.max)
                ins = e.reciprocal(out=ssb[:], in_=isb[:])
            ins = e.tensor_scalar(out=ISXU[:], in0=ISX[:], scalar1=iswu[:, 0:1],
                                  scalar2=None, op0=OP.mult)
        mark("f0_scales", sV)

        for g in range(4):
            @blk.gpsimd
            def _(e, g=g):
                e = ES(e, sP)
                if g == 0:
                    wv(e, "f0_red_all")
                else:
                    wv(e, "a4_mm_%d" % (g - 1))
                ins = e.dma_start(out=pw_bf[:], in_=wproj[g])
                inc(ins, sGp, 16)
            mark("a4_w_%d" % g, sGp)

            for kc in range(KC):
                @blk.sync
                def _(e, g=g, kc=kc):
                    wv(e, "at_spill_%d" % kc)
                    gk = g * KC + kc
                    if gk >= 2:
                        wv(e, "a4_mmk_%d" % (gk - 2))
                    ins = e.dma_start(out=atrb[:, (kc % 2) * T:(kc % 2) * T + T], in_=atscr[kc])
                    inc(ins, sXa[kc % 2], 16)
                mark("a4_rb_%d_%d" % (g, kc), sXa[kc % 2])

                @blk.tensor
                def _(e, g=g, kc=kc):
                    if kc == 0:
                        wv(e, "a4_w_%d" % g)
                        if g == 0:
                            wv(e, "f0_scales")
                        else:
                            wv(e, "a4_ev_%d" % (g - 1))
                    wv(e, "a4_rb_%d_%d" % (g, kc))
                    for mi in range(4):
                        for nt in range(2):
                            ins = e.matmul(bank(mi * 2 + nt),
                                           pw_bf[:, kc * 512 + mi * 128: kc * 512 + (mi + 1) * 128],
                                           atrb[:, (kc % 2) * T + nt * 512:
                                                (kc % 2) * T + (nt + 1) * 512],
                                           start=(kc == 0), stop=(kc == KC - 1))
                    inc(ins, sM, 1)
                mark("a4_mmk_%d" % (g * KC + kc), sM)
            mark("a4_mm_%d" % g, sM)

            @blk.vector
            def _(e, g=g):
                e = ES(e, sV)
                wv(e, "a4_mm_%d" % g)
                if g >= 1:
                    wv(e, "a4_spill_%d" % (g - 1))
                for mi in range(4):
                    for nt in range(2):
                        m = g * 4 + mi
                        ins = e.tensor_scalar(out=astg[:, mi * T + nt * 512: mi * T + (nt + 1) * 512],
                                              in0=bank(mi * 2 + nt), scalar1=pb_sb[:, m:m + 1],
                                              scalar2=None, op0=OP.add)
            mark("a4_ev_%d" % g, sV)

            @blk.scalar
            def _(e, g=g):
                e = ES(e, sS)
                wv(e, "a4_ev_%d" % g)
                ins = e.dma_start(out=ascr[g * 4:(g + 1) * 4].rearrange("k p c -> p k c"),
                                  in_=astg[:].rearrange("p (k c) -> p k c", k=4))
                inc(ins, sTa4, 16)
            mark("a4_spill_%d" % g, sTa4)

        for t in range(NT):
            @blk.sync
            def _(e, t=t):
                if t == 0:
                    wv(e, "a4_spill_3")
                else:
                    wv(e, "f1r_t_%d" % (t - 1))
                ins = e.dma_start(out=xtok2[:], in_=xh[t * 128:(t + 1) * 128, :])
                inc(ins, sXx, 16)
            mark("f1r_x_%d" % t, sXx)

            @blk.vector
            def _(e, t=t):
                e = ES(e, sV)
                wv(e, "f1r_x_%d" % t)
                ins = e.tensor_scalar(out=xn2[:], in0=xtok2[:], scalar1=ln_mean[:, t:t + 1],
                                      scalar2=ln_rstd[:, t:t + 1], op0=OP.subtract, op1=OP.mult)
            mark("f1r_xn_%d" % t, sV)

            @blk.scalar
            def _(e, t=t):
                e = ES(e, sS)
                wv(e, "f1r_xn_%d" % t)
                ins = e.activation(out=xn2[:], in_=xn2[:], func=AF.Identity,
                                   scale=ln_srow[:, t:t + 1], bias=c_round[:, 0:1])
            mark("f1r_y_%d" % t, sS)

            @blk.vector
            def _(e, t=t):
                e = ES(e, sV)
                wv(e, "f1r_y_%d" % t)
                ins = e.tensor_scalar(out=qx2[:], in0=xn2[:], scalar1=CR, scalar2=None, op0=OP.subtract)
            mark("f1r_qx_%d" % t, sV)

            @blk.scalar
            def _(e, t=t):
                e = ES(e, sS)
                wv(e, "f1r_qx_%d" % t)
                for kc in range(KC):
                    ins = e.dma_start(out=qxT[:, kc * TP + t * 128: kc * TP + (t + 1) * 128],
                                      in_=qx2[:, kc * 128:(kc + 1) * 128], transpose=True)
                    inc(ins, sTt, 16)
            mark("f1r_t_%d" % t, sTt)
        mark("f1r_all", sTt)

        for g in range(16):
            for which in range(2):
                lab = "%d_%d" % (g, which)
                warr = (wg_d, wu_d)[which]
                tdst = (tern_g, tern_u)[which]
                ssc = (swg, swu)[which]

                @blk.sync
                def _(e, g=g, which=which, warr=warr):
                    if which == 1:
                        wv(e, "f1_tern_%d_0" % g)
                    elif g > 0:
                        wv(e, "f1_tern_%d_1" % (g - 1))
                    else:
                        wv(e, "f1r_all")
                        wv(e, "at_spill_%d" % (NPAIR - 1))
                        wv(e, "a4_mm_3")
                    ins = e.dma_start(out=wstage[:].rearrange("p (k c) -> p k c", k=KC),
                                      in_=warr[g].rearrange("k p c -> p k c"))
                    inc(ins, sXw, 16)
                mark("f1_wdma_%s" % lab, sXw)

                @blk.scalar
                def _(e, g=g, which=which, lab=lab, ssc=ssc):
                    e = ES(e, sS)
                    wv(e, "f1_wdma_%s" % lab)
                    if g == 0 and which == 0:
                        wv(e, "f0_scales")
                    ins = e.activation(out=wstage[:], in_=wstage[:], func=AF.Identity,
                                       scale=ssc[:, 0:1], bias=c_round[:, 0:1])
                mark("f1_tern_s_%s" % lab, sS)

                @blk.vector
                def _(e, g=g, which=which, lab=lab, tdst=tdst):
                    e = ES(e, sV)
                    wv(e, "f1_tern_s_%s" % lab)
                    if g > 0:
                        wv(e, "f1_mm_%d_%d" % (g - 1, which))
                    e.tensor_scalar(out=wstage[:], in0=wstage[:], scalar1=CR, scalar2=1.0,
                                    op0=OP.subtract, op1=OP.min)
                    ins = e.tensor_scalar(out=tdst[:], in0=wstage[:], scalar1=-1.0,
                                          scalar2=None, op0=OP.max)
                mark("f1_tern_%s" % lab, sV)

            for mi in range(4):
                c = g * 4 + mi
                for nt in range(2):
                    u = mi * 2 + nt
                    gu = g * 8 + u
                    lab = "%d_%d" % (c, nt)

                    @blk.tensor
                    def _(e, g=g, mi=mi, nt=nt, u=u, gu=gu):
                        if u == 0:
                            wv(e, "f1_tern_%d_0" % g)
                            if g == 0:
                                wv(e, "f1r_all")
                                wv(e, "f1s_prev_none" if False else "f0_scales")
                        if gu >= 2:
                            wv(e, "f1_gg_ev_%d" % (gu - 2))
                        for kc in range(KC):
                            ins = e.matmul(bank(u % 2),
                                           tern_g[:, kc * 512 + mi * 128: kc * 512 + (mi + 1) * 128],
                                           qxT[:, kc * TP + 8 + nt * 512: kc * TP + 8 + (nt + 1) * 512],
                                           start=(kc == 0), stop=(kc == KC - 1))
                        inc(ins, sM, 1)
                    mark("f1_gp_%d" % gu, sM)

                    @blk.tensor
                    def _(e, g=g, mi=mi, nt=nt, u=u, gu=gu):
                        if u == 0:
                            wv(e, "f1_tern_%d_1" % g)
                        if gu >= 2:
                            wv(e, "f1_h_ev_%d" % (gu - 2))
                        for kc in range(KC):
                            ins = e.matmul(bank(2 + u % 2),
                                           tern_u[:, kc * 512 + mi * 128: kc * 512 + (mi + 1) * 128],
                                           qxT[:, kc * TP + 8 + nt * 512: kc * TP + 8 + (nt + 1) * 512],
                                           start=(kc == 0), stop=(kc == KC - 1))
                        inc(ins, sM, 1)
                    mark("f1_up_%d" % gu, sM)

                    @blk.vector
                    def _(e, u=u, nt=nt, gu=gu):
                        e = ES(e, sV)
                        wv(e, "f1_gp_%d" % gu)
                        ins = e.scalar_tensor_tensor(out=sg_t[:], in0=bank(u % 2), scalar=iswg[:, 0:1],
                                                     in1=ISX[:, nt * 512:(nt + 1) * 512],
                                                     op0=OP.mult, op1=OP.mult)
                    mark("f1_gg_%s" % lab, sV)
                    mark("f1_gg_ev_%d" % gu, sV)

                    @blk.scalar
                    def _(e, lab=lab):
                        e = ES(e, sS)
                        wv(e, "f1_gg_%s" % lab)
                        ins = e.activation(out=sg_t[:], in_=sg_t[:], func=AF.Silu)
                    mark("f1_sil_%s" % lab, sS)

                    @blk.vector
                    def _(e, nt=nt, lab=lab):
                        e = ES(e, sV)
                        wv(e, "f1_sil_%s" % lab)
                        ins = e.tensor_tensor(out=sgp_t[:], in0=sg_t[:],
                                              in1=ISXU[:, nt * 512:(nt + 1) * 512], op=OP.mult)
                    mark("f1_sgp_%s" % lab, sV)

                    @blk.vector
                    def _(e, g=g, mi=mi, nt=nt, u=u, gu=gu, lab=lab):
                        e = ES(e, sV)
                        wv(e, "f1_sgp_%s" % lab)
                        wv(e, "f1_up_%d" % gu)
                        if g > 0 and u <= 1:
                            wv(e, "f1_spill_%d" % (g - 1))
                        ins = e.tensor_tensor(out=hstage[:, mi * T + nt * 512: mi * T + (nt + 1) * 512],
                                              in0=bank(2 + u % 2), in1=sgp_t[:], op=OP.mult)
                    mark("f1_h_%s" % lab, sV)
                    mark("f1_h_ev_%d" % gu, sV)

                    @blk.scalar
                    def _(e, g=g, mi=mi, nt=nt, gu=gu, lab=lab):
                        e = ES(e, sS)
                        wv(e, "f1_h_%s" % lab)
                        if gu >= 1:
                            wv(e, "f1_ssq_%d_%d" % ((gu - 1) // 2, (gu - 1) % 2))
                        ins = e.activation(out=hsq_t[:],
                                           in_=hstage[:, mi * T + nt * 512: mi * T + (nt + 1) * 512],
                                           func=AF.Square)
                    mark("f1_hsq_%s" % lab, sS)

                    @blk.tensor
                    def _(e, c=c, nt=nt, lab=lab):
                        wv(e, "f1_hsq_%s" % lab)
                        ins = e.matmul(psA[:, 3072 + nt * 512: 3072 + (nt + 1) * 512],
                                       ones_bf[:], hsq_t[:],
                                       start=(c == 0), stop=(c == IC - 1))
                        inc(ins, sM, 1)
                    mark("f1_ssq_%s" % lab, sM)

                    @blk.vector
                    def _(e, c=c, mi=mi, nt=nt, gu=gu, lab=lab):
                        e = ES(e, sV)
                        wv(e, "f1_h_%s" % lab)
                        hs = hstage[:, mi * T + nt * 512: mi * T + (nt + 1) * 512]
                        e.scalar_tensor_tensor(out=Mhi[:, nt * 512:(nt + 1) * 512], in0=hs,
                                               scalar=lnw_sb[:, c:c + 1],
                                               in1=Mhi[:, nt * 512:(nt + 1) * 512],
                                               op0=OP.mult, op1=OP.max)
                        ins = e.scalar_tensor_tensor(out=Mlo[:, nt * 512:(nt + 1) * 512], in0=hs,
                                                     scalar=lnw_sb[:, c:c + 1],
                                                     in1=Mlo[:, nt * 512:(nt + 1) * 512],
                                                     op0=OP.mult, op1=OP.min)
                    mark("f1_M_%s" % lab, sV)
            mark("f1_mm_%d_0" % g, sM)
            mark("f1_mm_%d_1" % g, sM)

            @blk.scalar
            def _(e, g=g):
                e = ES(e, sS)
                for mi in range(4):
                    for nt in range(2):
                        wv(e, "f1_hsq_%d_%d" % (g * 4 + mi, nt))
                        wv(e, "f1_M_%d_%d" % (g * 4 + mi, nt))
                ins = e.dma_start(out=hscr[g * 4:(g + 1) * 4].rearrange("k p c -> p k c"),
                                  in_=hstage[:].rearrange("p (k c) -> p k c", k=4))
                inc(ins, sTh, 16)
            mark("f1_spill_%d" % g, sTh)

        @blk.vector
        def _(e):
            e = ES(e, sV)
            wv(e, "f1_ssq_63_1")
            for mi in range(4):
                wv(e, "f1_M_%d_1" % (60 + mi))
            e.tensor_scalar(out=r_bc[:, 0:512], in0=psA[:, 3072:3584], scalar1=1.0 / I,
                            scalar2=1e-6, op0=OP.mult, op1=OP.add)
            ins = e.tensor_scalar(out=r_bc[:, 512:1024], in0=psA[:, 3584:4096], scalar1=1.0 / I,
                                  scalar2=1e-6, op0=OP.mult, op1=OP.add)
            e.tensor_scalar(out=Mlo[:], in0=Mlo[:], scalar1=-1.0, scalar2=None, op0=OP.mult)
            ins = e.tensor_tensor(out=Mhi[:], in0=Mhi[:], in1=Mlo[:], op=OP.max)
        mark("f1s_pre", sV)

        @blk.scalar
        def _(e):
            e = ES(e, sS)
            wv(e, "f1s_pre")
            e.activation(out=r_bc[:], in_=r_bc[:], func=AF.Ln)
            ins = e.activation(out=r_bc[:], in_=r_bc[:], func=AF.Exp, scale=c_nhalf[:, 0:1])
        mark("f1s_rbc", sS)

        @blk.tensor
        def _(e):
            wv(e, "f1s_pre")
            for b in range(8):
                if b >= 2:
                    e.wait_ge(sVm.h, b - 1)
                ins = e.transpose(psA[:, 2048 + (b % 2) * 512: 2048 + (b % 2) * 512 + 128],
                                  Mhi[:, b * 128:(b + 1) * 128], identf[:])
                inc(ins, sM, 1)
                mark("f1s_mt_%d" % b, sM)

        @blk.vector
        def _(e):
            e = ES(e, sV)
            for b in range(8):
                wv(e, "f1s_mt_%d" % b)
                ins = e.reduce_max(out=amax_tok[:, b:b + 1],
                                   in_=psA[:, 2048 + (b % 2) * 512: 2048 + (b % 2) * 512 + 128],
                                   axis=AX.X)
                ins2 = e.engine_nop()
                inc(ins2, sVm, 1)
        mark("f1s_am", sV)

        @blk.tensor
        def _(e):
            wv(e, "f1s_rbc")
            wv(e, "f1s_am")
            for b in range(8):
                ins = e.transpose(psA[:, 2048 + b * 128: 2048 + b * 128 + 128],
                                  r_bc[:, b * 128:(b + 1) * 128], identf[:])
                inc(ins, sM, 1)
                mark("f1s_rt_%d" % b, sM)

        @blk.vector
        def _(e):
            e = ES(e, sV)
            wv(e, "f1s_rt_%d" % 7)
            for b in range(8):
                ins = e.tensor_copy(out=rtok[:, b:b + 1],
                                    in_=psA[:, 2048 + b * 128: 2048 + b * 128 + 1])
            mark("f1s_rtok", sV)
            e.tensor_tensor(out=mrms[:], in0=rtok[:], in1=amax_tok[:], op=OP.mult)
            e.tensor_scalar(out=mrms[:], in0=mrms[:], scalar1=1e-5, scalar2=None, op0=OP.max)
            e.reciprocal(out=shtok[:], in_=mrms[:])
            e.tensor_scalar(out=shtok[:], in0=shtok[:], scalar1=127.0, scalar2=None, op0=OP.mult)
            e.tensor_scalar(out=ivtok[:], in0=mrms[:], scalar1=1.0 / 127.0,
                            scalar2=None, op0=OP.mult)
            ins = e.tensor_scalar(out=ivtok[:], in0=ivtok[:], scalar1=iswd[:, 0:1],
                                  scalar2=None, op0=OP.mult)
        mark("f1s_tok", sV)

        for b in range(8):
            for which, (colsrc, rowdst) in enumerate(((shtok, sh_row), (ivtok, iv_row))):
                lab = "%d_%d" % (b, which)

                @blk.vector
                def _(e, b=b, which=which, colsrc=colsrc, lab=lab):
                    e = ES(e, sV)
                    wv(e, "f1s_tok")
                    if b * 2 + which >= 2:
                        wv(e, "f1s_row_%d_%d" % ((b * 2 + which - 2) // 2, (b * 2 + which - 2) % 2))
                    ins = e.tensor_scalar(out=tmp128[:], in0=ones_f[:], scalar1=colsrc[:, b:b + 1],
                                          scalar2=None, op0=OP.mult)
                mark("f1s_bcc_%s" % lab, sV)

                @blk.tensor
                def _(e, b=b, which=which, lab=lab):
                    wv(e, "f1s_bcc_%s" % lab)
                    ins = e.transpose(psA[:, 2048 + ((b * 2 + which) % 2) * 512:
                                          2048 + ((b * 2 + which) % 2) * 512 + 128],
                                      tmp128[:], identf[:])
                    inc(ins, sM, 1)
                mark("f1s_ct_%s" % lab, sM)

                @blk.vector
                def _(e, b=b, which=which, rowdst=rowdst, lab=lab):
                    e = ES(e, sV)
                    wv(e, "f1s_ct_%s" % lab)
                    ins = e.tensor_copy(out=rowdst[0:1, b * 128:(b + 1) * 128],
                                        in_=psA[0:1, 2048 + ((b * 2 + which) % 2) * 512:
                                                2048 + ((b * 2 + which) % 2) * 512 + 128])
                mark("f1s_row_%s" % lab, sV)
        mark("f1s_rows", sV)

        @blk.tensor
        def _(e):
            wv(e, "f1s_rows")
            e.matmul(psA[:, 0:512], ones1[:], sh_row[0:1, 0:512], start=True, stop=True)
            e.matmul(psA[:, 512:1024], ones1[:], sh_row[0:1, 512:1024], start=True, stop=True)
            e.matmul(psA[:, 1024:1536], ones1[:], iv_row[0:1, 0:512], start=True, stop=True)
            ins = e.matmul(psA[:, 1536:2048], ones1[:], iv_row[0:1, 512:1024], start=True, stop=True)
            inc(ins, sM, 1)
        mark("f1s_bc", sM)

        @blk.vector
        def _(e):
            e = ES(e, sV)
            wv(e, "f1s_bc")
            e.tensor_tensor(out=RS[:, 0:512], in0=r_bc[:, 0:512], in1=psA[:, 0:512], op=OP.mult)
            e.tensor_tensor(out=RS[:, 512:1024], in0=r_bc[:, 512:1024], in1=psA[:, 512:1024], op=OP.mult)
            e.tensor_copy(out=INVD[:, 0:512], in_=psA[:, 1024:1536])
            ins = e.tensor_copy(out=INVD[:, 512:1024], in_=psA[:, 1536:2048])
        mark("f1s_done", sV)

        for half in range(2):
            c0 = half * 512
            for c in range(IC):
                @blk.scalar
                def _(e, half=half, c=c, c0=c0):
                    e = ES(e, sS)
                    if half == 0 and c < 2:
                        wv(e, "f1_spill_15")
                        wv(e, "f1s_done")
                    else:
                        lab = ("f2_q_%d_%d" % (half, c - 2)) if c >= 2 else \
                              ("f2_q_%d_%d" % (half - 1, IC - 2 + c))
                        wv(e, lab)
                        if half == 1 and c < 2:
                            wv(e, "f3_mm_0_%d" % (KC - 1))
                    ins = e.dma_start(out=hrb[:, (c % 2) * 512:(c % 2) * 512 + 512],
                                      in_=hscr[c][:, c0:c0 + 512])
                    inc(ins, sTr[c % 2], 16)
                mark("f2_rb_%d_%d" % (half, c), sTr[c % 2])

                @blk.vector
                def _(e, half=half, c=c, c0=c0):
                    e = ES(e, sV)
                    wv(e, "f2_rb_%d_%d" % (half, c))
                    buf = (hl_t, sgp_t)[c % 2]
                    ins = e.tensor_scalar(out=buf[:], in0=hrb[:, (c % 2) * 512:(c % 2) * 512 + 512],
                                          scalar1=lnw_sb[:, c:c + 1], scalar2=None, op0=OP.mult)
                    ins = e.tensor_tensor(out=buf[:], in0=buf[:], in1=RS[:, c0:c0 + 512], op=OP.mult)
                    ins = e.tensor_scalar(out=hq[:, c * 512:(c + 1) * 512], in0=buf[:],
                                          scalar1=CR, scalar2=CR, op0=OP.add, op1=OP.subtract)
                mark("f2_q_%d_%d" % (half, c), sV)
            mark("f2_done_%d" % half, sV)

            for b in range(4):
                @blk.sync
                def _(e, half=half, b=b):
                    if half > 0:
                        wv(e, "out_all_%d" % (half - 1))
                    else:
                        wv(e, "f1_tern_15_1")
                    tb = half * 4 + b
                    ins = e.dma_start(out=(ot0, ot1, ot2, ot3)[b][:],
                                      in_=xh[8 + tb * 128: 8 + (tb + 1) * 128, :])
                    inc(ins, sXx, 16)
                mark("f3_x_%d_%d" % (half, b), sXx)

            for m in range(KC):
                lab = "%d_%d" % (half, m)
                hm = half * KC + m

                @blk.sync
                def _(e, half=half, m=m, hm=hm):
                    if hm > 0:
                        wv(e, "f3_tern_%d_%d" % ((hm - 1) // KC, (hm - 1) % KC))
                    else:
                        wv(e, "f2_done_0")
                    ins = e.dma_start(out=wdstage[:].rearrange("p (k c) -> p k c", k=IC),
                                      in_=wd_d[m].rearrange("k p c -> p k c"))
                    inc(ins, sXw, 16)
                mark("f3_wdma_%s" % lab, sXw)

                @blk.scalar
                def _(e, lab=lab, hm=hm):
                    e = ES(e, sS)
                    wv(e, "f3_wdma_%s" % lab)
                    ins = e.activation(out=wdstage[:], in_=wdstage[:], func=AF.Identity,
                                       scale=swd[:, 0:1], bias=c_round[:, 0:1])
                mark("f3_tern_s_%s" % lab, sS)

                @blk.vector
                def _(e, lab=lab, hm=hm):
                    e = ES(e, sV)
                    wv(e, "f3_tern_s_%s" % lab)
                    if hm > 0:
                        wv(e, "f3_mm_%d_%d" % ((hm - 1) // KC, (hm - 1) % KC))
                    e.tensor_scalar(out=wdstage[:], in0=wdstage[:], scalar1=CR, scalar2=1.0,
                                    op0=OP.subtract, op1=OP.min)
                    ins = e.tensor_scalar(out=tern_d[:], in0=wdstage[:], scalar1=-1.0,
                                          scalar2=None, op0=OP.max)
                mark("f3_tern_%s" % lab, sV)

                @blk.scalar
                def _(e, half=half, m=m, hm=hm, c0=c0):
                    e = ES(e, sS)
                    if hm == 0:
                        wv(e, "a4_spill_3")
                        wv(e, "f1s_done")
                    if hm >= 2:
                        wv(e, "f3_fin_%d_%d" % ((hm - 2) // KC, (hm - 2) % KC))
                    ins = e.dma_start(out=aprb[:, (m % 2) * 512:(m % 2) * 512 + 512],
                                      in_=ascr[m][:, c0:c0 + 512])
                    inc(ins, sTap[m % 2], 16)
                mark("f3_ap_%s" % lab, sTap[m % 2])

                @blk.tensor
                def _(e, half=half, m=m, lab=lab, hm=hm):
                    wv(e, "f3_tern_%s" % lab)
                    wv(e, "f2_done_%d" % half)
                    if hm >= 2:
                        wv(e, "f3_ev_%d_%d" % ((hm - 2) // KC, (hm - 2) % KC))
                    for kc in range(IC):
                        ins = e.matmul(bank(m % 2), tern_d[:, kc * 128:(kc + 1) * 128],
                                       hq[:, kc * 512:(kc + 1) * 512],
                                       start=(kc == 0), stop=(kc == IC - 1))
                    inc(ins, sM, 1)
                mark("f3_mm_%s" % lab, sM)

                @blk.vector
                def _(e, half=half, m=m, lab=lab, c0=c0):
                    e = ES(e, sV)
                    wv(e, "f3_mm_%s" % lab)
                    wv(e, "f3_ap_%s" % lab)
                    e.tensor_tensor(out=fin_t[:], in0=bank(m % 2), in1=INVD[:, c0:c0 + 512], op=OP.mult)
                    ins = e.tensor_tensor(out=fin_t[:], in0=fin_t[:],
                                          in1=aprb[:, (m % 2) * 512:(m % 2) * 512 + 512], op=OP.add)
                mark("f3_fin_%s" % lab, sV)
                mark("f3_ev_%s" % lab, sV)

                @blk.tensor
                def _(e, half=half, m=m, lab=lab, hm=hm):
                    wv(e, "f3_fin_%s" % lab)
                    for b in range(4):
                        gidx = hm * 4 + b
                        if gidx >= 2:
                            e.wait_ge(sVt.h, gidx - 1)
                        ins = e.transpose(psA[:, 2048 + (gidx % 2) * 512: 2048 + (gidx % 2) * 512 + 128],
                                          fin_t[:, b * 128:(b + 1) * 128], identf[:])
                        inc(ins, sM, 1)
                        mark("f3_tt_%d_%d_%d" % (half, m, b), sM)

                @blk.vector
                def _(e, half=half, m=m, hm=hm):
                    e = ES(e, sV)
                    for b in range(4):
                        gidx = hm * 4 + b
                        wv(e, "f3_tt_%d_%d_%d" % (half, m, b))
                        wv(e, "f3_x_%d_%d" % (half, 3))
                        ot = (ot0, ot1, ot2, ot3)[b]
                        ins = e.tensor_tensor(out=ot[:, m * 128:(m + 1) * 128],
                                              in0=psA[:, 2048 + (gidx % 2) * 512:
                                                      2048 + (gidx % 2) * 512 + 128],
                                              in1=ot[:, m * 128:(m + 1) * 128], op=OP.add)
                        ins2 = e.engine_nop()
                        inc(ins2, sVt, 1)
                        mark("f3_ta_%d_%d_%d" % (half, m, b), sV)

            for b in range(4):
                @blk.scalar
                def _(e, half=half, b=b):
                    e = ES(e, sS)
                    for m in range(KC):
                        wv(e, "f3_ta_%d_%d_%d" % (half, m, b))
                    tb = half * 4 + b
                    ins = e.dma_start(out=y[tb * 128:(tb + 1) * 128, :],
                                      in_=(ot0, ot1, ot2, ot3)[b][:])
                    inc(ins, sO, 16)
                mark("out_%d_%d" % (half, b), sO)
            mark("out_all_%d" % half, sO)

        @blk.scalar
        def _(e):
            e = ES(e, sS)
            e.wait_ge(sO.h, sO.n)

    return nc


_NC_CACHE = None


def _get_nc():
    global _NC_CACHE
    if _NC_CACHE is None:
        _NC_CACHE = build_nc()
    return _NC_CACHE


def _host_prep(inputs):
    x = np.ascontiguousarray(np.asarray(inputs["L0_attn_x"], dtype=np.float32))
    qkv_w = np.asarray(inputs["L0_attn_qkv_w"], dtype=np.float32)
    proj_w = np.asarray(inputs["L0_attn_proj_w"], dtype=np.float32)
    proj_b = np.asarray(inputs["L0_attn_proj_b"], dtype=np.float32)
    w_gate = np.asarray(inputs["L0_ffn_w_gate"], dtype=np.float32)
    w_up = np.asarray(inputs["L0_ffn_w_up"], dtype=np.float32)
    w_down = np.asarray(inputs["L0_ffn_w_down"], dtype=np.float32)
    lnw = np.asarray(inputs["L0_ffn_ln_weight"], dtype=np.float32)

    wqkv = np.ascontiguousarray(
        qkv_w.reshape(3, 16, 128, D).transpose(1, 3, 0, 2).reshape(NPAIR, KC, 128, 384)
        .transpose(0, 2, 1, 3).reshape(NPAIR, 128, KC * 384))
    wproj = np.ascontiguousarray(
        proj_w.T.reshape(KC, 128, 4, 512).transpose(2, 0, 1, 3)
        .transpose(0, 2, 1, 3).reshape(4, 128, KC * 512))
    wg_a = np.ascontiguousarray(w_gate.T.reshape(KC, 128, 16, 512).transpose(2, 0, 1, 3))
    wu_a = np.ascontiguousarray(w_up.T.reshape(KC, 128, 16, 512).transpose(2, 0, 1, 3))
    wd_a = np.ascontiguousarray(w_down.T.reshape(IC, 128, 16, 128).transpose(2, 0, 1, 3))
    pbh = np.ascontiguousarray(proj_b.reshape(16, 128).T)
    lnwh = np.ascontiguousarray(lnw.reshape(64, 128).T)

    pp = np.arange(128)[:, None]
    ff = np.arange(SLOTW)[None, :]
    rel = ff - pp - 8
    band = (rel >= -8) & (rel < 8)
    mband = np.where(band, (8.0 - np.abs(rel) / 2.0) / 8.0, 0.0).astype(np.float32)
    mmulh = np.zeros((128, TSBW), np.float32)
    for s in range(8):
        mmulh[:, s * SLOTW:(s + 1) * SLOTW] = mband
    mmulh = mmulh.astype(ml_dtypes.bfloat16)

    in_maps = []
    for cidx in range(8):
        b, q0 = cidx // 4, (cidx % 4) * T
        xhc = np.zeros((TP, D), np.float32)
        klo = q0 - 8
        lo = max(klo, 0)
        hi = min(klo + TP, SEQ)
        xhc[lo - klo: hi - klo] = x[b, lo:hi]

        maddh = np.full((2, 128, TSBW), NEG, np.float32)
        for half in range(2):
            for jj in range(4):
                j = half * 4 + jj
                ig = q0 + j * 128 + pp
                jg = klo + j * 128 + ff
                valid = band & (jg >= 0) & (jg < SEQ)
                Aa = np.where(valid, 0.0, NEG).astype(np.float32)
                for h in range(2):
                    s = jj * 2 + h
                    maddh[half, :, s * SLOTW:(s + 1) * SLOTW] = Aa
            maddh[half, :, 8 * SLOTW] = 0.0
        maddh = maddh.astype(ml_dtypes.bfloat16)

        in_maps.append({
            "xh": xhc, "wqkv": wqkv, "wproj": wproj, "wg": wg_a, "wu": wu_a,
            "wd": wd_a, "pbh": pbh, "lnwh": lnwh, "mmulh": mmulh, "maddh": maddh,
        })
    return in_maps


def kernel(**inputs):
    in_maps = _host_prep(inputs)
    nc = _get_nc()
    res = run_bass_kernel_spmd(nc, in_maps, core_ids=list(range(8)))
    out = np.stack([r["y"] for r in res.results], axis=0)
    return out.reshape(2, SEQ, D)


# version 05
# speedup vs baseline: 1.3441x; 1.0210x over previous
import numpy as np
import ml_dtypes
import concourse.bass as bass
import concourse.mybir as mybir
from concourse.bass_utils import run_bass_kernel_spmd

F32 = mybir.dt.float32
BF16 = mybir.dt.bfloat16
AF = mybir.ActivationFunctionType
OP = mybir.AluOpType
AX = mybir.AxisListType

D = 2048
KC = 16
T = 1024
TP = 1152
TH = 1040
NT = 9
I = 8192
IC = 64
NPAIR = 16
SEQ = 4096
CR = float(1.5 * 2.0 ** 23)
NEG = -1.0e30
SLOTW = 144
TSBW = 9 * SLOTW
QKV_NT = [(0, 352), (352, 352), (704, 336)]
QKV_NT_V = [(0, 384), (384, 384), (768, 384)]


class Sem:
    def __init__(self, nc, name):
        self.h = nc.alloc_semaphore(name)
        self.n = 0


_COMPUTE_OPS = {
    "tensor_tensor", "tensor_scalar", "tensor_copy", "tensor_add", "tensor_sub",
    "tensor_mul", "tensor_max", "reduce_sum", "reduce_max", "tensor_reduce",
    "reciprocal", "activation", "copy", "mul", "add", "square", "sqrt", "sign",
    "memset", "memzero", "iota", "scalar_tensor_tensor", "tensor_single_scalar",
    "tensor_scalar_add", "tensor_scalar_sub", "tensor_scalar_mul",
    "tensor_scalar_max", "tensor_scalar_min", "select", "bn_stats", "bn_aggr",
}


class ES:

    def __init__(self, e, sem):
        self._e = e
        self._s = sem

    def __getattr__(self, name):
        fn = getattr(self._e, name)
        if name in _COMPUTE_OPS:
            s = self._s
            e = self._e

            def wrapped(*a, **k):
                if s.n > 0:
                    e.wait_ge(s.h, s.n)
                ins = fn(*a, **k)
                s.n += 1
                ins.then_inc(s.h, 1)
                return ins
            return wrapped
        if name == "dma_start":
            s = self._s
            e = self._e

            def wrapped_dma(*a, **k):
                if s.n > 0:
                    e.wait_ge(s.h, s.n)
                return fn(*a, **k)
            return wrapped_dma
        return fn


def build_nc():
    nc = bass.Bass("TRN2", target_bir_lowering=False, debug=False, num_swdge_queues=4)

    xh = nc.dram_tensor("xh", [TP, D], F32, kind="ExternalInput").ap()
    wqkv = nc.dram_tensor("wqkv", [NPAIR, 128, KC * 384], F32, kind="ExternalInput").ap()
    wproj = nc.dram_tensor("wproj", [4, 128, KC * 512], F32, kind="ExternalInput").ap()
    wg_d = nc.dram_tensor("wg", [16, KC, 128, 512], F32, kind="ExternalInput").ap()
    wu_d = nc.dram_tensor("wu", [16, KC, 128, 512], F32, kind="ExternalInput").ap()
    wd_d = nc.dram_tensor("wd", [16, IC, 128, 128], F32, kind="ExternalInput").ap()
    pbh = nc.dram_tensor("pbh", [128, 16], F32, kind="ExternalInput").ap()
    lnwh = nc.dram_tensor("lnwh", [128, 64], F32, kind="ExternalInput").ap()
    mmulh = nc.dram_tensor("mmulh", [128, TSBW], BF16, kind="ExternalInput").ap()
    maddh = nc.dram_tensor("maddh", [2, 128, TSBW], BF16, kind="ExternalInput").ap()

    y = nc.dram_tensor("y", [T, D], F32, kind="ExternalOutput").ap()
    hscr = nc.dram_tensor("hscr", [IC, 128, T], F32, kind="Internal").ap()
    ascr = nc.dram_tensor("ascr", [KC, 128, T], F32, kind="Internal").ap()
    atscr = nc.dram_tensor("atscr", [KC, 128, T], BF16, kind="Internal").ap()

    A = nc.alloc_sbuf_tensor_at

    p = 16512
    def pal(name, shape, dtype):
        nonlocal p
        p = (p + 63) // 64 * 64
        t = A(name, shape, dtype, offset=p)
        esz = {BF16: 2, F32: 4}[dtype]
        p += int(np.prod(shape[1:])) * esz
        return t

    identf = pal("identf", [128, 128], F32)
    iof = pal("iof", [128, 128], F32)
    iop = pal("iop", [128, 1], F32)
    ones_bf = pal("ones_bf", [128, 128], BF16)
    ones_f = pal("ones_f", [128, 128], F32)
    ones1 = pal("ones1", [1, 128], F32)
    tmp128 = pal("tmp128", [128, 128], F32)
    c_round = pal("c_round", [128, 1], F32)
    c_nhalf = pal("c_nhalf", [128, 1], F32)
    mmul = pal("mmul", [128, TSBW], BF16)
    madd = pal("madd", [128, 2 * TSBW], BF16)
    pb_sb = pal("pb_sb", [128, 16], F32)
    lnw_sb = pal("lnw_sb", [128, 64], F32)
    partials = pal("partials", [128, 48], F32)
    swg = pal("swg", [128, 1], F32); iswg = pal("iswg", [128, 1], F32)
    swu = pal("swu", [128, 1], F32); iswu = pal("iswu", [128, 1], F32)
    swd = pal("swd", [128, 1], F32); iswd = pal("iswd", [128, 1], F32)
    ln_mean = pal("ln_mean", [128, NT], F32)
    ln_rstd = pal("ln_rstd", [128, NT], F32)
    ln_srow = pal("ln_srow", [128, NT], F32)
    isx_row = pal("isx_row", [1, TP], F32)
    ISX = pal("ISX", [128, T], F32)
    Mhi = pal("Mhi", [128, T], F32)
    Mlo = pal("Mlo", [128, T], F32)
    r_bc = pal("r_bc", [128, T], F32)
    RS = pal("RS", [128, T], F32)
    INVD = pal("INVD", [128, T], F32)
    sh_row = pal("sh_row", [1, T], F32)
    iv_row = pal("iv_row", [1, T], F32)
    amax_tok = pal("amax_tok", [128, 8], F32)
    rtok = pal("rtok", [128, 8], F32)
    mrms = pal("mrms", [128, 8], F32)
    shtok = pal("shtok", [128, 8], F32)
    ivtok = pal("ivtok", [128, 8], F32)
    negmax = pal("negmax", [128, 1], F32)
    rsum = pal("rsum", [128, 9], F32)
    rinv = pal("rinv", [128, 9], F32)
    tv1 = pal("tv1", [128, 8], F32)
    tv2 = pal("tv2", [128, 8], F32)
    assert p < 66512, p

    xnT = A("xnT", [128, KC * TP], BF16, offset=66560)
    qxT = A("qxT", [128, KC * TP], BF16, offset=66560)
    hq = A("hq", [128, IC * 512], BF16, offset=66560)
    ot3 = A("ot3", [128, D], F32, offset=132096)
    xtok = A("xtok", [128, D], F32, offset=103424)
    xn_t = A("xn_t", [128, D], F32, offset=111744)
    xnb_t = A("xnb_t", [128, D], BF16, offset=119936)
    sq_t = A("sq_t", [128, D], BF16, offset=124032)
    T_sb = A("T_sb", [128, TSBW], F32, offset=103424)
    E_bf = A("E_bf", [128, TSBW], BF16, offset=108672)
    P_bf = A("P_bf", [128, TSBW], BF16, offset=111296)
    PT_a = A("PT_a", [128, 8 * 128], BF16, offset=113920)
    PT_b = A("PT_b", [128, 8 * 128], BF16, offset=115968)
    ath0 = A("ath0", [64, 2 * T], BF16, offset=118016)
    ath1 = A("ath1", [64, 2 * T], BF16, offset=130304)
    wstage = A("wstage", [128, KC * 512], F32, offset=103424)
    wq_bf0 = A("wq_bf0", [128, KC * 384], BF16, offset=140352)
    wq_bf1 = A("wq_bf1", [128, KC * 384], BF16, offset=152640)
    qm0 = A("qm0", [128, T], BF16, offset=164928)
    qm1 = A("qm1", [128, T], BF16, offset=167040)
    kT_p = A("kT_p", [128, TH], BF16, offset=128192)
    vT_p = A("vT_p", [128, TP], BF16, offset=169152)
    atrb = A("atrb", [128, 2 * T], BF16, offset=140352)
    tern_g = A("tern_g", [128, KC * 512], BF16, offset=140352)
    tern_u = A("tern_u", [128, KC * 512], BF16, offset=156736)
    wdstage = A("wdstage", [128, IC * 128], F32, offset=140352)
    v_tok = [A("v_tok0", [128, NT * 128], BF16, offset=173120),
             A("v_tok1", [128, NT * 128], BF16, offset=124032)]
    f0st = A("f0st", [128, 2 * 2048], F32, offset=175552)
    astg = A("astg", [128, 4 * T], F32, offset=173120)
    xtok2 = A("xtok2", [128, D], F32, offset=173120)
    xn2 = A("xn2", [128, D], F32, offset=181312)
    qx2 = A("qx2", [128, D], BF16, offset=189568)
    hstage = A("hstage", [128, 4 * T], F32, offset=173120)
    sg_t = A("sg_t", [128, 512], F32, offset=189568)
    sgp_t = A("sgp_t", [128, 512], F32, offset=191616)
    hl_t = A("hl_t", [128, 512], F32, offset=193728)
    hsq_t = A("hsq_t", [128, 512], BF16, offset=195840)
    hrb = A("hrb", [128, 2 * 512], F32, offset=189568)
    tern_d = A("tern_d", [128, IC * 128], BF16, offset=173120)
    aprb = A("aprb", [128, 2 * 512], F32, offset=196928)
    ISXU = A("ISXU", [128, T], F32, offset=196928)
    fin_t = A("fin_t", [128, 512], F32, offset=201024)
    ot2 = A("ot2", [128, D], F32, offset=203136)
    pw_bf = A("pw_bf", [128, KC * 512], BF16, offset=203136)
    ot0 = A("ot0", [128, D], F32, offset=211328)
    ot1 = A("ot1", [128, D], F32, offset=219520)

    psA = nc.alloc_psum_tensor("psA", [128, 4096], F32)

    def bank(i, w=512):
        return psA[:, i * 512:i * 512 + w]

    sV = Sem(nc, "sV")
    sS = Sem(nc, "sS")
    sM = Sem(nc, "sM")
    sP = Sem(nc, "sP")
    sO = Sem(nc, "sO")
    sF0 = Sem(nc, "sF0")
    sVt = Sem(nc, "sVt")
    sVm = Sem(nc, "sVm")
    sXm = Sem(nc, "sXm")
    sXx = Sem(nc, "sXx")
    sXw = Sem(nc, "sXw")
    sXw2 = Sem(nc, "sXw2")
    sXa = [Sem(nc, "sXa0"), Sem(nc, "sXa1")]
    sTp = Sem(nc, "sTp")
    sTt = Sem(nc, "sTt")
    sTv = Sem(nc, "sTv")
    sTat = [Sem(nc, "sTat0"), Sem(nc, "sTat1")]
    sTa4 = Sem(nc, "sTa4")
    sTh = Sem(nc, "sTh")
    sTr = [Sem(nc, "sTr0"), Sem(nc, "sTr1")]
    sTap = [Sem(nc, "sTap0"), Sem(nc, "sTap1")]
    sGw = [Sem(nc, "sGw0"), Sem(nc, "sGw1")]
    sGf = [Sem(nc, "sGf0"), Sem(nc, "sGf1")]
    sGp = Sem(nc, "sGp")

    ev = {}

    def mark(label, s):
        ev[label] = (s, s.n)

    def wv(eng, label):
        s, n = ev[label]
        if n > 0:
            eng.wait_ge(s.h, n)

    def inc(ins, s, k=1):
        s.n += k
        ins.then_inc(s.h, k)

    f0_list = []
    for wi in range(3):
        for g in range(16):
            for q in range(4):
                f0_list.append((wi, g, q))

    def f0_dma(e, qi):
        wi, g, q = f0_list[qi]
        if qi >= 2:
            e.wait_ge(sF0.h, qi - 1)
        wt = (wg_d, wu_d, wd_d)[wi]
        nk = 4 if wi < 2 else 16
        src = wt[g, q * nk:(q + 1) * nk, :, :]
        dst = f0st[:, (qi % 2) * 2048:(qi % 2) * 2048 + 2048].rearrange("p (k c) -> p k c", k=nk)
        ins = e.dma_start(out=dst, in_=src.rearrange("k p c -> p k c"))
        inc(ins, sGf[qi % 2], 16)
        mark("f0_dma_%d" % qi, sGf[qi % 2])

    def f0_abs_s(e, qi):
        wv(e, "f0_dma_%d" % qi)
        if qi >= 8:
            e.wait_ge(sF0.h, qi - 7)
        sl = f0st[:, (qi % 2) * 2048:(qi % 2) * 2048 + 2048]
        ins = e.activation(out=sl, in_=sl, func=AF.Abs,
                           accum_out=tv2[:, qi % 8: qi % 8 + 1])
        mark("f0_abs_%d" % qi, sS)

    def f0_red(e, qi):
        wi, g, q = f0_list[qi]
        wv(e, "f0_abs_%d" % qi)
        col = wi * 16 + g
        if q == 0:
            ins = e.tensor_copy(out=partials[:, col:col + 1], in_=tv2[:, qi % 8: qi % 8 + 1])
        else:
            ins = e.tensor_add(out=partials[:, col:col + 1], in0=partials[:, col:col + 1],
                               in1=tv2[:, qi % 8: qi % 8 + 1])
        mark("f0_addv_%d" % qi, sV)
        ins2 = e.engine_nop()
        inc(ins2, sF0, 1)

    with nc.Block() as blk:
        @blk.sync
        def _(e):
            ins = e.dma_start(out=mmul[:], in_=mmulh[:, :]); inc(ins, sXm, 16)
            ins = e.dma_start(out=madd[:].rearrange("p (a c) -> p a c", a=2),
                              in_=maddh.rearrange("a p c -> p a c")); inc(ins, sXm, 16)
            ins = e.dma_start(out=pb_sb[:], in_=pbh[:, :]); inc(ins, sXm, 16)
            ins = e.dma_start(out=lnw_sb[:], in_=lnwh[:, :]); inc(ins, sXm, 16)
        mark("init_dma", sXm)

        @blk.gpsimd
        def _(e):
            e = ES(e, sP)
            ins = e.iota(iof[:], [[1, 128]], channel_multiplier=0, allow_small_or_imprecise_dtypes=True)
            ins = e.iota(iop[:], [[0, 1]], channel_multiplier=1, allow_small_or_imprecise_dtypes=True)
            e.tensor_scalar(out=identf[:], in0=iof[:], scalar1=iop[:, 0:1], scalar2=None, op0=OP.is_equal)
            e.memset(ones_bf[:], 1.0)
            e.memset(ones_f[:], 1.0)
            e.memset(ones1[:], 1.0)
            e.memset(c_round[:], CR)
            e.memset(c_nhalf[:], -0.5)
            e.memset(qm0[:], 0.0)
            e.memset(qm1[:], 0.0)
            e.memset(Mhi[:], -1.0e30)
            e.memset(Mlo[:], 1.0e30)
            e.memset(v_tok[0][:], 0.0)
            ins = e.memset(v_tok[1][:], 0.0)
        mark("init_gp", sP)

        for t in range(NT):
            @blk.sync
            def _(e, t=t):
                if t > 0:
                    wv(e, "a1_done_%d" % (t - 1))
                ins = e.dma_start(out=xtok[:], in_=xh[t * 128:(t + 1) * 128, :])
                inc(ins, sXx, 16)
            mark("a1_x_%d" % t, sXx)

            @blk.vector
            def _(e, t=t):
                e = ES(e, sV)
                if t == 0:
                    wv(e, "init_gp")
                wv(e, "a1_x_%d" % t)
                e.reduce_sum(out=tv1[:, 0:1], in_=xtok[:], axis=AX.X)
                ins = e.tensor_scalar(out=ln_mean[:, t:t + 1], in0=tv1[:, 0:1],
                                      scalar1=1.0 / D, scalar2=None, op0=OP.mult)
            mark("a1_mean_%d" % t, sV)

            @blk.scalar
            def _(e, t=t):
                e = ES(e, sS)
                if t == 0:
                    wv(e, "init_gp")
                wv(e, "a1_x_%d" % t)
                ins = e.activation(out=sq_t[:], in_=xtok[:], func=AF.Square, accum_out=tv2[:, 0:1])
            mark("a1_ssq_%d" % t, sS)

            @blk.vector
            def _(e, t=t):
                e = ES(e, sV)
                wv(e, "a1_ssq_%d" % t)
                wv(e, "a1_mean_%d" % t)
                e.tensor_scalar(out=tv2[:, 1:2], in0=tv2[:, 0:1], scalar1=1.0 / D, scalar2=1e-5,
                                op0=OP.mult, op1=OP.add)
                e.tensor_mul(out=tv2[:, 2:3], in0=ln_mean[:, t:t + 1], in1=ln_mean[:, t:t + 1])
                ins = e.tensor_sub(out=tv2[:, 3:4], in0=tv2[:, 1:2], in1=tv2[:, 2:3])
            mark("a1_var_%d" % t, sV)

            @blk.scalar
            def _(e, t=t):
                e = ES(e, sS)
                wv(e, "a1_var_%d" % t)
                e.activation(out=tv2[:, 4:5], in_=tv2[:, 3:4], func=AF.Ln)
                ins = e.activation(out=ln_rstd[:, t:t + 1], in_=tv2[:, 4:5], func=AF.Exp,
                                   scale=c_nhalf[:, 0:1])
            mark("a1_rstd_%d" % t, sS)

            @blk.vector
            def _(e, t=t):
                e = ES(e, sV)
                wv(e, "a1_rstd_%d" % t)
                ins = e.tensor_scalar(out=xn_t[:], in0=xtok[:], scalar1=ln_mean[:, t:t + 1],
                                      scalar2=ln_rstd[:, t:t + 1], op0=OP.subtract, op1=OP.mult)
                e.reduce_max(out=tv1[:, 1:2], in_=xn_t[:], axis=AX.X, apply_absolute_value=True)
                e.tensor_scalar(out=tv1[:, 2:3], in0=tv1[:, 1:2], scalar1=1e-5, scalar2=None, op0=OP.max)
                e.reciprocal(out=tv1[:, 3:4], in_=tv1[:, 2:3])
                e.tensor_scalar(out=ln_srow[:, t:t + 1], in0=tv1[:, 3:4], scalar1=127.0,
                                scalar2=None, op0=OP.mult)
                ins = e.tensor_scalar(out=tmp128[:], in0=ones_f[:],
                                      scalar1=tv1[:, 2:3], scalar2=1.0 / 127.0, op0=OP.mult, op1=OP.mult)
            mark("a1_xn_%d" % t, sV)

            @blk.tensor
            def _(e, t=t):
                wv(e, "a1_xn_%d" % t)
                if t == 0:
                    wv(e, "init_gp")
                if t > 0:
                    wv(e, "a1_isxrow_%d" % (t - 1))
                ins = e.transpose(psA[:, 3584:3584 + 128], tmp128[:], identf[:])
                inc(ins, sM, 1)
            mark("a1_isxT_%d" % t, sM)

            @blk.vector
            def _(e, t=t):
                e = ES(e, sV)
                wv(e, "a1_isxT_%d" % t)
                ins = e.tensor_copy(out=isx_row[0:1, t * 128:(t + 1) * 128],
                                    in_=psA[0:1, 3584:3584 + 128])
            mark("a1_isxrow_%d" % t, sV)

            @blk.scalar
            def _(e, t=t):
                e = ES(e, sS)
                wv(e, "a1_xn_%d" % t)
                if t > 0:
                    wv(e, "a1_done_%d" % (t - 1))
                ins = e.copy(out=xnb_t[:], in_=xn_t[:])
            mark("a1_cast_%d" % t, sS)

            @blk.scalar
            def _(e, t=t):
                e = ES(e, sS)
                wv(e, "a1_cast_%d" % t)
                for kc in range(KC):
                    ins = e.dma_start(out=xnT[:, kc * TP + t * 128: kc * TP + (t + 1) * 128],
                                      in_=xnb_t[:, kc * 128:(kc + 1) * 128], transpose=True)
                    inc(ins, sTt, 16)
            mark("a1_done_%d" % t, sTt)
        mark("a1_all", sTt)

        @blk.tensor
        def _(e):
            wv(e, "a1_isxrow_%d" % (NT - 1))
            e.matmul(psA[:, 3072:3584], ones1[:], isx_row[0:1, 8:520], start=True, stop=True)
            ins = e.matmul(psA[:, 3584:4096], ones1[:], isx_row[0:1, 520:1032], start=True, stop=True)
            inc(ins, sM, 1)
        mark("isx_mm", sM)

        @blk.vector
        def _(e):
            e = ES(e, sV)
            wv(e, "isx_mm")
            e.tensor_copy(out=ISX[:, 0:512], in_=psA[:, 3072:3584])
            ins = e.tensor_copy(out=ISX[:, 512:1024], in_=psA[:, 3584:4096])
        mark("isx_done", sV)

        f0_emitted = 0
        f0_s_emitted = 0

        def emit_wq(pr):
            wbuf = (wq_bf0, wq_bf1)[pr % 2]

            @blk.gpsimd
            def _(e, pr=pr, wbuf=wbuf):
                e = ES(e, sP)
                if pr >= 2:
                    wv(e, "qkv_mm_%d" % (pr - 2))
                ins = e.dma_start(out=wbuf[:], in_=wqkv[pr])
                inc(ins, sGw[pr % 2], 16)
            mark("wq_%d" % pr, sGw[pr % 2])

        emit_wq(0)
        def emit_qkv_pair(pr):
            wbuf = (wq_bf0, wq_bf1)[pr % 2]
            if pr + 1 < NPAIR:
                emit_wq(pr + 1)

            @blk.gpsimd
            def _(e, pr=pr):
                e = ES(e, sP)
                for k in range(12):
                    qi = pr * 12 + k
                    if qi < 192:
                        f0_dma(e, qi)

            for mi in range(3):
                for ti, (n0, nn) in enumerate(QKV_NT if mi < 2 else QKV_NT_V):
                    u = mi * 3 + ti

                    @blk.tensor
                    def _(e, pr=pr, mi=mi, ti=ti, n0=n0, nn=nn, u=u, wbuf=wbuf):
                        if u == 0:
                            wv(e, "wq_%d" % pr)
                            if pr == 0:
                                wv(e, "a1_all")
                                wv(e, "isx_done")
                        gu = pr * 9 + u
                        if gu >= 2:
                            wv(e, "qkv_ev_%d" % (gu - 2))
                        for kc in range(KC):
                            ins = e.matmul(bank(gu % 2, nn),
                                           wbuf[:, kc * 384 + mi * 128: kc * 384 + (mi + 1) * 128],
                                           xnT[:, kc * TP + n0: kc * TP + n0 + nn],
                                           start=(kc == 0), stop=(kc == KC - 1))
                        inc(ins, sM, 1)
                    mark("qkv_ps_%d_%d" % (pr, u), sM)

                    @blk.vector
                    def _(e, pr=pr, mi=mi, n0=n0, nn=nn, u=u):
                        e = ES(e, sV)
                        gu = pr * 9 + u
                        wv(e, "qkv_ps_%d_%d" % (pr, u))
                        if mi == 0:
                            s0 = max(n0, 8) - n0
                            s1 = min(n0 + nn, 8 + T) - n0
                            d0 = n0 + s0 - 8
                            ins = e.tensor_scalar(out=qm0[0:64, d0:d0 + (s1 - s0)],
                                                  in0=bank(gu % 2, nn)[0:64, s0:s1],
                                                  scalar1=0.125, scalar2=None, op0=OP.mult)
                            ins = e.tensor_scalar(out=qm1[64:128, d0:d0 + (s1 - s0)],
                                                  in0=bank(gu % 2, nn)[64:128, s0:s1],
                                                  scalar1=0.125, scalar2=None, op0=OP.mult)
                        else:
                            if mi == 2 and ti == 0 and pr > 0:
                                wv(e, "vtok_%d" % (pr - 1))
                            dst = (None, kT_p, vT_p)[mi]
                            ins = e.tensor_copy(out=dst[:, n0:n0 + nn], in_=bank(gu % 2, nn))
                    mark("qkv_ev_%d" % (pr * 9 + u), sV)
            mark("qkv_mm_%d" % pr, sM)
            mark("qkv_ev_done_%d" % pr, sV)

            @blk.scalar
            def _(e, pr=pr):
                e = ES(e, sS)
                wv(e, "qkv_ev_done_%d" % pr)
                if pr >= 2:
                    wv(e, "a3_pv_%d_1" % (pr - 2))
                for tt in range(NT):
                    ins = e.dma_start(out=v_tok[pr % 2][:, tt * 128: tt * 128 + 128],
                                      in_=vT_p[:, tt * 128: tt * 128 + 128], transpose=True)
                    inc(ins, sTv, 16)
            mark("vtok_%d" % pr, sTv)


        def emit_scores_half(pr, half):
            if True:
                hh = pr * 2 + half

                @blk.tensor
                def _(e, pr=pr, half=half, hh=hh):
                    if hh > 0:
                        wv(e, "a3_Tsb_%d" % (hh - 1))
                    for jj in range(4):
                        j = half * 4 + jj
                        for h in range(2):
                            slot = jj * 2 + h
                            off = 1024 + (slot // 3) * 512 + (slot % 3) * 144
                            qm = (qm0, qm1)[h]
                            ins = e.matmul(psA[:, off:off + 144],
                                           qm[:, j * 128: (j + 1) * 128],
                                           kT_p[:, j * 128: j * 128 + 144],
                                           start=True, stop=True)
                    off8 = 1024 + 2 * 512 + 2 * 144
                    ins = e.matmul(psA[:, off8:off8 + 144],
                                   qm0[:, half * 512: half * 512 + 128],
                                   kT_p[:, half * 512: half * 512 + 144],
                                   start=True, stop=True)
                    inc(ins, sM, 1)
                mark("a3_sc_%d" % hh, sM)

                @blk.vector
                def _(e, pr=pr, half=half, hh=hh):
                    e = ES(e, sV)
                    if hh == 0:
                        wv(e, "init_dma")
                    else:
                        wv(e, "a3_exp_%d" % (hh - 1))
                    wv(e, "a3_sc_%d" % hh)
                    psc = psA[:, 1024:1024 + 1536].rearrange("p (b s) -> p b s", b=3)[:, :, 0:432]
                    ins = e.tensor_tensor(out=T_sb[:].rearrange("p (b c) -> p b c", b=3), in0=psc,
                                          in1=mmul[:].rearrange("p (b c) -> p b c", b=3), op=OP.mult)
                    mark("a3_Tsb_%d" % hh, sV)
                    e.tensor_tensor(out=T_sb[:], in0=T_sb[:],
                                    in1=madd[:, half * TSBW:(half + 1) * TSBW], op=OP.add)
                    ins = e.reduce_max(out=negmax[:], in_=T_sb[:], axis=AX.X, negate=True)
                mark("a3_mx_%d" % hh, sV)

                @blk.scalar
                def _(e, hh=hh):
                    e = ES(e, sS)
                    wv(e, "a3_mx_%d" % hh)
                    ins = e.activation(out=E_bf[:], in_=T_sb[:], func=AF.Exp, bias=negmax[:, 0:1])
                mark("a3_exp_%d" % hh, sS)

                @blk.vector
                def _(e, hh=hh):
                    e = ES(e, sV)
                    wv(e, "a3_exp_%d" % hh)
                    if hh > 0:
                        wv(e, "a3_pt_%d" % (hh - 1))
                    e.reduce_sum(out=rsum[:], in_=E_bf[:].rearrange("p (s c) -> p s c", s=9), axis=AX.X)
                    e.reciprocal(out=rinv[:], in_=rsum[:])
                    for slot in range(9):
                        ins = e.tensor_scalar(out=P_bf[:, slot * 144:(slot + 1) * 144],
                                              in0=E_bf[:, slot * 144:(slot + 1) * 144],
                                              scalar1=rinv[:, slot:slot + 1], scalar2=None, op0=OP.mult)
                mark("a3_P_%d" % hh, sV)

                @blk.sync
                def _(e, hh=hh):
                    wv(e, "a3_P_%d" % hh)
                    if hh > 0:
                        wv(e, "a3_pv_%d_%d" % ((hh - 1) // 2, (hh - 1) % 2))
                    for slot in range(8):
                        ins = e.dma_start(out=PT_a[:, slot * 128:(slot + 1) * 128],
                                          in_=P_bf[:, slot * 144: slot * 144 + 128], transpose=True)
                        inc(ins, sTp, 16)
                        ins = e.dma_start(out=PT_b[:, slot * 128:(slot + 1) * 128],
                                          in_=P_bf[:, slot * 144 + 128: slot * 144 + 256], transpose=True)
                        inc(ins, sTp, 16)
                mark("a3_pt_%d" % hh, sTp)


        def emit_pv_half(pr, half):
            if True:
                hh = pr * 2 + half
                @blk.tensor
                def _(e, pr=pr, half=half, hh=hh):
                    wv(e, "a3_pt_%d" % hh)
                    wv(e, "vtok_%d" % pr)
                    if hh > 0:
                        wv(e, "a3_oev_%d" % (hh - 1))
                    for jj in range(4):
                        j = half * 4 + jj
                        for h in range(2):
                            slot = jj * 2 + h
                            out = psA[0:64, 2560 + slot * 128: 2560 + (slot + 1) * 128]
                            vt = v_tok[pr % 2]
                            e.matmul(out, vt[:, j * 128 + 64 * h: j * 128 + 64 * h + 64],
                                     PT_a[:, slot * 128:(slot + 1) * 128], start=True, stop=False)
                            ins = e.matmul(out,
                                           vt[0:16, (j + 1) * 128 + 64 * h: (j + 1) * 128 + 64 * h + 64],
                                           PT_b[0:16, slot * 128:(slot + 1) * 128], start=False, stop=True)
                    inc(ins, sM, 1)
                mark("a3_pv_%d_%d" % (pr, half), sM)

                @blk.vector
                def _(e, pr=pr, half=half):
                    e = ES(e, sV)
                    wv(e, "a3_pv_%d_%d" % (pr, half))
                    if pr >= 2 and half == 0:
                        wv(e, "at_spill_%d" % (pr - 2))
                    for jj in range(4):
                        for h in range(2):
                            slot = jj * 2 + h
                            dst = (ath0, ath1)[h]
                            ins = e.tensor_copy(out=dst[:, (pr % 2) * T + half * 512 + jj * 128:
                                                        (pr % 2) * T + half * 512 + (jj + 1) * 128],
                                                in_=psA[0:64, 2560 + slot * 128: 2560 + (slot + 1) * 128])
                mark("a3_oev_%d" % (pr * 2 + half), sV)

            if half == 1:
                @blk.scalar
                def _(e, pr=pr):
                    e = ES(e, sS)
                    wv(e, "a3_oev_%d" % (pr * 2 + 1))
                    ins = e.dma_start(out=atscr[pr][0:64, :], in_=ath0[:, (pr % 2) * T:(pr % 2) * T + T])
                    inc(ins, sTat[pr % 2], 16)
                    ins = e.dma_start(out=atscr[pr][64:128, :], in_=ath1[:, (pr % 2) * T:(pr % 2) * T + T])
                    inc(ins, sTat[pr % 2], 16)
                mark("at_spill_%d" % pr, sTat[pr % 2])


        for pr in range(NPAIR):
            emit_qkv_pair(pr)
            for half in range(2):
                hh = pr * 2 + half
                if hh >= 1:
                    emit_pv_half((hh - 1) // 2, (hh - 1) % 2)
                emit_scores_half(pr, half)

            @blk.scalar
            def _(e, pr=pr):
                e = ES(e, sS)
                nonlocal f0_s_emitted
                hi = min((pr + 1) * 12, 192)
                while f0_s_emitted < hi:
                    f0_abs_s(e, f0_s_emitted)
                    f0_s_emitted += 1

            @blk.vector
            def _(e, pr=pr):
                e = ES(e, sV)
                nonlocal f0_emitted
                hi = min((pr + 1) * 12, 192)
                while f0_emitted < hi:
                    f0_red(e, f0_emitted)
                    f0_emitted += 1
        emit_pv_half(NPAIR - 1, 1)
        @blk.scalar
        def _(e):
            e = ES(e, sS)
            nonlocal f0_s_emitted
            while f0_s_emitted < 192:
                f0_abs_s(e, f0_s_emitted)
                f0_s_emitted += 1

        @blk.vector
        def _(e):
            e = ES(e, sV)
            nonlocal f0_emitted
            while f0_emitted < 192:
                f0_red(e, f0_emitted)
                f0_emitted += 1
        mark("f0_red_all", sV)

        @blk.vector
        def _(e):
            e = ES(e, sV)
            wv(e, "f0_red_all")
            e.reduce_sum(out=tv1[:, 0:1], in_=partials[:, 0:16], axis=AX.X)
            e.reduce_sum(out=tv1[:, 1:2], in_=partials[:, 16:32], axis=AX.X)
            ins = e.reduce_sum(out=tv1[:, 2:3], in_=partials[:, 32:48], axis=AX.X)
        mark("f0_sums", sV)

        @blk.tensor
        def _(e):
            wv(e, "f0_sums")
            wv(e, "a3_oev_%d" % (NPAIR * 2 - 1))
            e.matmul(psA[:, 3072:3073], ones_f[:], tv1[:, 0:1], start=True, stop=True)
            e.matmul(psA[:, 3073:3074], ones_f[:], tv1[:, 1:2], start=True, stop=True)
            ins = e.matmul(psA[:, 3074:3075], ones_f[:], tv1[:, 2:3], start=True, stop=True)
            inc(ins, sM, 1)
        mark("f0_bc", sM)

        @blk.vector
        def _(e):
            e = ES(e, sV)
            wv(e, "f0_bc")
            nel = float(I * D)
            for wi, (ssb, isb) in enumerate(((swg, iswg), (swu, iswu), (swd, iswd))):
                e.tensor_scalar(out=isb[:], in0=psA[:, 3072 + wi:3073 + wi],
                                scalar1=1.0 / nel, scalar2=1e-5, op0=OP.mult, op1=OP.max)
                ins = e.reciprocal(out=ssb[:], in_=isb[:])
            ins = e.tensor_scalar(out=ISXU[:], in0=ISX[:], scalar1=iswu[:, 0:1],
                                  scalar2=None, op0=OP.mult)
        mark("f0_scales", sV)

        for g in range(4):
            @blk.gpsimd
            def _(e, g=g):
                e = ES(e, sP)
                if g == 0:
                    wv(e, "f0_red_all")
                else:
                    wv(e, "a4_mm_%d" % (g - 1))
                ins = e.dma_start(out=pw_bf[:], in_=wproj[g])
                inc(ins, sGp, 16)
            mark("a4_w_%d" % g, sGp)

            for kc in range(KC):
                @blk.sync
                def _(e, g=g, kc=kc):
                    wv(e, "at_spill_%d" % kc)
                    gk = g * KC + kc
                    if gk >= 2:
                        wv(e, "a4_mmk_%d" % (gk - 2))
                    ins = e.dma_start(out=atrb[:, (kc % 2) * T:(kc % 2) * T + T], in_=atscr[kc])
                    inc(ins, sXa[kc % 2], 16)
                mark("a4_rb_%d_%d" % (g, kc), sXa[kc % 2])

                @blk.tensor
                def _(e, g=g, kc=kc):
                    if kc == 0:
                        wv(e, "a4_w_%d" % g)
                        if g == 0:
                            wv(e, "f0_scales")
                        else:
                            wv(e, "a4_ev_%d" % (g - 1))
                    wv(e, "a4_rb_%d_%d" % (g, kc))
                    for mi in range(4):
                        for nt in range(2):
                            ins = e.matmul(bank(mi * 2 + nt),
                                           pw_bf[:, kc * 512 + mi * 128: kc * 512 + (mi + 1) * 128],
                                           atrb[:, (kc % 2) * T + nt * 512:
                                                (kc % 2) * T + (nt + 1) * 512],
                                           start=(kc == 0), stop=(kc == KC - 1))
                    inc(ins, sM, 1)
                mark("a4_mmk_%d" % (g * KC + kc), sM)
            mark("a4_mm_%d" % g, sM)

            @blk.vector
            def _(e, g=g):
                e = ES(e, sV)
                wv(e, "a4_mm_%d" % g)
                if g >= 1:
                    wv(e, "a4_spill_%d" % (g - 1))
                for mi in range(4):
                    for nt in range(2):
                        m = g * 4 + mi
                        ins = e.tensor_scalar(out=astg[:, mi * T + nt * 512: mi * T + (nt + 1) * 512],
                                              in0=bank(mi * 2 + nt), scalar1=pb_sb[:, m:m + 1],
                                              scalar2=None, op0=OP.add)
            mark("a4_ev_%d" % g, sV)

            @blk.scalar
            def _(e, g=g):
                e = ES(e, sS)
                wv(e, "a4_ev_%d" % g)
                ins = e.dma_start(out=ascr[g * 4:(g + 1) * 4].rearrange("k p c -> p k c"),
                                  in_=astg[:].rearrange("p (k c) -> p k c", k=4))
                inc(ins, sTa4, 16)
            mark("a4_spill_%d" % g, sTa4)

        for t in range(NT):
            @blk.sync
            def _(e, t=t):
                if t == 0:
                    wv(e, "a4_spill_3")
                else:
                    wv(e, "f1r_t_%d" % (t - 1))
                ins = e.dma_start(out=xtok2[:], in_=xh[t * 128:(t + 1) * 128, :])
                inc(ins, sXx, 16)
            mark("f1r_x_%d" % t, sXx)

            @blk.vector
            def _(e, t=t):
                e = ES(e, sV)
                wv(e, "f1r_x_%d" % t)
                ins = e.tensor_scalar(out=xn2[:], in0=xtok2[:], scalar1=ln_mean[:, t:t + 1],
                                      scalar2=ln_rstd[:, t:t + 1], op0=OP.subtract, op1=OP.mult)
            mark("f1r_xn_%d" % t, sV)

            @blk.scalar
            def _(e, t=t):
                e = ES(e, sS)
                wv(e, "f1r_xn_%d" % t)
                ins = e.activation(out=xn2[:], in_=xn2[:], func=AF.Identity,
                                   scale=ln_srow[:, t:t + 1], bias=c_round[:, 0:1])
            mark("f1r_y_%d" % t, sS)

            @blk.vector
            def _(e, t=t):
                e = ES(e, sV)
                wv(e, "f1r_y_%d" % t)
                ins = e.tensor_scalar(out=qx2[:], in0=xn2[:], scalar1=CR, scalar2=None, op0=OP.subtract)
            mark("f1r_qx_%d" % t, sV)

            @blk.scalar
            def _(e, t=t):
                e = ES(e, sS)
                wv(e, "f1r_qx_%d" % t)
                for kc in range(KC):
                    ins = e.dma_start(out=qxT[:, kc * TP + t * 128: kc * TP + (t + 1) * 128],
                                      in_=qx2[:, kc * 128:(kc + 1) * 128], transpose=True)
                    inc(ins, sTt, 16)
            mark("f1r_t_%d" % t, sTt)
        mark("f1r_all", sTt)

        for g in range(16):
            for which in range(2):
                lab = "%d_%d" % (g, which)
                warr = (wg_d, wu_d)[which]
                tdst = (tern_g, tern_u)[which]
                ssc = (swg, swu)[which]

                def _w_guards(e, g=g, which=which):
                    if which == 1:
                        wv(e, "f1_tern_%d_0" % g)
                    elif g > 0:
                        wv(e, "f1_tern_%d_1" % (g - 1))
                    else:
                        wv(e, "f1r_all")
                        wv(e, "at_spill_%d" % (NPAIR - 1))
                        wv(e, "a4_mm_3")

                @blk.sync
                def _(e, g=g, warr=warr, _w_guards=_w_guards):
                    _w_guards(e)
                    ins = e.dma_start(out=wstage[:, 0:8 * 512].rearrange("p (k c) -> p k c", k=8),
                                      in_=warr[g][0:8].rearrange("k p c -> p k c"))
                    inc(ins, sXw, 16)
                mark("f1_wdma_%s" % lab, sXw)

                @blk.gpsimd
                def _(e, g=g, warr=warr, _w_guards=_w_guards):
                    _w_guards(e)
                    ins = e.dma_start(out=wstage[:, 8 * 512:16 * 512].rearrange("p (k c) -> p k c", k=8),
                                      in_=warr[g][8:16].rearrange("k p c -> p k c"))
                    inc(ins, sXw2, 16)
                mark("f1_wdma2_%s" % lab, sXw2)

                @blk.scalar
                def _(e, g=g, which=which, lab=lab, ssc=ssc):
                    e = ES(e, sS)
                    wv(e, "f1_wdma_%s" % lab)
                    wv(e, "f1_wdma2_%s" % lab)
                    if g == 0 and which == 0:
                        wv(e, "f0_scales")
                    ins = e.activation(out=wstage[:], in_=wstage[:], func=AF.Identity,
                                       scale=ssc[:, 0:1], bias=c_round[:, 0:1])
                mark("f1_tern_s_%s" % lab, sS)

                @blk.vector
                def _(e, g=g, which=which, lab=lab, tdst=tdst):
                    e = ES(e, sV)
                    wv(e, "f1_tern_s_%s" % lab)
                    if g > 0:
                        wv(e, "f1_mm_%d_%d" % (g - 1, which))
                    e.tensor_scalar(out=wstage[:], in0=wstage[:], scalar1=CR, scalar2=1.0,
                                    op0=OP.subtract, op1=OP.min)
                    ins = e.tensor_scalar(out=tdst[:], in0=wstage[:], scalar1=-1.0,
                                          scalar2=None, op0=OP.max)
                mark("f1_tern_%s" % lab, sV)

            for mi in range(4):
                c = g * 4 + mi
                for nt in range(2):
                    u = mi * 2 + nt
                    gu = g * 8 + u
                    lab = "%d_%d" % (c, nt)

                    @blk.tensor
                    def _(e, g=g, mi=mi, nt=nt, u=u, gu=gu):
                        if u == 0:
                            wv(e, "f1_tern_%d_0" % g)
                            if g == 0:
                                wv(e, "f1r_all")
                                wv(e, "f1s_prev_none" if False else "f0_scales")
                        if gu >= 2:
                            wv(e, "f1_gg_ev_%d" % (gu - 2))
                        for kc in range(KC):
                            ins = e.matmul(bank(u % 2),
                                           tern_g[:, kc * 512 + mi * 128: kc * 512 + (mi + 1) * 128],
                                           qxT[:, kc * TP + 8 + nt * 512: kc * TP + 8 + (nt + 1) * 512],
                                           start=(kc == 0), stop=(kc == KC - 1))
                        inc(ins, sM, 1)
                    mark("f1_gp_%d" % gu, sM)

                    @blk.tensor
                    def _(e, g=g, mi=mi, nt=nt, u=u, gu=gu):
                        if u == 0:
                            wv(e, "f1_tern_%d_1" % g)
                        if gu >= 2:
                            wv(e, "f1_h_ev_%d" % (gu - 2))
                        for kc in range(KC):
                            ins = e.matmul(bank(2 + u % 2),
                                           tern_u[:, kc * 512 + mi * 128: kc * 512 + (mi + 1) * 128],
                                           qxT[:, kc * TP + 8 + nt * 512: kc * TP + 8 + (nt + 1) * 512],
                                           start=(kc == 0), stop=(kc == KC - 1))
                        inc(ins, sM, 1)
                    mark("f1_up_%d" % gu, sM)

                    @blk.vector
                    def _(e, u=u, nt=nt, gu=gu):
                        e = ES(e, sV)
                        wv(e, "f1_gp_%d" % gu)
                        ins = e.scalar_tensor_tensor(out=sg_t[:], in0=bank(u % 2), scalar=iswg[:, 0:1],
                                                     in1=ISX[:, nt * 512:(nt + 1) * 512],
                                                     op0=OP.mult, op1=OP.mult)
                    mark("f1_gg_%s" % lab, sV)
                    mark("f1_gg_ev_%d" % gu, sV)

                    @blk.scalar
                    def _(e, lab=lab):
                        e = ES(e, sS)
                        wv(e, "f1_gg_%s" % lab)
                        ins = e.activation(out=sg_t[:], in_=sg_t[:], func=AF.Silu)
                    mark("f1_sil_%s" % lab, sS)

                    @blk.vector
                    def _(e, nt=nt, lab=lab):
                        e = ES(e, sV)
                        wv(e, "f1_sil_%s" % lab)
                        ins = e.tensor_tensor(out=sgp_t[:], in0=sg_t[:],
                                              in1=ISXU[:, nt * 512:(nt + 1) * 512], op=OP.mult)
                    mark("f1_sgp_%s" % lab, sV)

                    @blk.vector
                    def _(e, g=g, mi=mi, nt=nt, u=u, gu=gu, lab=lab):
                        e = ES(e, sV)
                        wv(e, "f1_sgp_%s" % lab)
                        wv(e, "f1_up_%d" % gu)
                        if g > 0 and u <= 1:
                            wv(e, "f1_spill_%d" % (g - 1))
                        ins = e.tensor_tensor(out=hstage[:, mi * T + nt * 512: mi * T + (nt + 1) * 512],
                                              in0=bank(2 + u % 2), in1=sgp_t[:], op=OP.mult)
                    mark("f1_h_%s" % lab, sV)
                    mark("f1_h_ev_%d" % gu, sV)

                    @blk.scalar
                    def _(e, g=g, mi=mi, nt=nt, gu=gu, lab=lab):
                        e = ES(e, sS)
                        wv(e, "f1_h_%s" % lab)
                        if gu >= 1:
                            wv(e, "f1_ssq_%d_%d" % ((gu - 1) // 2, (gu - 1) % 2))
                        ins = e.activation(out=hsq_t[:],
                                           in_=hstage[:, mi * T + nt * 512: mi * T + (nt + 1) * 512],
                                           func=AF.Square)
                    mark("f1_hsq_%s" % lab, sS)

                    @blk.tensor
                    def _(e, c=c, nt=nt, lab=lab):
                        wv(e, "f1_hsq_%s" % lab)
                        ins = e.matmul(psA[:, 3072 + nt * 512: 3072 + (nt + 1) * 512],
                                       ones_bf[:], hsq_t[:],
                                       start=(c == 0), stop=(c == IC - 1))
                        inc(ins, sM, 1)
                    mark("f1_ssq_%s" % lab, sM)

                    @blk.vector
                    def _(e, c=c, mi=mi, nt=nt, gu=gu, lab=lab):
                        e = ES(e, sV)
                        wv(e, "f1_h_%s" % lab)
                        hs = hstage[:, mi * T + nt * 512: mi * T + (nt + 1) * 512]
                        e.scalar_tensor_tensor(out=Mhi[:, nt * 512:(nt + 1) * 512], in0=hs,
                                               scalar=lnw_sb[:, c:c + 1],
                                               in1=Mhi[:, nt * 512:(nt + 1) * 512],
                                               op0=OP.mult, op1=OP.max)
                        ins = e.scalar_tensor_tensor(out=Mlo[:, nt * 512:(nt + 1) * 512], in0=hs,
                                                     scalar=lnw_sb[:, c:c + 1],
                                                     in1=Mlo[:, nt * 512:(nt + 1) * 512],
                                                     op0=OP.mult, op1=OP.min)
                    mark("f1_M_%s" % lab, sV)
            mark("f1_mm_%d_0" % g, sM)
            mark("f1_mm_%d_1" % g, sM)

            @blk.scalar
            def _(e, g=g):
                e = ES(e, sS)
                for mi in range(4):
                    for nt in range(2):
                        wv(e, "f1_hsq_%d_%d" % (g * 4 + mi, nt))
                        wv(e, "f1_M_%d_%d" % (g * 4 + mi, nt))
                ins = e.dma_start(out=hscr[g * 4:(g + 1) * 4].rearrange("k p c -> p k c"),
                                  in_=hstage[:].rearrange("p (k c) -> p k c", k=4))
                inc(ins, sTh, 16)
            mark("f1_spill_%d" % g, sTh)

        @blk.vector
        def _(e):
            e = ES(e, sV)
            wv(e, "f1_ssq_63_1")
            for mi in range(4):
                wv(e, "f1_M_%d_1" % (60 + mi))
            e.tensor_scalar(out=r_bc[:, 0:512], in0=psA[:, 3072:3584], scalar1=1.0 / I,
                            scalar2=1e-6, op0=OP.mult, op1=OP.add)
            ins = e.tensor_scalar(out=r_bc[:, 512:1024], in0=psA[:, 3584:4096], scalar1=1.0 / I,
                                  scalar2=1e-6, op0=OP.mult, op1=OP.add)
            e.tensor_scalar(out=Mlo[:], in0=Mlo[:], scalar1=-1.0, scalar2=None, op0=OP.mult)
            ins = e.tensor_tensor(out=Mhi[:], in0=Mhi[:], in1=Mlo[:], op=OP.max)
        mark("f1s_pre", sV)

        @blk.scalar
        def _(e):
            e = ES(e, sS)
            wv(e, "f1s_pre")
            e.activation(out=r_bc[:], in_=r_bc[:], func=AF.Ln)
            ins = e.activation(out=r_bc[:], in_=r_bc[:], func=AF.Exp, scale=c_nhalf[:, 0:1])
        mark("f1s_rbc", sS)

        @blk.tensor
        def _(e):
            wv(e, "f1s_pre")
            for b in range(8):
                if b >= 2:
                    e.wait_ge(sVm.h, b - 1)
                ins = e.transpose(psA[:, 2048 + (b % 2) * 512: 2048 + (b % 2) * 512 + 128],
                                  Mhi[:, b * 128:(b + 1) * 128], identf[:])
                inc(ins, sM, 1)
                mark("f1s_mt_%d" % b, sM)

        @blk.vector
        def _(e):
            e = ES(e, sV)
            for b in range(8):
                wv(e, "f1s_mt_%d" % b)
                ins = e.reduce_max(out=amax_tok[:, b:b + 1],
                                   in_=psA[:, 2048 + (b % 2) * 512: 2048 + (b % 2) * 512 + 128],
                                   axis=AX.X)
                ins2 = e.engine_nop()
                inc(ins2, sVm, 1)
        mark("f1s_am", sV)

        @blk.tensor
        def _(e):
            wv(e, "f1s_rbc")
            wv(e, "f1s_am")
            for b in range(8):
                ins = e.transpose(psA[:, 2048 + b * 128: 2048 + b * 128 + 128],
                                  r_bc[:, b * 128:(b + 1) * 128], identf[:])
                inc(ins, sM, 1)
                mark("f1s_rt_%d" % b, sM)

        @blk.vector
        def _(e):
            e = ES(e, sV)
            wv(e, "f1s_rt_%d" % 7)
            for b in range(8):
                ins = e.tensor_copy(out=rtok[:, b:b + 1],
                                    in_=psA[:, 2048 + b * 128: 2048 + b * 128 + 1])
            mark("f1s_rtok", sV)
            e.tensor_tensor(out=mrms[:], in0=rtok[:], in1=amax_tok[:], op=OP.mult)
            e.tensor_scalar(out=mrms[:], in0=mrms[:], scalar1=1e-5, scalar2=None, op0=OP.max)
            e.reciprocal(out=shtok[:], in_=mrms[:])
            e.tensor_scalar(out=shtok[:], in0=shtok[:], scalar1=127.0, scalar2=None, op0=OP.mult)
            e.tensor_scalar(out=ivtok[:], in0=mrms[:], scalar1=1.0 / 127.0,
                            scalar2=None, op0=OP.mult)
            ins = e.tensor_scalar(out=ivtok[:], in0=ivtok[:], scalar1=iswd[:, 0:1],
                                  scalar2=None, op0=OP.mult)
        mark("f1s_tok", sV)

        for b in range(8):
            for which, (colsrc, rowdst) in enumerate(((shtok, sh_row), (ivtok, iv_row))):
                lab = "%d_%d" % (b, which)

                @blk.vector
                def _(e, b=b, which=which, colsrc=colsrc, lab=lab):
                    e = ES(e, sV)
                    wv(e, "f1s_tok")
                    if b * 2 + which >= 2:
                        wv(e, "f1s_row_%d_%d" % ((b * 2 + which - 2) // 2, (b * 2 + which - 2) % 2))
                    ins = e.tensor_scalar(out=tmp128[:], in0=ones_f[:], scalar1=colsrc[:, b:b + 1],
                                          scalar2=None, op0=OP.mult)
                mark("f1s_bcc_%s" % lab, sV)

                @blk.tensor
                def _(e, b=b, which=which, lab=lab):
                    wv(e, "f1s_bcc_%s" % lab)
                    ins = e.transpose(psA[:, 2048 + ((b * 2 + which) % 2) * 512:
                                          2048 + ((b * 2 + which) % 2) * 512 + 128],
                                      tmp128[:], identf[:])
                    inc(ins, sM, 1)
                mark("f1s_ct_%s" % lab, sM)

                @blk.vector
                def _(e, b=b, which=which, rowdst=rowdst, lab=lab):
                    e = ES(e, sV)
                    wv(e, "f1s_ct_%s" % lab)
                    ins = e.tensor_copy(out=rowdst[0:1, b * 128:(b + 1) * 128],
                                        in_=psA[0:1, 2048 + ((b * 2 + which) % 2) * 512:
                                                2048 + ((b * 2 + which) % 2) * 512 + 128])
                mark("f1s_row_%s" % lab, sV)
        mark("f1s_rows", sV)

        @blk.tensor
        def _(e):
            wv(e, "f1s_rows")
            e.matmul(psA[:, 0:512], ones1[:], sh_row[0:1, 0:512], start=True, stop=True)
            e.matmul(psA[:, 512:1024], ones1[:], sh_row[0:1, 512:1024], start=True, stop=True)
            e.matmul(psA[:, 1024:1536], ones1[:], iv_row[0:1, 0:512], start=True, stop=True)
            ins = e.matmul(psA[:, 1536:2048], ones1[:], iv_row[0:1, 512:1024], start=True, stop=True)
            inc(ins, sM, 1)
        mark("f1s_bc", sM)

        @blk.vector
        def _(e):
            e = ES(e, sV)
            wv(e, "f1s_bc")
            e.tensor_tensor(out=RS[:, 0:512], in0=r_bc[:, 0:512], in1=psA[:, 0:512], op=OP.mult)
            e.tensor_tensor(out=RS[:, 512:1024], in0=r_bc[:, 512:1024], in1=psA[:, 512:1024], op=OP.mult)
            e.tensor_copy(out=INVD[:, 0:512], in_=psA[:, 1024:1536])
            ins = e.tensor_copy(out=INVD[:, 512:1024], in_=psA[:, 1536:2048])
        mark("f1s_done", sV)

        for half in range(2):
            c0 = half * 512
            for c in range(IC):
                @blk.scalar
                def _(e, half=half, c=c, c0=c0):
                    e = ES(e, sS)
                    if half == 0 and c < 2:
                        wv(e, "f1_spill_15")
                        wv(e, "f1s_done")
                    else:
                        lab = ("f2_q_%d_%d" % (half, c - 2)) if c >= 2 else \
                              ("f2_q_%d_%d" % (half - 1, IC - 2 + c))
                        wv(e, lab)
                        if half == 1 and c < 2:
                            wv(e, "f3_mm_0_%d" % (KC - 1))
                    ins = e.dma_start(out=hrb[:, (c % 2) * 512:(c % 2) * 512 + 512],
                                      in_=hscr[c][:, c0:c0 + 512])
                    inc(ins, sTr[c % 2], 16)
                mark("f2_rb_%d_%d" % (half, c), sTr[c % 2])

                @blk.vector
                def _(e, half=half, c=c, c0=c0):
                    e = ES(e, sV)
                    wv(e, "f2_rb_%d_%d" % (half, c))
                    buf = (hl_t, sgp_t)[c % 2]
                    ins = e.tensor_scalar(out=buf[:], in0=hrb[:, (c % 2) * 512:(c % 2) * 512 + 512],
                                          scalar1=lnw_sb[:, c:c + 1], scalar2=None, op0=OP.mult)
                    ins = e.tensor_tensor(out=buf[:], in0=buf[:], in1=RS[:, c0:c0 + 512], op=OP.mult)
                    ins = e.tensor_scalar(out=hq[:, c * 512:(c + 1) * 512], in0=buf[:],
                                          scalar1=CR, scalar2=CR, op0=OP.add, op1=OP.subtract)
                mark("f2_q_%d_%d" % (half, c), sV)
            mark("f2_done_%d" % half, sV)

            for b in range(4):
                @blk.sync
                def _(e, half=half, b=b):
                    if half > 0:
                        wv(e, "out_all_%d" % (half - 1))
                    else:
                        wv(e, "f1_tern_15_1")
                    tb = half * 4 + b
                    ins = e.dma_start(out=(ot0, ot1, ot2, ot3)[b][:],
                                      in_=xh[8 + tb * 128: 8 + (tb + 1) * 128, :])
                    inc(ins, sXx, 16)
                mark("f3_x_%d_%d" % (half, b), sXx)

            for m in range(KC):
                lab = "%d_%d" % (half, m)
                hm = half * KC + m

                def _wd_guards(e, hm=hm):
                    if hm > 0:
                        wv(e, "f3_tern_%d_%d" % ((hm - 1) // KC, (hm - 1) % KC))
                    else:
                        wv(e, "f2_done_0")

                @blk.sync
                def _(e, m=m, _wd_guards=_wd_guards):
                    _wd_guards(e)
                    ins = e.dma_start(out=wdstage[:, 0:32 * 128].rearrange("p (k c) -> p k c", k=32),
                                      in_=wd_d[m][0:32].rearrange("k p c -> p k c"))
                    inc(ins, sXw, 16)
                mark("f3_wdma_%s" % lab, sXw)

                @blk.gpsimd
                def _(e, m=m, _wd_guards=_wd_guards):
                    _wd_guards(e)
                    ins = e.dma_start(out=wdstage[:, 32 * 128:64 * 128].rearrange("p (k c) -> p k c", k=32),
                                      in_=wd_d[m][32:64].rearrange("k p c -> p k c"))
                    inc(ins, sXw2, 16)
                mark("f3_wdma2_%s" % lab, sXw2)

                @blk.scalar
                def _(e, lab=lab, hm=hm):
                    e = ES(e, sS)
                    wv(e, "f3_wdma_%s" % lab)
                    wv(e, "f3_wdma2_%s" % lab)
                    ins = e.activation(out=wdstage[:], in_=wdstage[:], func=AF.Identity,
                                       scale=swd[:, 0:1], bias=c_round[:, 0:1])
                mark("f3_tern_s_%s" % lab, sS)

                @blk.vector
                def _(e, lab=lab, hm=hm):
                    e = ES(e, sV)
                    wv(e, "f3_tern_s_%s" % lab)
                    if hm > 0:
                        wv(e, "f3_mm_%d_%d" % ((hm - 1) // KC, (hm - 1) % KC))
                    e.tensor_scalar(out=wdstage[:], in0=wdstage[:], scalar1=CR, scalar2=1.0,
                                    op0=OP.subtract, op1=OP.min)
                    ins = e.tensor_scalar(out=tern_d[:], in0=wdstage[:], scalar1=-1.0,
                                          scalar2=None, op0=OP.max)
                mark("f3_tern_%s" % lab, sV)

                @blk.scalar
                def _(e, half=half, m=m, hm=hm, c0=c0):
                    e = ES(e, sS)
                    if hm == 0:
                        wv(e, "a4_spill_3")
                        wv(e, "f1s_done")
                    if hm >= 2:
                        wv(e, "f3_fin_%d_%d" % ((hm - 2) // KC, (hm - 2) % KC))
                    ins = e.dma_start(out=aprb[:, (m % 2) * 512:(m % 2) * 512 + 512],
                                      in_=ascr[m][:, c0:c0 + 512])
                    inc(ins, sTap[m % 2], 16)
                mark("f3_ap_%s" % lab, sTap[m % 2])

                @blk.tensor
                def _(e, half=half, m=m, lab=lab, hm=hm):
                    wv(e, "f3_tern_%s" % lab)
                    wv(e, "f2_done_%d" % half)
                    if hm >= 2:
                        wv(e, "f3_ev_%d_%d" % ((hm - 2) // KC, (hm - 2) % KC))
                    for kc in range(IC):
                        ins = e.matmul(bank(m % 2), tern_d[:, kc * 128:(kc + 1) * 128],
                                       hq[:, kc * 512:(kc + 1) * 512],
                                       start=(kc == 0), stop=(kc == IC - 1))
                    inc(ins, sM, 1)
                mark("f3_mm_%s" % lab, sM)

                @blk.vector
                def _(e, half=half, m=m, lab=lab, c0=c0):
                    e = ES(e, sV)
                    wv(e, "f3_mm_%s" % lab)
                    wv(e, "f3_ap_%s" % lab)
                    e.tensor_tensor(out=fin_t[:], in0=bank(m % 2), in1=INVD[:, c0:c0 + 512], op=OP.mult)
                    ins = e.tensor_tensor(out=fin_t[:], in0=fin_t[:],
                                          in1=aprb[:, (m % 2) * 512:(m % 2) * 512 + 512], op=OP.add)
                mark("f3_fin_%s" % lab, sV)
                mark("f3_ev_%s" % lab, sV)

                @blk.tensor
                def _(e, half=half, m=m, lab=lab, hm=hm):
                    wv(e, "f3_fin_%s" % lab)
                    for b in range(4):
                        gidx = hm * 4 + b
                        if gidx >= 2:
                            e.wait_ge(sVt.h, gidx - 1)
                        ins = e.transpose(psA[:, 2048 + (gidx % 2) * 512: 2048 + (gidx % 2) * 512 + 128],
                                          fin_t[:, b * 128:(b + 1) * 128], identf[:])
                        inc(ins, sM, 1)
                        mark("f3_tt_%d_%d_%d" % (half, m, b), sM)

                @blk.vector
                def _(e, half=half, m=m, hm=hm):
                    e = ES(e, sV)
                    for b in range(4):
                        gidx = hm * 4 + b
                        wv(e, "f3_tt_%d_%d_%d" % (half, m, b))
                        wv(e, "f3_x_%d_%d" % (half, 3))
                        ot = (ot0, ot1, ot2, ot3)[b]
                        ins = e.tensor_tensor(out=ot[:, m * 128:(m + 1) * 128],
                                              in0=psA[:, 2048 + (gidx % 2) * 512:
                                                      2048 + (gidx % 2) * 512 + 128],
                                              in1=ot[:, m * 128:(m + 1) * 128], op=OP.add)
                        ins2 = e.engine_nop()
                        inc(ins2, sVt, 1)
                        mark("f3_ta_%d_%d_%d" % (half, m, b), sV)

            for b in range(4):
                @blk.scalar
                def _(e, half=half, b=b):
                    e = ES(e, sS)
                    for m in range(KC):
                        wv(e, "f3_ta_%d_%d_%d" % (half, m, b))
                    tb = half * 4 + b
                    ins = e.dma_start(out=y[tb * 128:(tb + 1) * 128, :],
                                      in_=(ot0, ot1, ot2, ot3)[b][:])
                    inc(ins, sO, 16)
                mark("out_%d_%d" % (half, b), sO)
            mark("out_all_%d" % half, sO)

        @blk.scalar
        def _(e):
            e = ES(e, sS)
            e.wait_ge(sO.h, sO.n)

    return nc


_NC_CACHE = None


def _get_nc():
    global _NC_CACHE
    if _NC_CACHE is None:
        _NC_CACHE = build_nc()
    return _NC_CACHE


def _host_prep(inputs):
    x = np.ascontiguousarray(np.asarray(inputs["L0_attn_x"], dtype=np.float32))
    qkv_w = np.asarray(inputs["L0_attn_qkv_w"], dtype=np.float32)
    proj_w = np.asarray(inputs["L0_attn_proj_w"], dtype=np.float32)
    proj_b = np.asarray(inputs["L0_attn_proj_b"], dtype=np.float32)
    w_gate = np.asarray(inputs["L0_ffn_w_gate"], dtype=np.float32)
    w_up = np.asarray(inputs["L0_ffn_w_up"], dtype=np.float32)
    w_down = np.asarray(inputs["L0_ffn_w_down"], dtype=np.float32)
    lnw = np.asarray(inputs["L0_ffn_ln_weight"], dtype=np.float32)

    wqkv = np.ascontiguousarray(
        qkv_w.reshape(3, 16, 128, D).transpose(1, 3, 0, 2).reshape(NPAIR, KC, 128, 384)
        .transpose(0, 2, 1, 3).reshape(NPAIR, 128, KC * 384))
    wproj = np.ascontiguousarray(
        proj_w.T.reshape(KC, 128, 4, 512).transpose(2, 0, 1, 3)
        .transpose(0, 2, 1, 3).reshape(4, 128, KC * 512))
    wg_a = np.ascontiguousarray(w_gate.T.reshape(KC, 128, 16, 512).transpose(2, 0, 1, 3))
    wu_a = np.ascontiguousarray(w_up.T.reshape(KC, 128, 16, 512).transpose(2, 0, 1, 3))
    wd_a = np.ascontiguousarray(w_down.T.reshape(IC, 128, 16, 128).transpose(2, 0, 1, 3))
    pbh = np.ascontiguousarray(proj_b.reshape(16, 128).T)
    lnwh = np.ascontiguousarray(lnw.reshape(64, 128).T)

    pp = np.arange(128)[:, None]
    ff = np.arange(SLOTW)[None, :]
    rel = ff - pp - 8
    band = (rel >= -8) & (rel < 8)
    mband = np.where(band, (8.0 - np.abs(rel) / 2.0) / 8.0, 0.0).astype(np.float32)
    mmulh = np.zeros((128, TSBW), np.float32)
    for s in range(8):
        mmulh[:, s * SLOTW:(s + 1) * SLOTW] = mband
    mmulh = mmulh.astype(ml_dtypes.bfloat16)

    in_maps = []
    for cidx in range(8):
        b, q0 = cidx // 4, (cidx % 4) * T
        xhc = np.zeros((TP, D), np.float32)
        klo = q0 - 8
        lo = max(klo, 0)
        hi = min(klo + TP, SEQ)
        xhc[lo - klo: hi - klo] = x[b, lo:hi]

        maddh = np.full((2, 128, TSBW), NEG, np.float32)
        for half in range(2):
            for jj in range(4):
                j = half * 4 + jj
                ig = q0 + j * 128 + pp
                jg = klo + j * 128 + ff
                valid = band & (jg >= 0) & (jg < SEQ)
                Aa = np.where(valid, 0.0, NEG).astype(np.float32)
                for h in range(2):
                    s = jj * 2 + h
                    maddh[half, :, s * SLOTW:(s + 1) * SLOTW] = Aa
            maddh[half, :, 8 * SLOTW] = 0.0
        maddh = maddh.astype(ml_dtypes.bfloat16)

        in_maps.append({
            "xh": xhc, "wqkv": wqkv, "wproj": wproj, "wg": wg_a, "wu": wu_a,
            "wd": wd_a, "pbh": pbh, "lnwh": lnwh, "mmulh": mmulh, "maddh": maddh,
        })
    return in_maps


def kernel(**inputs):
    in_maps = _host_prep(inputs)
    nc = _get_nc()
    res = run_bass_kernel_spmd(nc, in_maps, core_ids=list(range(8)))
    out = np.stack([r["y"] for r in res.results], axis=0)
    return out.reshape(2, SEQ, D)


# version 06
# speedup vs baseline: 1.4847x; 1.1046x over previous
import numpy as np
import ml_dtypes
import concourse.bass as bass
import concourse.mybir as mybir
from concourse.bass_utils import run_bass_kernel_spmd

F32 = mybir.dt.float32
BF16 = mybir.dt.bfloat16
AF = mybir.ActivationFunctionType
OP = mybir.AluOpType
AX = mybir.AxisListType

D = 2048
KC = 16
T = 1024
TP = 1152
TH = 1040
NT = 9
I = 8192
IC = 64
NPAIR = 16
SEQ = 4096
CR = float(1.5 * 2.0 ** 23)
NEG = -1.0e30
SLOTW = 144
TSBW = 9 * SLOTW
QKV_NT = [(0, 352), (352, 352), (704, 336)]
QKV_NT_V = [(0, 384), (384, 384), (768, 384)]


class Sem:
    def __init__(self, nc, name):
        self.h = nc.alloc_semaphore(name)
        self.n = 0


_COMPUTE_OPS = {
    "tensor_tensor", "tensor_scalar", "tensor_copy", "tensor_add", "tensor_sub",
    "tensor_mul", "tensor_max", "reduce_sum", "reduce_max", "tensor_reduce",
    "reciprocal", "activation", "copy", "mul", "add", "square", "sqrt", "sign",
    "memset", "memzero", "iota", "scalar_tensor_tensor", "tensor_single_scalar",
    "tensor_scalar_add", "tensor_scalar_sub", "tensor_scalar_mul",
    "tensor_scalar_max", "tensor_scalar_min", "select", "bn_stats", "bn_aggr",
}


class ES:

    def __init__(self, e, sem):
        self._e = e
        self._s = sem

    def __getattr__(self, name):
        fn = getattr(self._e, name)
        if name in _COMPUTE_OPS:
            s = self._s
            e = self._e

            def wrapped(*a, **k):
                if s.n > 0:
                    e.wait_ge(s.h, s.n)
                ins = fn(*a, **k)
                s.n += 1
                ins.then_inc(s.h, 1)
                return ins
            return wrapped
        if name == "dma_start":
            s = self._s
            e = self._e

            def wrapped_dma(*a, **k):
                if s.n > 0:
                    e.wait_ge(s.h, s.n)
                return fn(*a, **k)
            return wrapped_dma
        return fn


def build_nc():
    nc = bass.Bass("TRN2", target_bir_lowering=False, debug=False, num_swdge_queues=4)

    xh = nc.dram_tensor("xh", [TP, D], F32, kind="ExternalInput").ap()
    wqkv = nc.dram_tensor("wqkv", [NPAIR, 128, KC * 384], F32, kind="ExternalInput").ap()
    wproj = nc.dram_tensor("wproj", [4, 128, KC * 512], F32, kind="ExternalInput").ap()
    wg_d = nc.dram_tensor("wg", [16, KC, 128, 512], F32, kind="ExternalInput").ap()
    wu_d = nc.dram_tensor("wu", [16, KC, 128, 512], F32, kind="ExternalInput").ap()
    wd_d = nc.dram_tensor("wd", [16, IC, 128, 128], F32, kind="ExternalInput").ap()
    pbh = nc.dram_tensor("pbh", [128, 16], F32, kind="ExternalInput").ap()
    lnwh = nc.dram_tensor("lnwh", [128, 64], F32, kind="ExternalInput").ap()
    mmulh = nc.dram_tensor("mmulh", [128, TSBW], BF16, kind="ExternalInput").ap()
    maddh = nc.dram_tensor("maddh", [2, 128, TSBW], BF16, kind="ExternalInput").ap()

    y = nc.dram_tensor("y", [T, D], F32, kind="ExternalOutput").ap()
    hscr = nc.dram_tensor("hscr", [IC, 128, T], F32, kind="Internal").ap()
    ascr = nc.dram_tensor("ascr", [KC, 128, T], F32, kind="Internal").ap()
    atscr = nc.dram_tensor("atscr", [KC, 128, T], BF16, kind="Internal").ap()

    A = nc.alloc_sbuf_tensor_at

    p = 16512
    def pal(name, shape, dtype):
        nonlocal p
        p = (p + 63) // 64 * 64
        t = A(name, shape, dtype, offset=p)
        esz = {BF16: 2, F32: 4}[dtype]
        p += int(np.prod(shape[1:])) * esz
        return t

    identf = pal("identf", [128, 128], F32)
    iof = pal("iof", [128, 128], F32)
    iop = pal("iop", [128, 1], F32)
    ones_bf = pal("ones_bf", [128, 128], BF16)
    ones_f = pal("ones_f", [128, 128], F32)
    ones1 = pal("ones1", [1, 128], F32)
    tmp128 = pal("tmp128", [128, 128], F32)
    c_round = pal("c_round", [128, 1], F32)
    c_nhalf = pal("c_nhalf", [128, 1], F32)
    mmul = pal("mmul", [128, TSBW], BF16)
    madd = pal("madd", [128, 2 * TSBW], BF16)
    pb_sb = pal("pb_sb", [128, 16], F32)
    lnw_sb = pal("lnw_sb", [128, 64], F32)
    partials = pal("partials", [128, 48], F32)
    swg = pal("swg", [128, 1], F32); iswg = pal("iswg", [128, 1], F32)
    swu = pal("swu", [128, 1], F32); iswu = pal("iswu", [128, 1], F32)
    swd = pal("swd", [128, 1], F32); iswd = pal("iswd", [128, 1], F32)
    ln_mean = pal("ln_mean", [128, NT], F32)
    ln_rstd = pal("ln_rstd", [128, NT], F32)
    ln_srow = pal("ln_srow", [128, NT], F32)
    isx_row = pal("isx_row", [1, TP], F32)
    ISX = pal("ISX", [128, T], F32)
    Mhi = pal("Mhi", [128, T], F32)
    Mlo = pal("Mlo", [128, T], F32)
    r_bc = pal("r_bc", [128, T], F32)
    RS = pal("RS", [128, T], F32)
    INVD = pal("INVD", [128, T], F32)
    sh_row = pal("sh_row", [1, T], F32)
    iv_row = pal("iv_row", [1, T], F32)
    amax_tok = pal("amax_tok", [128, 8], F32)
    rtok = pal("rtok", [128, 8], F32)
    mrms = pal("mrms", [128, 8], F32)
    shtok = pal("shtok", [128, 8], F32)
    ivtok = pal("ivtok", [128, 8], F32)
    negmax = pal("negmax", [128, 1], F32)
    rsum = pal("rsum", [128, 9], F32)
    rinv = pal("rinv", [128, 9], F32)
    tv1 = pal("tv1", [128, 8], F32)
    tv2 = pal("tv2", [128, 8], F32)
    assert p < 66512, p

    xnT = A("xnT", [128, KC * TP], BF16, offset=66560)
    qxT = A("qxT", [128, KC * TP], BF16, offset=66560)
    hq = A("hq", [128, IC * 512], BF16, offset=66560)
    ot3 = A("ot3", [128, D], F32, offset=132096)
    xtok = A("xtok", [128, D], F32, offset=103424)
    xn_t = A("xn_t", [128, D], F32, offset=111744)
    xnb_t = A("xnb_t", [128, D], BF16, offset=119936)
    sq_t = A("sq_t", [128, D], BF16, offset=124032)
    T_sb = A("T_sb", [128, TSBW], F32, offset=103424)
    E_bf = A("E_bf", [128, TSBW], BF16, offset=108672)
    P_bf = A("P_bf", [128, TSBW], BF16, offset=111296)
    PT_a = A("PT_a", [128, 8 * 128], BF16, offset=113920)
    PT_b = A("PT_b", [128, 8 * 128], BF16, offset=115968)
    ath0 = A("ath0", [64, 2 * T], BF16, offset=118016)
    ath1 = A("ath1", [64, 2 * T], BF16, offset=130304)
    wstage = A("wstage", [128, KC * 512], F32, offset=103424)
    wq_bf0 = A("wq_bf0", [128, KC * 384], BF16, offset=140352)
    wq_bf1 = A("wq_bf1", [128, KC * 384], BF16, offset=152640)
    qm0 = A("qm0", [128, T], BF16, offset=164928)
    qm1 = A("qm1", [128, T], BF16, offset=167040)
    kT_p = A("kT_p", [128, TH], BF16, offset=128192)
    vT_p = A("vT_p", [128, TP], BF16, offset=169152)
    atrb = A("atrb", [128, 2 * T], BF16, offset=140352)
    tern_g = A("tern_g", [128, KC * 512], BF16, offset=140352)
    tern_u = A("tern_u", [128, KC * 512], BF16, offset=156736)
    wdstage = A("wdstage", [128, IC * 128], F32, offset=140352)
    v_tok = [A("v_tok0", [128, NT * 128], BF16, offset=173120),
             A("v_tok1", [128, NT * 128], BF16, offset=124032)]
    f0st = A("f0st", [128, 2 * 2048], F32, offset=175552)
    astg = A("astg", [128, 4 * T], F32, offset=173120)
    xtok2 = A("xtok2", [128, D], F32, offset=173120)
    xn2 = A("xn2", [128, D], F32, offset=181312)
    qx2 = A("qx2", [128, D], BF16, offset=189568)
    hstage = A("hstage", [128, 4 * T], F32, offset=173120)
    sg_t = A("sg_t", [128, 512], F32, offset=189568)
    sgp_t = A("sgp_t", [128, 512], F32, offset=191616)
    hl_t = A("hl_t", [128, 512], F32, offset=193728)
    hsq_t = A("hsq_t", [128, 512], BF16, offset=195840)
    hrb = A("hrb", [128, 2 * 512], F32, offset=189568)
    tern_d = A("tern_d", [128, IC * 128], BF16, offset=173120)
    aprb = A("aprb", [128, 2 * 512], F32, offset=196928)
    ISXU = A("ISXU", [128, T], F32, offset=196928)
    fin_t = A("fin_t", [128, 512], F32, offset=201024)
    ot2 = A("ot2", [128, D], F32, offset=203136)
    pw_bf = A("pw_bf", [128, KC * 512], BF16, offset=203136)
    ot0 = A("ot0", [128, D], F32, offset=211328)
    ot1 = A("ot1", [128, D], F32, offset=219520)

    psA = nc.alloc_psum_tensor("psA", [128, 4096], F32)

    def bank(i, w=512):
        return psA[:, i * 512:i * 512 + w]

    sV = Sem(nc, "sV")
    sS = Sem(nc, "sS")
    sM = Sem(nc, "sM")
    sP = Sem(nc, "sP")
    sO = Sem(nc, "sO")
    sF0 = Sem(nc, "sF0")
    sVt = Sem(nc, "sVt")
    sVm = Sem(nc, "sVm")
    sXm = Sem(nc, "sXm")
    sXx = Sem(nc, "sXx")
    sXw = Sem(nc, "sXw")
    sXw2 = Sem(nc, "sXw2")
    sXa = [Sem(nc, "sXa0"), Sem(nc, "sXa1")]
    sTp = Sem(nc, "sTp")
    sTt = Sem(nc, "sTt")
    sTv = Sem(nc, "sTv")
    sTat = [Sem(nc, "sTat0"), Sem(nc, "sTat1")]
    sTa4 = Sem(nc, "sTa4")
    sTh = Sem(nc, "sTh")
    sTr = [Sem(nc, "sTr0"), Sem(nc, "sTr1")]
    sTap = [Sem(nc, "sTap0"), Sem(nc, "sTap1")]
    sGw = [Sem(nc, "sGw0"), Sem(nc, "sGw1")]
    sGf = [Sem(nc, "sGf0"), Sem(nc, "sGf1")]
    sGp = Sem(nc, "sGp")

    ev = {}

    def mark(label, s):
        ev[label] = (s, s.n)

    def wv(eng, label):
        s, n = ev[label]
        if n > 0:
            eng.wait_ge(s.h, n)

    def inc(ins, s, k=1):
        s.n += k
        ins.then_inc(s.h, k)

    f0_list = []
    for wi in range(3):
        for g in range(16):
            for q in range(4):
                f0_list.append((wi, g, q))

    def f0_dma(e, qi):
        wi, g, q = f0_list[qi]
        if qi >= 2:
            e.wait_ge(sF0.h, qi - 1)
        wt = (wg_d, wu_d, wd_d)[wi]
        nk = 4 if wi < 2 else 16
        src = wt[g, q * nk:(q + 1) * nk, :, :]
        dst = f0st[:, (qi % 2) * 2048:(qi % 2) * 2048 + 2048].rearrange("p (k c) -> p k c", k=nk)
        ins = e.dma_start(out=dst, in_=src.rearrange("k p c -> p k c"))
        inc(ins, sGf[qi % 2], 16)
        mark("f0_dma_%d" % qi, sGf[qi % 2])

    def f0_abs_s(e, qi):
        wv(e, "f0_dma_%d" % qi)
        if qi >= 8:
            e.wait_ge(sF0.h, qi - 7)
        sl = f0st[:, (qi % 2) * 2048:(qi % 2) * 2048 + 2048]
        ins = e.activation(out=sl, in_=sl, func=AF.Abs,
                           accum_out=tv2[:, qi % 8: qi % 8 + 1])
        mark("f0_abs_%d" % qi, sS)

    def f0_red(e, qi):
        wi, g, q = f0_list[qi]
        wv(e, "f0_abs_%d" % qi)
        col = wi * 16 + g
        if q == 0:
            ins = e.tensor_copy(out=partials[:, col:col + 1], in_=tv2[:, qi % 8: qi % 8 + 1])
        else:
            ins = e.tensor_add(out=partials[:, col:col + 1], in0=partials[:, col:col + 1],
                               in1=tv2[:, qi % 8: qi % 8 + 1])
        mark("f0_addv_%d" % qi, sV)
        ins2 = e.engine_nop()
        inc(ins2, sF0, 1)

    with nc.Block() as blk:
        @blk.sync
        def _(e):
            ins = e.dma_start(out=mmul[:], in_=mmulh[:, :]); inc(ins, sXm, 16)
            ins = e.dma_start(out=madd[:].rearrange("p (a c) -> p a c", a=2),
                              in_=maddh.rearrange("a p c -> p a c")); inc(ins, sXm, 16)
            ins = e.dma_start(out=pb_sb[:], in_=pbh[:, :]); inc(ins, sXm, 16)
            ins = e.dma_start(out=lnw_sb[:], in_=lnwh[:, :]); inc(ins, sXm, 16)
        mark("init_dma", sXm)

        @blk.gpsimd
        def _(e):
            e = ES(e, sP)
            ins = e.iota(iof[:], [[1, 128]], channel_multiplier=0, allow_small_or_imprecise_dtypes=True)
            ins = e.iota(iop[:], [[0, 1]], channel_multiplier=1, allow_small_or_imprecise_dtypes=True)
            e.tensor_scalar(out=identf[:], in0=iof[:], scalar1=iop[:, 0:1], scalar2=None, op0=OP.is_equal)
            e.memset(ones_bf[:], 1.0)
            e.memset(ones_f[:], 1.0)
            e.memset(ones1[:], 1.0)
            e.memset(c_round[:], CR)
            e.memset(c_nhalf[:], -0.5)
            e.memset(qm0[:], 0.0)
            e.memset(qm1[:], 0.0)
            e.memset(Mhi[:], -1.0e30)
            e.memset(Mlo[:], 1.0e30)
            e.memset(v_tok[0][:], 0.0)
            ins = e.memset(v_tok[1][:], 0.0)
        mark("init_gp", sP)

        for t in range(NT):
            @blk.sync
            def _(e, t=t):
                if t > 0:
                    wv(e, "a1_done_%d" % (t - 1))
                ins = e.dma_start(out=xtok[:], in_=xh[t * 128:(t + 1) * 128, :])
                inc(ins, sXx, 16)
            mark("a1_x_%d" % t, sXx)

            @blk.vector
            def _(e, t=t):
                e = ES(e, sV)
                if t == 0:
                    wv(e, "init_gp")
                wv(e, "a1_x_%d" % t)
                e.reduce_sum(out=tv1[:, 0:1], in_=xtok[:], axis=AX.X)
                ins = e.tensor_scalar(out=ln_mean[:, t:t + 1], in0=tv1[:, 0:1],
                                      scalar1=1.0 / D, scalar2=None, op0=OP.mult)
            mark("a1_mean_%d" % t, sV)

            @blk.scalar
            def _(e, t=t):
                e = ES(e, sS)
                if t == 0:
                    wv(e, "init_gp")
                wv(e, "a1_x_%d" % t)
                ins = e.activation(out=sq_t[:], in_=xtok[:], func=AF.Square, accum_out=tv2[:, 0:1])
            mark("a1_ssq_%d" % t, sS)

            @blk.vector
            def _(e, t=t):
                e = ES(e, sV)
                wv(e, "a1_ssq_%d" % t)
                wv(e, "a1_mean_%d" % t)
                e.tensor_scalar(out=tv2[:, 1:2], in0=tv2[:, 0:1], scalar1=1.0 / D, scalar2=1e-5,
                                op0=OP.mult, op1=OP.add)
                e.tensor_mul(out=tv2[:, 2:3], in0=ln_mean[:, t:t + 1], in1=ln_mean[:, t:t + 1])
                ins = e.tensor_sub(out=tv2[:, 3:4], in0=tv2[:, 1:2], in1=tv2[:, 2:3])
            mark("a1_var_%d" % t, sV)

            @blk.scalar
            def _(e, t=t):
                e = ES(e, sS)
                wv(e, "a1_var_%d" % t)
                e.activation(out=tv2[:, 4:5], in_=tv2[:, 3:4], func=AF.Ln)
                ins = e.activation(out=ln_rstd[:, t:t + 1], in_=tv2[:, 4:5], func=AF.Exp,
                                   scale=c_nhalf[:, 0:1])
            mark("a1_rstd_%d" % t, sS)

            @blk.vector
            def _(e, t=t):
                e = ES(e, sV)
                wv(e, "a1_rstd_%d" % t)
                ins = e.tensor_scalar(out=xn_t[:], in0=xtok[:], scalar1=ln_mean[:, t:t + 1],
                                      scalar2=ln_rstd[:, t:t + 1], op0=OP.subtract, op1=OP.mult)
                e.reduce_max(out=tv1[:, 1:2], in_=xn_t[:], axis=AX.X, apply_absolute_value=True)
                e.tensor_scalar(out=tv1[:, 2:3], in0=tv1[:, 1:2], scalar1=1e-5, scalar2=None, op0=OP.max)
                e.reciprocal(out=tv1[:, 3:4], in_=tv1[:, 2:3])
                e.tensor_scalar(out=ln_srow[:, t:t + 1], in0=tv1[:, 3:4], scalar1=127.0,
                                scalar2=None, op0=OP.mult)
                ins = e.tensor_scalar(out=tmp128[:], in0=ones_f[:],
                                      scalar1=tv1[:, 2:3], scalar2=1.0 / 127.0, op0=OP.mult, op1=OP.mult)
            mark("a1_xn_%d" % t, sV)

            @blk.tensor
            def _(e, t=t):
                wv(e, "a1_xn_%d" % t)
                if t == 0:
                    wv(e, "init_gp")
                if t > 0:
                    wv(e, "a1_isxrow_%d" % (t - 1))
                ins = e.transpose(psA[:, 3584:3584 + 128], tmp128[:], identf[:])
                inc(ins, sM, 1)
            mark("a1_isxT_%d" % t, sM)

            @blk.vector
            def _(e, t=t):
                e = ES(e, sV)
                wv(e, "a1_isxT_%d" % t)
                ins = e.tensor_copy(out=isx_row[0:1, t * 128:(t + 1) * 128],
                                    in_=psA[0:1, 3584:3584 + 128])
            mark("a1_isxrow_%d" % t, sV)

            @blk.scalar
            def _(e, t=t):
                e = ES(e, sS)
                wv(e, "a1_xn_%d" % t)
                if t > 0:
                    wv(e, "a1_done_%d" % (t - 1))
                ins = e.copy(out=xnb_t[:], in_=xn_t[:])
            mark("a1_cast_%d" % t, sS)

            @blk.scalar
            def _(e, t=t):
                e = ES(e, sS)
                wv(e, "a1_cast_%d" % t)
                for kc in range(KC):
                    ins = e.dma_start(out=xnT[:, kc * TP + t * 128: kc * TP + (t + 1) * 128],
                                      in_=xnb_t[:, kc * 128:(kc + 1) * 128], transpose=True)
                    inc(ins, sTt, 16)
            mark("a1_done_%d" % t, sTt)
        mark("a1_all", sTt)

        @blk.tensor
        def _(e):
            wv(e, "a1_isxrow_%d" % (NT - 1))
            e.matmul(psA[:, 3072:3584], ones1[:], isx_row[0:1, 8:520], start=True, stop=True)
            ins = e.matmul(psA[:, 3584:4096], ones1[:], isx_row[0:1, 520:1032], start=True, stop=True)
            inc(ins, sM, 1)
        mark("isx_mm", sM)

        @blk.vector
        def _(e):
            e = ES(e, sV)
            wv(e, "isx_mm")
            e.tensor_copy(out=ISX[:, 0:512], in_=psA[:, 3072:3584])
            ins = e.tensor_copy(out=ISX[:, 512:1024], in_=psA[:, 3584:4096])
        mark("isx_done", sV)

        f0_emitted = 0
        f0_s_emitted = 0

        def emit_wq(pr):
            wbuf = (wq_bf0, wq_bf1)[pr % 2]

            @blk.gpsimd
            def _(e, pr=pr, wbuf=wbuf):
                e = ES(e, sP)
                if pr >= 2:
                    wv(e, "qkv_mm_%d" % (pr - 2))
                ins = e.dma_start(out=wbuf[:], in_=wqkv[pr])
                inc(ins, sGw[pr % 2], 16)
            mark("wq_%d" % pr, sGw[pr % 2])

        emit_wq(0)
        def emit_qkv_pair(pr):
            wbuf = (wq_bf0, wq_bf1)[pr % 2]
            if pr + 1 < NPAIR:
                emit_wq(pr + 1)

            @blk.gpsimd
            def _(e, pr=pr):
                e = ES(e, sP)
                for k in range(12):
                    qi = pr * 12 + k
                    if qi < 192:
                        f0_dma(e, qi)

            for mi in range(3):
                for ti, (n0, nn) in enumerate(QKV_NT if mi < 2 else QKV_NT_V):
                    u = mi * 3 + ti

                    @blk.tensor
                    def _(e, pr=pr, mi=mi, ti=ti, n0=n0, nn=nn, u=u, wbuf=wbuf):
                        if u == 0:
                            wv(e, "wq_%d" % pr)
                            if pr == 0:
                                wv(e, "a1_all")
                                wv(e, "isx_done")
                        gu = pr * 9 + u
                        if gu >= 2:
                            wv(e, "qkv_ev_%d" % (gu - 2))
                        for kc in range(KC):
                            ins = e.matmul(bank(gu % 2, nn),
                                           wbuf[:, kc * 384 + mi * 128: kc * 384 + (mi + 1) * 128],
                                           xnT[:, kc * TP + n0: kc * TP + n0 + nn],
                                           start=(kc == 0), stop=(kc == KC - 1))
                        inc(ins, sM, 1)
                    mark("qkv_ps_%d_%d" % (pr, u), sM)

                    @blk.vector
                    def _(e, pr=pr, mi=mi, n0=n0, nn=nn, u=u):
                        e = ES(e, sV)
                        gu = pr * 9 + u
                        wv(e, "qkv_ps_%d_%d" % (pr, u))
                        if mi == 0:
                            s0 = max(n0, 8) - n0
                            s1 = min(n0 + nn, 8 + T) - n0
                            d0 = n0 + s0 - 8
                            ins = e.tensor_scalar(out=qm0[0:64, d0:d0 + (s1 - s0)],
                                                  in0=bank(gu % 2, nn)[0:64, s0:s1],
                                                  scalar1=0.125, scalar2=None, op0=OP.mult)
                            ins = e.tensor_scalar(out=qm1[64:128, d0:d0 + (s1 - s0)],
                                                  in0=bank(gu % 2, nn)[64:128, s0:s1],
                                                  scalar1=0.125, scalar2=None, op0=OP.mult)
                        else:
                            if mi == 2 and ti == 0 and pr > 0:
                                wv(e, "vtok_%d" % (pr - 1))
                            dst = (None, kT_p, vT_p)[mi]
                            ins = e.tensor_copy(out=dst[:, n0:n0 + nn], in_=bank(gu % 2, nn))
                    mark("qkv_ev_%d" % (pr * 9 + u), sV)
            mark("qkv_mm_%d" % pr, sM)
            mark("qkv_ev_done_%d" % pr, sV)

            @blk.scalar
            def _(e, pr=pr):
                e = ES(e, sS)
                wv(e, "qkv_ev_done_%d" % pr)
                if pr >= 2:
                    wv(e, "a3_pv_%d_1" % (pr - 2))
                for tt in range(NT):
                    ins = e.dma_start(out=v_tok[pr % 2][:, tt * 128: tt * 128 + 128],
                                      in_=vT_p[:, tt * 128: tt * 128 + 128], transpose=True)
                    inc(ins, sTv, 16)
            mark("vtok_%d" % pr, sTv)


        def emit_scores_half(pr, half):
            if True:
                hh = pr * 2 + half

                @blk.tensor
                def _(e, pr=pr, half=half, hh=hh):
                    if hh > 0:
                        wv(e, "a3_Tsb_%d" % (hh - 1))
                    for jj in range(4):
                        j = half * 4 + jj
                        for h in range(2):
                            slot = jj * 2 + h
                            off = 1024 + (slot // 3) * 512 + (slot % 3) * 144
                            qm = (qm0, qm1)[h]
                            ins = e.matmul(psA[:, off:off + 144],
                                           qm[:, j * 128: (j + 1) * 128],
                                           kT_p[:, j * 128: j * 128 + 144],
                                           start=True, stop=True)
                    off8 = 1024 + 2 * 512 + 2 * 144
                    ins = e.matmul(psA[:, off8:off8 + 144],
                                   qm0[:, half * 512: half * 512 + 128],
                                   kT_p[:, half * 512: half * 512 + 144],
                                   start=True, stop=True)
                    inc(ins, sM, 1)
                mark("a3_sc_%d" % hh, sM)

                @blk.vector
                def _(e, pr=pr, half=half, hh=hh):
                    e = ES(e, sV)
                    if hh == 0:
                        wv(e, "init_dma")
                    else:
                        wv(e, "a3_exp_%d" % (hh - 1))
                    wv(e, "a3_sc_%d" % hh)
                    psc = psA[:, 1024:1024 + 1536].rearrange("p (b s) -> p b s", b=3)[:, :, 0:432]
                    ins = e.tensor_tensor(out=T_sb[:].rearrange("p (b c) -> p b c", b=3), in0=psc,
                                          in1=mmul[:].rearrange("p (b c) -> p b c", b=3), op=OP.mult)
                    mark("a3_Tsb_%d" % hh, sV)
                    e.tensor_tensor(out=T_sb[:], in0=T_sb[:],
                                    in1=madd[:, half * TSBW:(half + 1) * TSBW], op=OP.add)
                    ins = e.reduce_max(out=negmax[:], in_=T_sb[:], axis=AX.X, negate=True)
                mark("a3_mx_%d" % hh, sV)

                @blk.scalar
                def _(e, hh=hh):
                    e = ES(e, sS)
                    wv(e, "a3_mx_%d" % hh)
                    ins = e.activation(out=E_bf[:], in_=T_sb[:], func=AF.Exp, bias=negmax[:, 0:1])
                mark("a3_exp_%d" % hh, sS)

                @blk.vector
                def _(e, hh=hh):
                    e = ES(e, sV)
                    wv(e, "a3_exp_%d" % hh)
                    if hh > 0:
                        wv(e, "a3_pt_%d" % (hh - 1))
                    e.reduce_sum(out=rsum[:], in_=E_bf[:].rearrange("p (s c) -> p s c", s=9), axis=AX.X)
                    e.reciprocal(out=rinv[:], in_=rsum[:])
                    for slot in range(9):
                        ins = e.tensor_scalar(out=P_bf[:, slot * 144:(slot + 1) * 144],
                                              in0=E_bf[:, slot * 144:(slot + 1) * 144],
                                              scalar1=rinv[:, slot:slot + 1], scalar2=None, op0=OP.mult)
                mark("a3_P_%d" % hh, sV)

                @blk.sync
                def _(e, hh=hh):
                    wv(e, "a3_P_%d" % hh)
                    if hh > 0:
                        wv(e, "a3_pv_%d_%d" % ((hh - 1) // 2, (hh - 1) % 2))
                    for slot in range(8):
                        ins = e.dma_start(out=PT_a[:, slot * 128:(slot + 1) * 128],
                                          in_=P_bf[:, slot * 144: slot * 144 + 128], transpose=True)
                        inc(ins, sTp, 16)
                        ins = e.dma_start(out=PT_b[:, slot * 128:(slot + 1) * 128],
                                          in_=P_bf[:, slot * 144 + 128: slot * 144 + 256], transpose=True)
                        inc(ins, sTp, 16)
                mark("a3_pt_%d" % hh, sTp)


        def emit_pv_half(pr, half):
            if True:
                hh = pr * 2 + half
                @blk.tensor
                def _(e, pr=pr, half=half, hh=hh):
                    wv(e, "a3_pt_%d" % hh)
                    wv(e, "vtok_%d" % pr)
                    if hh > 0:
                        wv(e, "a3_oev_%d" % (hh - 1))
                    for jj in range(4):
                        j = half * 4 + jj
                        for h in range(2):
                            slot = jj * 2 + h
                            out = psA[0:64, 2560 + slot * 128: 2560 + (slot + 1) * 128]
                            vt = v_tok[pr % 2]
                            e.matmul(out, vt[:, j * 128 + 64 * h: j * 128 + 64 * h + 64],
                                     PT_a[:, slot * 128:(slot + 1) * 128], start=True, stop=False)
                            ins = e.matmul(out,
                                           vt[0:16, (j + 1) * 128 + 64 * h: (j + 1) * 128 + 64 * h + 64],
                                           PT_b[0:16, slot * 128:(slot + 1) * 128], start=False, stop=True)
                    inc(ins, sM, 1)
                mark("a3_pv_%d_%d" % (pr, half), sM)

                @blk.vector
                def _(e, pr=pr, half=half):
                    e = ES(e, sV)
                    wv(e, "a3_pv_%d_%d" % (pr, half))
                    if pr >= 2 and half == 0:
                        wv(e, "at_spill_%d" % (pr - 2))
                    for jj in range(4):
                        for h in range(2):
                            slot = jj * 2 + h
                            dst = (ath0, ath1)[h]
                            ins = e.tensor_copy(out=dst[:, (pr % 2) * T + half * 512 + jj * 128:
                                                        (pr % 2) * T + half * 512 + (jj + 1) * 128],
                                                in_=psA[0:64, 2560 + slot * 128: 2560 + (slot + 1) * 128])
                mark("a3_oev_%d" % (pr * 2 + half), sV)

            if half == 1:
                @blk.scalar
                def _(e, pr=pr):
                    e = ES(e, sS)
                    wv(e, "a3_oev_%d" % (pr * 2 + 1))
                    ins = e.dma_start(out=atscr[pr][0:64, :], in_=ath0[:, (pr % 2) * T:(pr % 2) * T + T])
                    inc(ins, sTat[pr % 2], 16)
                    ins = e.dma_start(out=atscr[pr][64:128, :], in_=ath1[:, (pr % 2) * T:(pr % 2) * T + T])
                    inc(ins, sTat[pr % 2], 16)
                mark("at_spill_%d" % pr, sTat[pr % 2])


        for pr in range(NPAIR):
            emit_qkv_pair(pr)
            for half in range(2):
                hh = pr * 2 + half
                if hh >= 1:
                    emit_pv_half((hh - 1) // 2, (hh - 1) % 2)
                emit_scores_half(pr, half)

            @blk.scalar
            def _(e, pr=pr):
                e = ES(e, sS)
                nonlocal f0_s_emitted
                hi = min((pr + 1) * 12, 192)
                while f0_s_emitted < hi:
                    f0_abs_s(e, f0_s_emitted)
                    f0_s_emitted += 1

            @blk.vector
            def _(e, pr=pr):
                e = ES(e, sV)
                nonlocal f0_emitted
                hi = min((pr + 1) * 12, 192)
                while f0_emitted < hi:
                    f0_red(e, f0_emitted)
                    f0_emitted += 1
        emit_pv_half(NPAIR - 1, 1)
        @blk.scalar
        def _(e):
            e = ES(e, sS)
            nonlocal f0_s_emitted
            while f0_s_emitted < 192:
                f0_abs_s(e, f0_s_emitted)
                f0_s_emitted += 1

        @blk.vector
        def _(e):
            e = ES(e, sV)
            nonlocal f0_emitted
            while f0_emitted < 192:
                f0_red(e, f0_emitted)
                f0_emitted += 1
        mark("f0_red_all", sV)

        @blk.vector
        def _(e):
            e = ES(e, sV)
            wv(e, "f0_red_all")
            e.reduce_sum(out=tv1[:, 0:1], in_=partials[:, 0:16], axis=AX.X)
            e.reduce_sum(out=tv1[:, 1:2], in_=partials[:, 16:32], axis=AX.X)
            ins = e.reduce_sum(out=tv1[:, 2:3], in_=partials[:, 32:48], axis=AX.X)
        mark("f0_sums", sV)

        @blk.tensor
        def _(e):
            wv(e, "f0_sums")
            wv(e, "a3_oev_%d" % (NPAIR * 2 - 1))
            e.matmul(psA[:, 3072:3073], ones_f[:], tv1[:, 0:1], start=True, stop=True)
            e.matmul(psA[:, 3073:3074], ones_f[:], tv1[:, 1:2], start=True, stop=True)
            ins = e.matmul(psA[:, 3074:3075], ones_f[:], tv1[:, 2:3], start=True, stop=True)
            inc(ins, sM, 1)
        mark("f0_bc", sM)

        @blk.vector
        def _(e):
            e = ES(e, sV)
            wv(e, "f0_bc")
            nel = float(I * D)
            for wi, (ssb, isb) in enumerate(((swg, iswg), (swu, iswu), (swd, iswd))):
                e.tensor_scalar(out=isb[:], in0=psA[:, 3072 + wi:3073 + wi],
                                scalar1=1.0 / nel, scalar2=1e-5, op0=OP.mult, op1=OP.max)
                ins = e.reciprocal(out=ssb[:], in_=isb[:])
            ins = e.tensor_scalar(out=ISXU[:], in0=ISX[:], scalar1=iswu[:, 0:1],
                                  scalar2=None, op0=OP.mult)
        mark("f0_scales", sV)

        for g in range(4):
            @blk.gpsimd
            def _(e, g=g):
                e = ES(e, sP)
                if g == 0:
                    wv(e, "f0_red_all")
                else:
                    wv(e, "a4_mm_%d" % (g - 1))
                ins = e.dma_start(out=pw_bf[:], in_=wproj[g])
                inc(ins, sGp, 16)
            mark("a4_w_%d" % g, sGp)

            for kc in range(KC):
                @blk.sync
                def _(e, g=g, kc=kc):
                    wv(e, "at_spill_%d" % kc)
                    gk = g * KC + kc
                    if gk >= 2:
                        wv(e, "a4_mmk_%d" % (gk - 2))
                    ins = e.dma_start(out=atrb[:, (kc % 2) * T:(kc % 2) * T + T], in_=atscr[kc])
                    inc(ins, sXa[kc % 2], 16)
                mark("a4_rb_%d_%d" % (g, kc), sXa[kc % 2])

                @blk.tensor
                def _(e, g=g, kc=kc):
                    if kc == 0:
                        wv(e, "a4_w_%d" % g)
                        if g == 0:
                            wv(e, "f0_scales")
                        else:
                            wv(e, "a4_ev_%d" % (g - 1))
                    wv(e, "a4_rb_%d_%d" % (g, kc))
                    for mi in range(4):
                        for nt in range(2):
                            ins = e.matmul(bank(mi * 2 + nt),
                                           pw_bf[:, kc * 512 + mi * 128: kc * 512 + (mi + 1) * 128],
                                           atrb[:, (kc % 2) * T + nt * 512:
                                                (kc % 2) * T + (nt + 1) * 512],
                                           start=(kc == 0), stop=(kc == KC - 1))
                    inc(ins, sM, 1)
                mark("a4_mmk_%d" % (g * KC + kc), sM)
            mark("a4_mm_%d" % g, sM)

            @blk.vector
            def _(e, g=g):
                e = ES(e, sV)
                wv(e, "a4_mm_%d" % g)
                if g >= 1:
                    wv(e, "a4_spill_%d" % (g - 1))
                for mi in range(4):
                    for nt in range(2):
                        m = g * 4 + mi
                        ins = e.tensor_scalar(out=astg[:, mi * T + nt * 512: mi * T + (nt + 1) * 512],
                                              in0=bank(mi * 2 + nt), scalar1=pb_sb[:, m:m + 1],
                                              scalar2=None, op0=OP.add)
            mark("a4_ev_%d" % g, sV)

            @blk.scalar
            def _(e, g=g):
                e = ES(e, sS)
                wv(e, "a4_ev_%d" % g)
                ins = e.dma_start(out=ascr[g * 4:(g + 1) * 4].rearrange("k p c -> p k c"),
                                  in_=astg[:].rearrange("p (k c) -> p k c", k=4))
                inc(ins, sTa4, 16)
            mark("a4_spill_%d" % g, sTa4)

        for t in range(NT):
            @blk.sync
            def _(e, t=t):
                if t == 0:
                    wv(e, "a4_spill_3")
                else:
                    wv(e, "f1r_t_%d" % (t - 1))
                ins = e.dma_start(out=xtok2[:], in_=xh[t * 128:(t + 1) * 128, :])
                inc(ins, sXx, 16)
            mark("f1r_x_%d" % t, sXx)

            @blk.vector
            def _(e, t=t):
                e = ES(e, sV)
                wv(e, "f1r_x_%d" % t)
                ins = e.tensor_scalar(out=xn2[:], in0=xtok2[:], scalar1=ln_mean[:, t:t + 1],
                                      scalar2=ln_rstd[:, t:t + 1], op0=OP.subtract, op1=OP.mult)
            mark("f1r_xn_%d" % t, sV)

            @blk.scalar
            def _(e, t=t):
                e = ES(e, sS)
                wv(e, "f1r_xn_%d" % t)
                ins = e.activation(out=xn2[:], in_=xn2[:], func=AF.Identity,
                                   scale=ln_srow[:, t:t + 1], bias=c_round[:, 0:1])
            mark("f1r_y_%d" % t, sS)

            @blk.vector
            def _(e, t=t):
                e = ES(e, sV)
                wv(e, "f1r_y_%d" % t)
                ins = e.tensor_scalar(out=qx2[:], in0=xn2[:], scalar1=CR, scalar2=None, op0=OP.subtract)
            mark("f1r_qx_%d" % t, sV)

            @blk.scalar
            def _(e, t=t):
                e = ES(e, sS)
                wv(e, "f1r_qx_%d" % t)
                for kc in range(KC):
                    ins = e.dma_start(out=qxT[:, kc * TP + t * 128: kc * TP + (t + 1) * 128],
                                      in_=qx2[:, kc * 128:(kc + 1) * 128], transpose=True)
                    inc(ins, sTt, 16)
            mark("f1r_t_%d" % t, sTt)
        mark("f1r_all", sTt)

        for g in range(16):
            for which in range(2):
                lab = "%d_%d" % (g, which)
                warr = (wg_d, wu_d)[which]
                tdst = (tern_g, tern_u)[which]
                ssc = (swg, swu)[which]

                def _w_guards(e, h, g=g, which=which):
                    if which == 1:
                        wv(e, "f1_tern_%d_0_h%d" % (g, h))
                    elif g > 0:
                        wv(e, "f1_tern_%d_1_h%d" % (g - 1, h))
                    else:
                        wv(e, "f1r_all")
                        wv(e, "at_spill_%d" % (NPAIR - 1))
                        wv(e, "a4_mm_3")

                @blk.sync
                def _(e, g=g, warr=warr, _w_guards=_w_guards):
                    _w_guards(e, 0)
                    ins = e.dma_start(out=wstage[:, 0:8 * 512].rearrange("p (k c) -> p k c", k=8),
                                      in_=warr[g][0:8].rearrange("k p c -> p k c"))
                    inc(ins, sXw, 16)
                mark("f1_wdma_%s" % lab, sXw)

                @blk.gpsimd
                def _(e, g=g, warr=warr, _w_guards=_w_guards):
                    _w_guards(e, 1)
                    ins = e.dma_start(out=wstage[:, 8 * 512:16 * 512].rearrange("p (k c) -> p k c", k=8),
                                      in_=warr[g][8:16].rearrange("k p c -> p k c"))
                    inc(ins, sXw2, 16)
                mark("f1_wdma2_%s" % lab, sXw2)

                @blk.scalar
                def _(e, g=g, which=which, lab=lab, ssc=ssc):
                    e = ES(e, sS)
                    if g == 0 and which == 0:
                        wv(e, "f0_scales")
                    wv(e, "f1_wdma_%s" % lab)
                    ins = e.activation(out=wstage[:, 0:8 * 512], in_=wstage[:, 0:8 * 512],
                                       func=AF.Identity, scale=ssc[:, 0:1], bias=c_round[:, 0:1])
                    mark("f1_tern_s_%s_h0" % lab, sS)
                    wv(e, "f1_wdma2_%s" % lab)
                    ins = e.activation(out=wstage[:, 8 * 512:16 * 512], in_=wstage[:, 8 * 512:16 * 512],
                                       func=AF.Identity, scale=ssc[:, 0:1], bias=c_round[:, 0:1])
                mark("f1_tern_s_%s_h1" % lab, sS)

                @blk.vector
                def _(e, g=g, which=which, lab=lab, tdst=tdst):
                    e = ES(e, sV)
                    wv(e, "f1_tern_s_%s_h0" % lab)
                    if g > 0:
                        wv(e, "f1_mm_%d_%d" % (g - 1, which))
                    e.tensor_scalar(out=wstage[:, 0:8 * 512], in0=wstage[:, 0:8 * 512],
                                    scalar1=CR, scalar2=1.0, op0=OP.subtract, op1=OP.min)
                    ins = e.tensor_scalar(out=tdst[:, 0:8 * 512], in0=wstage[:, 0:8 * 512],
                                          scalar1=-1.0, scalar2=None, op0=OP.max)
                    mark("f1_tern_%s_h0" % lab, sV)
                    wv(e, "f1_tern_s_%s_h1" % lab)
                    e.tensor_scalar(out=wstage[:, 8 * 512:16 * 512], in0=wstage[:, 8 * 512:16 * 512],
                                    scalar1=CR, scalar2=1.0, op0=OP.subtract, op1=OP.min)
                    ins = e.tensor_scalar(out=tdst[:, 8 * 512:16 * 512], in0=wstage[:, 8 * 512:16 * 512],
                                          scalar1=-1.0, scalar2=None, op0=OP.max)
                mark("f1_tern_%s_h1" % lab, sV)
                mark("f1_tern_%s" % lab, sV)

            for mi in range(4):
                c = g * 4 + mi
                for nt in range(2):
                    u = mi * 2 + nt
                    gu = g * 8 + u
                    lab = "%d_%d" % (c, nt)

                    @blk.tensor
                    def _(e, g=g, mi=mi, nt=nt, u=u, gu=gu):
                        if u == 0:
                            wv(e, "f1_tern_%d_0" % g)
                            if g == 0:
                                wv(e, "f1r_all")
                                wv(e, "f1s_prev_none" if False else "f0_scales")
                        if gu >= 2:
                            wv(e, "f1_gg_ev_%d" % (gu - 2))
                        for kc in range(KC):
                            ins = e.matmul(bank(u % 2),
                                           tern_g[:, kc * 512 + mi * 128: kc * 512 + (mi + 1) * 128],
                                           qxT[:, kc * TP + 8 + nt * 512: kc * TP + 8 + (nt + 1) * 512],
                                           start=(kc == 0), stop=(kc == KC - 1))
                        inc(ins, sM, 1)
                    mark("f1_gp_%d" % gu, sM)

                    @blk.tensor
                    def _(e, g=g, mi=mi, nt=nt, u=u, gu=gu):
                        if u == 0:
                            wv(e, "f1_tern_%d_1" % g)
                        if gu >= 2:
                            wv(e, "f1_h_ev_%d" % (gu - 2))
                        for kc in range(KC):
                            ins = e.matmul(bank(2 + u % 2),
                                           tern_u[:, kc * 512 + mi * 128: kc * 512 + (mi + 1) * 128],
                                           qxT[:, kc * TP + 8 + nt * 512: kc * TP + 8 + (nt + 1) * 512],
                                           start=(kc == 0), stop=(kc == KC - 1))
                        inc(ins, sM, 1)
                    mark("f1_up_%d" % gu, sM)

                    @blk.vector
                    def _(e, u=u, nt=nt, gu=gu):
                        e = ES(e, sV)
                        wv(e, "f1_gp_%d" % gu)
                        ins = e.scalar_tensor_tensor(out=sg_t[:], in0=bank(u % 2), scalar=iswg[:, 0:1],
                                                     in1=ISX[:, nt * 512:(nt + 1) * 512],
                                                     op0=OP.mult, op1=OP.mult)
                    mark("f1_gg_%s" % lab, sV)
                    mark("f1_gg_ev_%d" % gu, sV)

                    @blk.scalar
                    def _(e, lab=lab):
                        e = ES(e, sS)
                        wv(e, "f1_gg_%s" % lab)
                        ins = e.activation(out=sg_t[:], in_=sg_t[:], func=AF.Silu)
                    mark("f1_sil_%s" % lab, sS)

                    @blk.vector
                    def _(e, nt=nt, lab=lab):
                        e = ES(e, sV)
                        wv(e, "f1_sil_%s" % lab)
                        ins = e.tensor_tensor(out=sgp_t[:], in0=sg_t[:],
                                              in1=ISXU[:, nt * 512:(nt + 1) * 512], op=OP.mult)
                    mark("f1_sgp_%s" % lab, sV)

                    @blk.vector
                    def _(e, g=g, mi=mi, nt=nt, u=u, gu=gu, lab=lab):
                        e = ES(e, sV)
                        wv(e, "f1_sgp_%s" % lab)
                        wv(e, "f1_up_%d" % gu)
                        if g > 0 and u <= 1:
                            wv(e, "f1_spill_%d" % (g - 1))
                        ins = e.tensor_tensor(out=hstage[:, mi * T + nt * 512: mi * T + (nt + 1) * 512],
                                              in0=bank(2 + u % 2), in1=sgp_t[:], op=OP.mult)
                    mark("f1_h_%s" % lab, sV)
                    mark("f1_h_ev_%d" % gu, sV)

                    @blk.vector
                    def _(e, g=g, mi=mi, nt=nt, gu=gu, lab=lab):
                        e = ES(e, sV)
                        wv(e, "f1_h_%s" % lab)
                        if gu >= 1:
                            wv(e, "f1_ssq_%d_%d" % ((gu - 1) // 2, (gu - 1) % 2))
                        hs = hstage[:, mi * T + nt * 512: mi * T + (nt + 1) * 512]
                        ins = e.tensor_tensor(out=hsq_t[:], in0=hs, in1=hs, op=OP.mult)
                    mark("f1_hsq_%s" % lab, sV)

                    @blk.tensor
                    def _(e, c=c, nt=nt, lab=lab):
                        wv(e, "f1_hsq_%s" % lab)
                        ins = e.matmul(psA[:, 3072 + nt * 512: 3072 + (nt + 1) * 512],
                                       ones_bf[:], hsq_t[:],
                                       start=(c == 0), stop=(c == IC - 1))
                        inc(ins, sM, 1)
                    mark("f1_ssq_%s" % lab, sM)

                    @blk.vector
                    def _(e, c=c, mi=mi, nt=nt, gu=gu, lab=lab):
                        e = ES(e, sV)
                        wv(e, "f1_h_%s" % lab)
                        hs = hstage[:, mi * T + nt * 512: mi * T + (nt + 1) * 512]
                        e.scalar_tensor_tensor(out=Mhi[:, nt * 512:(nt + 1) * 512], in0=hs,
                                               scalar=lnw_sb[:, c:c + 1],
                                               in1=Mhi[:, nt * 512:(nt + 1) * 512],
                                               op0=OP.mult, op1=OP.max)
                        ins = e.scalar_tensor_tensor(out=Mlo[:, nt * 512:(nt + 1) * 512], in0=hs,
                                                     scalar=lnw_sb[:, c:c + 1],
                                                     in1=Mlo[:, nt * 512:(nt + 1) * 512],
                                                     op0=OP.mult, op1=OP.min)
                    mark("f1_M_%s" % lab, sV)
            mark("f1_mm_%d_0" % g, sM)
            mark("f1_mm_%d_1" % g, sM)

            @blk.scalar
            def _(e, g=g):
                e = ES(e, sS)
                for mi in range(4):
                    for nt in range(2):
                        wv(e, "f1_hsq_%d_%d" % (g * 4 + mi, nt))
                        wv(e, "f1_M_%d_%d" % (g * 4 + mi, nt))
                ins = e.dma_start(out=hscr[g * 4:(g + 1) * 4].rearrange("k p c -> p k c"),
                                  in_=hstage[:].rearrange("p (k c) -> p k c", k=4))
                inc(ins, sTh, 16)
            mark("f1_spill_%d" % g, sTh)

        @blk.vector
        def _(e):
            e = ES(e, sV)
            wv(e, "f1_ssq_63_1")
            for mi in range(4):
                wv(e, "f1_M_%d_1" % (60 + mi))
            e.tensor_scalar(out=r_bc[:, 0:512], in0=psA[:, 3072:3584], scalar1=1.0 / I,
                            scalar2=1e-6, op0=OP.mult, op1=OP.add)
            ins = e.tensor_scalar(out=r_bc[:, 512:1024], in0=psA[:, 3584:4096], scalar1=1.0 / I,
                                  scalar2=1e-6, op0=OP.mult, op1=OP.add)
            e.tensor_scalar(out=Mlo[:], in0=Mlo[:], scalar1=-1.0, scalar2=None, op0=OP.mult)
            ins = e.tensor_tensor(out=Mhi[:], in0=Mhi[:], in1=Mlo[:], op=OP.max)
        mark("f1s_pre", sV)

        @blk.scalar
        def _(e):
            e = ES(e, sS)
            wv(e, "f1s_pre")
            e.activation(out=r_bc[:], in_=r_bc[:], func=AF.Ln)
            ins = e.activation(out=r_bc[:], in_=r_bc[:], func=AF.Exp, scale=c_nhalf[:, 0:1])
        mark("f1s_rbc", sS)

        @blk.tensor
        def _(e):
            wv(e, "f1s_pre")
            for b in range(8):
                if b >= 2:
                    e.wait_ge(sVm.h, b - 1)
                ins = e.transpose(psA[:, 2048 + (b % 2) * 512: 2048 + (b % 2) * 512 + 128],
                                  Mhi[:, b * 128:(b + 1) * 128], identf[:])
                inc(ins, sM, 1)
                mark("f1s_mt_%d" % b, sM)

        @blk.vector
        def _(e):
            e = ES(e, sV)
            for b in range(8):
                wv(e, "f1s_mt_%d" % b)
                ins = e.reduce_max(out=amax_tok[:, b:b + 1],
                                   in_=psA[:, 2048 + (b % 2) * 512: 2048 + (b % 2) * 512 + 128],
                                   axis=AX.X)
                ins2 = e.engine_nop()
                inc(ins2, sVm, 1)
        mark("f1s_am", sV)

        @blk.tensor
        def _(e):
            wv(e, "f1s_rbc")
            wv(e, "f1s_am")
            for b in range(8):
                ins = e.transpose(psA[:, 2048 + b * 128: 2048 + b * 128 + 128],
                                  r_bc[:, b * 128:(b + 1) * 128], identf[:])
                inc(ins, sM, 1)
                mark("f1s_rt_%d" % b, sM)

        @blk.vector
        def _(e):
            e = ES(e, sV)
            wv(e, "f1s_rt_%d" % 7)
            for b in range(8):
                ins = e.tensor_copy(out=rtok[:, b:b + 1],
                                    in_=psA[:, 2048 + b * 128: 2048 + b * 128 + 1])
            mark("f1s_rtok", sV)
            e.tensor_tensor(out=mrms[:], in0=rtok[:], in1=amax_tok[:], op=OP.mult)
            e.tensor_scalar(out=mrms[:], in0=mrms[:], scalar1=1e-5, scalar2=None, op0=OP.max)
            e.reciprocal(out=shtok[:], in_=mrms[:])
            e.tensor_scalar(out=shtok[:], in0=shtok[:], scalar1=127.0, scalar2=None, op0=OP.mult)
            e.tensor_scalar(out=ivtok[:], in0=mrms[:], scalar1=1.0 / 127.0,
                            scalar2=None, op0=OP.mult)
            ins = e.tensor_scalar(out=ivtok[:], in0=ivtok[:], scalar1=iswd[:, 0:1],
                                  scalar2=None, op0=OP.mult)
        mark("f1s_tok", sV)

        for b in range(8):
            for which, (colsrc, rowdst) in enumerate(((shtok, sh_row), (ivtok, iv_row))):
                lab = "%d_%d" % (b, which)

                @blk.vector
                def _(e, b=b, which=which, colsrc=colsrc, lab=lab):
                    e = ES(e, sV)
                    wv(e, "f1s_tok")
                    if b * 2 + which >= 2:
                        wv(e, "f1s_row_%d_%d" % ((b * 2 + which - 2) // 2, (b * 2 + which - 2) % 2))
                    ins = e.tensor_scalar(out=tmp128[:], in0=ones_f[:], scalar1=colsrc[:, b:b + 1],
                                          scalar2=None, op0=OP.mult)
                mark("f1s_bcc_%s" % lab, sV)

                @blk.tensor
                def _(e, b=b, which=which, lab=lab):
                    wv(e, "f1s_bcc_%s" % lab)
                    ins = e.transpose(psA[:, 2048 + ((b * 2 + which) % 2) * 512:
                                          2048 + ((b * 2 + which) % 2) * 512 + 128],
                                      tmp128[:], identf[:])
                    inc(ins, sM, 1)
                mark("f1s_ct_%s" % lab, sM)

                @blk.vector
                def _(e, b=b, which=which, rowdst=rowdst, lab=lab):
                    e = ES(e, sV)
                    wv(e, "f1s_ct_%s" % lab)
                    ins = e.tensor_copy(out=rowdst[0:1, b * 128:(b + 1) * 128],
                                        in_=psA[0:1, 2048 + ((b * 2 + which) % 2) * 512:
                                                2048 + ((b * 2 + which) % 2) * 512 + 128])
                mark("f1s_row_%s" % lab, sV)
        mark("f1s_rows", sV)

        @blk.tensor
        def _(e):
            wv(e, "f1s_rows")
            e.matmul(psA[:, 0:512], ones1[:], sh_row[0:1, 0:512], start=True, stop=True)
            e.matmul(psA[:, 512:1024], ones1[:], sh_row[0:1, 512:1024], start=True, stop=True)
            e.matmul(psA[:, 1024:1536], ones1[:], iv_row[0:1, 0:512], start=True, stop=True)
            ins = e.matmul(psA[:, 1536:2048], ones1[:], iv_row[0:1, 512:1024], start=True, stop=True)
            inc(ins, sM, 1)
        mark("f1s_bc", sM)

        @blk.vector
        def _(e):
            e = ES(e, sV)
            wv(e, "f1s_bc")
            e.tensor_tensor(out=RS[:, 0:512], in0=r_bc[:, 0:512], in1=psA[:, 0:512], op=OP.mult)
            e.tensor_tensor(out=RS[:, 512:1024], in0=r_bc[:, 512:1024], in1=psA[:, 512:1024], op=OP.mult)
            e.tensor_copy(out=INVD[:, 0:512], in_=psA[:, 1024:1536])
            ins = e.tensor_copy(out=INVD[:, 512:1024], in_=psA[:, 1536:2048])
        mark("f1s_done", sV)

        for half in range(2):
            c0 = half * 512
            for c in range(IC):
                @blk.scalar
                def _(e, half=half, c=c, c0=c0):
                    e = ES(e, sS)
                    if half == 0 and c < 2:
                        wv(e, "f1_spill_15")
                        wv(e, "f1s_done")
                    else:
                        lab = ("f2_q_%d_%d" % (half, c - 2)) if c >= 2 else \
                              ("f2_q_%d_%d" % (half - 1, IC - 2 + c))
                        wv(e, lab)
                        if half == 1 and c < 2:
                            wv(e, "f3_mm_0_%d" % (KC - 1))
                    ins = e.dma_start(out=hrb[:, (c % 2) * 512:(c % 2) * 512 + 512],
                                      in_=hscr[c][:, c0:c0 + 512])
                    inc(ins, sTr[c % 2], 16)
                mark("f2_rb_%d_%d" % (half, c), sTr[c % 2])

                @blk.vector
                def _(e, half=half, c=c, c0=c0):
                    e = ES(e, sV)
                    wv(e, "f2_rb_%d_%d" % (half, c))
                    buf = (hl_t, sgp_t)[c % 2]
                    ins = e.tensor_scalar(out=buf[:], in0=hrb[:, (c % 2) * 512:(c % 2) * 512 + 512],
                                          scalar1=lnw_sb[:, c:c + 1], scalar2=None, op0=OP.mult)
                    ins = e.tensor_tensor(out=buf[:], in0=buf[:], in1=RS[:, c0:c0 + 512], op=OP.mult)
                    ins = e.tensor_scalar(out=hq[:, c * 512:(c + 1) * 512], in0=buf[:],
                                          scalar1=CR, scalar2=CR, op0=OP.add, op1=OP.subtract)
                mark("f2_q_%d_%d" % (half, c), sV)
            mark("f2_done_%d" % half, sV)

            for b in range(4):
                @blk.sync
                def _(e, half=half, b=b):
                    if half > 0:
                        wv(e, "out_all_%d" % (half - 1))
                    else:
                        wv(e, "f1_tern_15_1")
                    tb = half * 4 + b
                    ins = e.dma_start(out=(ot0, ot1, ot2, ot3)[b][:],
                                      in_=xh[8 + tb * 128: 8 + (tb + 1) * 128, :])
                    inc(ins, sXx, 16)
                mark("f3_x_%d_%d" % (half, b), sXx)

            for m in range(KC):
                lab = "%d_%d" % (half, m)
                hm = half * KC + m

                def _wd_guards(e, h, hm=hm):
                    if hm > 0:
                        wv(e, "f3_tern_%d_%d_h%d" % ((hm - 1) // KC, (hm - 1) % KC, h))
                    else:
                        wv(e, "f2_done_0")

                @blk.sync
                def _(e, m=m, _wd_guards=_wd_guards):
                    _wd_guards(e, 0)
                    ins = e.dma_start(out=wdstage[:, 0:32 * 128].rearrange("p (k c) -> p k c", k=32),
                                      in_=wd_d[m][0:32].rearrange("k p c -> p k c"))
                    inc(ins, sXw, 16)
                mark("f3_wdma_%s" % lab, sXw)

                @blk.gpsimd
                def _(e, m=m, _wd_guards=_wd_guards):
                    _wd_guards(e, 1)
                    ins = e.dma_start(out=wdstage[:, 32 * 128:64 * 128].rearrange("p (k c) -> p k c", k=32),
                                      in_=wd_d[m][32:64].rearrange("k p c -> p k c"))
                    inc(ins, sXw2, 16)
                mark("f3_wdma2_%s" % lab, sXw2)

                @blk.scalar
                def _(e, lab=lab, hm=hm):
                    e = ES(e, sS)
                    wv(e, "f3_wdma_%s" % lab)
                    ins = e.activation(out=wdstage[:, 0:32 * 128], in_=wdstage[:, 0:32 * 128],
                                       func=AF.Identity, scale=swd[:, 0:1], bias=c_round[:, 0:1])
                    mark("f3_tern_s_%s_h0" % lab, sS)
                    wv(e, "f3_wdma2_%s" % lab)
                    ins = e.activation(out=wdstage[:, 32 * 128:64 * 128], in_=wdstage[:, 32 * 128:64 * 128],
                                       func=AF.Identity, scale=swd[:, 0:1], bias=c_round[:, 0:1])
                mark("f3_tern_s_%s_h1" % lab, sS)

                @blk.vector
                def _(e, lab=lab, hm=hm):
                    e = ES(e, sV)
                    wv(e, "f3_tern_s_%s_h0" % lab)
                    if hm > 0:
                        wv(e, "f3_mm_%d_%d" % ((hm - 1) // KC, (hm - 1) % KC))
                    e.tensor_scalar(out=wdstage[:, 0:32 * 128], in0=wdstage[:, 0:32 * 128],
                                    scalar1=CR, scalar2=1.0, op0=OP.subtract, op1=OP.min)
                    ins = e.tensor_scalar(out=tern_d[:, 0:32 * 128], in0=wdstage[:, 0:32 * 128],
                                          scalar1=-1.0, scalar2=None, op0=OP.max)
                    mark("f3_tern_%s_h0" % lab, sV)
                    wv(e, "f3_tern_s_%s_h1" % lab)
                    e.tensor_scalar(out=wdstage[:, 32 * 128:64 * 128], in0=wdstage[:, 32 * 128:64 * 128],
                                    scalar1=CR, scalar2=1.0, op0=OP.subtract, op1=OP.min)
                    ins = e.tensor_scalar(out=tern_d[:, 32 * 128:64 * 128], in0=wdstage[:, 32 * 128:64 * 128],
                                          scalar1=-1.0, scalar2=None, op0=OP.max)
                mark("f3_tern_%s_h1" % lab, sV)
                mark("f3_tern_%s" % lab, sV)

                @blk.scalar
                def _(e, half=half, m=m, hm=hm, c0=c0):
                    e = ES(e, sS)
                    if hm == 0:
                        wv(e, "a4_spill_3")
                        wv(e, "f1s_done")
                    if hm >= 2:
                        wv(e, "f3_fin_%d_%d" % ((hm - 2) // KC, (hm - 2) % KC))
                    ins = e.dma_start(out=aprb[:, (m % 2) * 512:(m % 2) * 512 + 512],
                                      in_=ascr[m][:, c0:c0 + 512])
                    inc(ins, sTap[m % 2], 16)
                mark("f3_ap_%s" % lab, sTap[m % 2])

                @blk.tensor
                def _(e, half=half, m=m, lab=lab, hm=hm):
                    wv(e, "f3_tern_%s" % lab)
                    wv(e, "f2_done_%d" % half)
                    if hm >= 2:
                        wv(e, "f3_ev_%d_%d" % ((hm - 2) // KC, (hm - 2) % KC))
                    for kc in range(IC):
                        ins = e.matmul(bank(m % 2), tern_d[:, kc * 128:(kc + 1) * 128],
                                       hq[:, kc * 512:(kc + 1) * 512],
                                       start=(kc == 0), stop=(kc == IC - 1))
                    inc(ins, sM, 1)
                mark("f3_mm_%s" % lab, sM)

                @blk.vector
                def _(e, half=half, m=m, lab=lab, c0=c0):
                    e = ES(e, sV)
                    wv(e, "f3_mm_%s" % lab)
                    wv(e, "f3_ap_%s" % lab)
                    e.tensor_tensor(out=fin_t[:], in0=bank(m % 2), in1=INVD[:, c0:c0 + 512], op=OP.mult)
                    ins = e.tensor_tensor(out=fin_t[:], in0=fin_t[:],
                                          in1=aprb[:, (m % 2) * 512:(m % 2) * 512 + 512], op=OP.add)
                mark("f3_fin_%s" % lab, sV)
                mark("f3_ev_%s" % lab, sV)

                @blk.tensor
                def _(e, half=half, m=m, lab=lab, hm=hm):
                    wv(e, "f3_fin_%s" % lab)
                    for b in range(4):
                        gidx = hm * 4 + b
                        if gidx >= 2:
                            e.wait_ge(sVt.h, gidx - 1)
                        ins = e.transpose(psA[:, 2048 + (gidx % 2) * 512: 2048 + (gidx % 2) * 512 + 128],
                                          fin_t[:, b * 128:(b + 1) * 128], identf[:])
                        inc(ins, sM, 1)
                        mark("f3_tt_%d_%d_%d" % (half, m, b), sM)

                @blk.vector
                def _(e, half=half, m=m, hm=hm):
                    e = ES(e, sV)
                    for b in range(4):
                        gidx = hm * 4 + b
                        wv(e, "f3_tt_%d_%d_%d" % (half, m, b))
                        wv(e, "f3_x_%d_%d" % (half, 3))
                        ot = (ot0, ot1, ot2, ot3)[b]
                        ins = e.tensor_tensor(out=ot[:, m * 128:(m + 1) * 128],
                                              in0=psA[:, 2048 + (gidx % 2) * 512:
                                                      2048 + (gidx % 2) * 512 + 128],
                                              in1=ot[:, m * 128:(m + 1) * 128], op=OP.add)
                        ins2 = e.engine_nop()
                        inc(ins2, sVt, 1)
                        mark("f3_ta_%d_%d_%d" % (half, m, b), sV)

            for b in range(4):
                @blk.scalar
                def _(e, half=half, b=b):
                    e = ES(e, sS)
                    for m in range(KC):
                        wv(e, "f3_ta_%d_%d_%d" % (half, m, b))
                    tb = half * 4 + b
                    ins = e.dma_start(out=y[tb * 128:(tb + 1) * 128, :],
                                      in_=(ot0, ot1, ot2, ot3)[b][:])
                    inc(ins, sO, 16)
                mark("out_%d_%d" % (half, b), sO)
            mark("out_all_%d" % half, sO)

        @blk.scalar
        def _(e):
            e = ES(e, sS)
            e.wait_ge(sO.h, sO.n)

    return nc


_NC_CACHE = None


def _get_nc():
    global _NC_CACHE
    if _NC_CACHE is None:
        _NC_CACHE = build_nc()
    return _NC_CACHE


def _host_prep(inputs):
    x = np.ascontiguousarray(np.asarray(inputs["L0_attn_x"], dtype=np.float32))
    qkv_w = np.asarray(inputs["L0_attn_qkv_w"], dtype=np.float32)
    proj_w = np.asarray(inputs["L0_attn_proj_w"], dtype=np.float32)
    proj_b = np.asarray(inputs["L0_attn_proj_b"], dtype=np.float32)
    w_gate = np.asarray(inputs["L0_ffn_w_gate"], dtype=np.float32)
    w_up = np.asarray(inputs["L0_ffn_w_up"], dtype=np.float32)
    w_down = np.asarray(inputs["L0_ffn_w_down"], dtype=np.float32)
    lnw = np.asarray(inputs["L0_ffn_ln_weight"], dtype=np.float32)

    wqkv = np.ascontiguousarray(
        qkv_w.reshape(3, 16, 128, D).transpose(1, 3, 0, 2).reshape(NPAIR, KC, 128, 384)
        .transpose(0, 2, 1, 3).reshape(NPAIR, 128, KC * 384))
    wproj = np.ascontiguousarray(
        proj_w.T.reshape(KC, 128, 4, 512).transpose(2, 0, 1, 3)
        .transpose(0, 2, 1, 3).reshape(4, 128, KC * 512))
    wg_a = np.ascontiguousarray(w_gate.T.reshape(KC, 128, 16, 512).transpose(2, 0, 1, 3))
    wu_a = np.ascontiguousarray(w_up.T.reshape(KC, 128, 16, 512).transpose(2, 0, 1, 3))
    wd_a = np.ascontiguousarray(w_down.T.reshape(IC, 128, 16, 128).transpose(2, 0, 1, 3))
    pbh = np.ascontiguousarray(proj_b.reshape(16, 128).T)
    lnwh = np.ascontiguousarray(lnw.reshape(64, 128).T)

    pp = np.arange(128)[:, None]
    ff = np.arange(SLOTW)[None, :]
    rel = ff - pp - 8
    band = (rel >= -8) & (rel < 8)
    mband = np.where(band, (8.0 - np.abs(rel) / 2.0) / 8.0, 0.0).astype(np.float32)
    mmulh = np.zeros((128, TSBW), np.float32)
    for s in range(8):
        mmulh[:, s * SLOTW:(s + 1) * SLOTW] = mband
    mmulh = mmulh.astype(ml_dtypes.bfloat16)

    in_maps = []
    for cidx in range(8):
        b, q0 = cidx // 4, (cidx % 4) * T
        xhc = np.zeros((TP, D), np.float32)
        klo = q0 - 8
        lo = max(klo, 0)
        hi = min(klo + TP, SEQ)
        xhc[lo - klo: hi - klo] = x[b, lo:hi]

        maddh = np.full((2, 128, TSBW), NEG, np.float32)
        for half in range(2):
            for jj in range(4):
                j = half * 4 + jj
                ig = q0 + j * 128 + pp
                jg = klo + j * 128 + ff
                valid = band & (jg >= 0) & (jg < SEQ)
                Aa = np.where(valid, 0.0, NEG).astype(np.float32)
                for h in range(2):
                    s = jj * 2 + h
                    maddh[half, :, s * SLOTW:(s + 1) * SLOTW] = Aa
            maddh[half, :, 8 * SLOTW] = 0.0
        maddh = maddh.astype(ml_dtypes.bfloat16)

        in_maps.append({
            "xh": xhc, "wqkv": wqkv, "wproj": wproj, "wg": wg_a, "wu": wu_a,
            "wd": wd_a, "pbh": pbh, "lnwh": lnwh, "mmulh": mmulh, "maddh": maddh,
        })
    return in_maps


def kernel(**inputs):
    in_maps = _host_prep(inputs)
    nc = _get_nc()
    res = run_bass_kernel_spmd(nc, in_maps, core_ids=list(range(8)))
    out = np.stack([r["y"] for r in res.results], axis=0)
    return out.reshape(2, SEQ, D)


# version 07
# speedup vs baseline: 1.5568x; 1.0486x over previous
import numpy as np
import ml_dtypes
import concourse.bass as bass
import concourse.mybir as mybir
from concourse.bass_utils import run_bass_kernel_spmd

F32 = mybir.dt.float32
BF16 = mybir.dt.bfloat16
AF = mybir.ActivationFunctionType
OP = mybir.AluOpType
AX = mybir.AxisListType

D = 2048
KC = 16
T = 1024
TP = 1152
TH = 1040
NT = 9
I = 8192
IC = 64
NPAIR = 16
SEQ = 4096
CR = float(1.5 * 2.0 ** 23)
NEG = -1.0e30
SLOTW = 144
TSBW = 9 * SLOTW
QKV_NT = [(0, 352), (352, 352), (704, 336)]
QKV_NT_V = [(0, 384), (384, 384), (768, 384)]


class Sem:
    def __init__(self, nc, name):
        self.h = nc.alloc_semaphore(name)
        self.n = 0


_COMPUTE_OPS = {
    "tensor_tensor", "tensor_scalar", "tensor_copy", "tensor_add", "tensor_sub",
    "tensor_mul", "tensor_max", "reduce_sum", "reduce_max", "tensor_reduce",
    "reciprocal", "activation", "copy", "mul", "add", "square", "sqrt", "sign",
    "memset", "memzero", "iota", "scalar_tensor_tensor", "tensor_single_scalar",
    "tensor_scalar_add", "tensor_scalar_sub", "tensor_scalar_mul",
    "tensor_scalar_max", "tensor_scalar_min", "select", "bn_stats", "bn_aggr",
}


class ES:

    def __init__(self, e, sem):
        self._e = e
        self._s = sem

    def __getattr__(self, name):
        fn = getattr(self._e, name)
        if name in _COMPUTE_OPS:
            s = self._s
            e = self._e

            def wrapped(*a, **k):
                if s.n > 0:
                    e.wait_ge(s.h, s.n)
                ins = fn(*a, **k)
                s.n += 1
                ins.then_inc(s.h, 1)
                return ins
            return wrapped
        if name == "dma_start":
            s = self._s
            e = self._e

            def wrapped_dma(*a, **k):
                if s.n > 0:
                    e.wait_ge(s.h, s.n)
                return fn(*a, **k)
            return wrapped_dma
        return fn


def build_nc():
    nc = bass.Bass("TRN2", target_bir_lowering=False, debug=False, num_swdge_queues=4)

    xh = nc.dram_tensor("xh", [TP, D], F32, kind="ExternalInput").ap()
    wqkv = nc.dram_tensor("wqkv", [NPAIR, 128, KC * 384], F32, kind="ExternalInput").ap()
    wproj = nc.dram_tensor("wproj", [4, 128, KC * 512], F32, kind="ExternalInput").ap()
    wg_d = nc.dram_tensor("wg", [16, KC, 128, 512], F32, kind="ExternalInput").ap()
    wu_d = nc.dram_tensor("wu", [16, KC, 128, 512], F32, kind="ExternalInput").ap()
    wd_d = nc.dram_tensor("wd", [16, IC, 128, 128], F32, kind="ExternalInput").ap()
    pbh = nc.dram_tensor("pbh", [128, 16], F32, kind="ExternalInput").ap()
    lnwh = nc.dram_tensor("lnwh", [128, 64], F32, kind="ExternalInput").ap()
    mmulh = nc.dram_tensor("mmulh", [128, TSBW], BF16, kind="ExternalInput").ap()
    maddh = nc.dram_tensor("maddh", [2, 128, TSBW], BF16, kind="ExternalInput").ap()

    y = nc.dram_tensor("y", [T, D], F32, kind="ExternalOutput").ap()
    hscr = nc.dram_tensor("hscr", [IC, 128, T], F32, kind="Internal").ap()
    ascr = nc.dram_tensor("ascr", [KC, 128, T], F32, kind="Internal").ap()
    atscr = nc.dram_tensor("atscr", [KC, 128, T], BF16, kind="Internal").ap()

    A = nc.alloc_sbuf_tensor_at

    p = 16512
    def pal(name, shape, dtype):
        nonlocal p
        p = (p + 63) // 64 * 64
        t = A(name, shape, dtype, offset=p)
        esz = {BF16: 2, F32: 4}[dtype]
        p += int(np.prod(shape[1:])) * esz
        return t

    identf = pal("identf", [128, 128], F32)
    iof = pal("iof", [128, 128], F32)
    iop = pal("iop", [128, 1], F32)
    ones_bf = pal("ones_bf", [128, 128], BF16)
    ones_f = pal("ones_f", [128, 128], F32)
    ones1 = pal("ones1", [1, 128], F32)
    tmp128 = pal("tmp128", [128, 128], F32)
    c_round = pal("c_round", [128, 1], F32)
    c_nhalf = pal("c_nhalf", [128, 1], F32)
    mmul = pal("mmul", [128, TSBW], BF16)
    madd = pal("madd", [128, 2 * TSBW], BF16)
    pb_sb = pal("pb_sb", [128, 16], F32)
    lnw_sb = pal("lnw_sb", [128, 64], F32)
    partials = pal("partials", [128, 48], F32)
    swg = pal("swg", [128, 1], F32); iswg = pal("iswg", [128, 1], F32)
    swu = pal("swu", [128, 1], F32); iswu = pal("iswu", [128, 1], F32)
    swd = pal("swd", [128, 1], F32); iswd = pal("iswd", [128, 1], F32)
    ln_mean = pal("ln_mean", [128, NT], F32)
    ln_rstd = pal("ln_rstd", [128, NT], F32)
    ln_srow = pal("ln_srow", [128, NT], F32)
    isx_row = pal("isx_row", [1, TP], F32)
    ISX = pal("ISX", [128, T], F32)
    Mhi = pal("Mhi", [128, T], F32)
    Mlo = pal("Mlo", [128, T], F32)
    r_bc = pal("r_bc", [128, T], F32)
    RS = pal("RS", [128, T], F32)
    INVD = pal("INVD", [128, T], F32)
    sh_row = pal("sh_row", [1, T], F32)
    iv_row = pal("iv_row", [1, T], F32)
    amax_tok = pal("amax_tok", [128, 8], F32)
    rtok = pal("rtok", [128, 8], F32)
    mrms = pal("mrms", [128, 8], F32)
    shtok = pal("shtok", [128, 8], F32)
    ivtok = pal("ivtok", [128, 8], F32)
    negmax = pal("negmax", [128, 1], F32)
    rsum = pal("rsum", [128, 9], F32)
    rinv = pal("rinv", [128, 9], F32)
    tv1 = pal("tv1", [128, 8], F32)
    tv2 = pal("tv2", [128, 8], F32)
    assert p < 66512, p

    xnT = A("xnT", [128, KC * TP], BF16, offset=66560)
    qxT = A("qxT", [128, KC * TP], BF16, offset=66560)
    hq = A("hq", [128, IC * 512], BF16, offset=66560)
    ot3 = A("ot3", [128, D], F32, offset=132096)
    xtok = A("xtok", [128, D], F32, offset=103424)
    xn_t = A("xn_t", [128, D], F32, offset=111744)
    xnb_t = A("xnb_t", [128, D], BF16, offset=119936)
    sq_t = A("sq_t", [128, D], BF16, offset=124032)
    T_sb = A("T_sb", [128, TSBW], F32, offset=103424)
    E_bf = A("E_bf", [128, TSBW], BF16, offset=108672)
    P_bf = A("P_bf", [128, TSBW], BF16, offset=111296)
    PT_a = A("PT_a", [128, 8 * 128], BF16, offset=113920)
    PT_b = A("PT_b", [128, 8 * 128], BF16, offset=115968)
    ath0 = A("ath0", [64, 2 * T], BF16, offset=118016)
    ath1 = A("ath1", [64, 2 * T], BF16, offset=130304)
    wstage = A("wstage", [128, KC * 512], F32, offset=103424)
    wq_bf0 = A("wq_bf0", [128, KC * 384], BF16, offset=140352)
    wq_bf1 = A("wq_bf1", [128, KC * 384], BF16, offset=152640)
    qm0 = A("qm0", [128, T], BF16, offset=164928)
    qm1 = A("qm1", [128, T], BF16, offset=167040)
    kT_p = A("kT_p", [128, TH], BF16, offset=128192)
    vT_p = A("vT_p", [128, TP], BF16, offset=169152)
    atrb = A("atrb", [128, 2 * T], BF16, offset=140352)
    tern_g = A("tern_g", [128, KC * 512], BF16, offset=140352)
    tern_u = A("tern_u", [128, KC * 512], BF16, offset=156736)
    wdstage = A("wdstage", [128, IC * 128], F32, offset=140352)
    v_tok = [A("v_tok0", [128, NT * 128], BF16, offset=173120),
             A("v_tok1", [128, NT * 128], BF16, offset=124032)]
    f0st = A("f0st", [128, 2 * 2048], F32, offset=175552)
    astg = A("astg", [128, 4 * T], F32, offset=173120)
    xtok2 = A("xtok2", [128, D], F32, offset=173120)
    xn2 = A("xn2", [128, D], F32, offset=181312)
    qx2 = A("qx2", [128, D], BF16, offset=189568)
    hstage = A("hstage", [128, 4 * T], F32, offset=173120)
    sg_t = A("sg_t", [128, 512], F32, offset=189568)
    sgp_t = A("sgp_t", [128, 512], F32, offset=191616)
    hl_t = A("hl_t", [128, 512], F32, offset=193728)
    hsq_t = A("hsq_t", [128, 512], BF16, offset=195840)
    hrb = A("hrb", [128, 2 * 512], F32, offset=189568)
    tern_d = A("tern_d", [128, IC * 128], BF16, offset=173120)
    aprb = A("aprb", [128, 2 * 512], F32, offset=196928)
    ISXU = A("ISXU", [128, T], F32, offset=196928)
    hsq_t2 = A("hsq_t2", [128, 512], BF16, offset=201024)
    fin_t = A("fin_t", [128, 512], F32, offset=201024)
    ot2 = A("ot2", [128, D], F32, offset=203136)
    pw_bf = A("pw_bf", [128, KC * 512], BF16, offset=203136)
    ot0 = A("ot0", [128, D], F32, offset=211328)
    ot1 = A("ot1", [128, D], F32, offset=219520)

    psA = nc.alloc_psum_tensor("psA", [128, 4096], F32)

    def bank(i, w=512):
        return psA[:, i * 512:i * 512 + w]

    sV = Sem(nc, "sV")
    sS = Sem(nc, "sS")
    sM = Sem(nc, "sM")
    sP = Sem(nc, "sP")
    sO = Sem(nc, "sO")
    sF0 = Sem(nc, "sF0")
    sVt = Sem(nc, "sVt")
    sVm = Sem(nc, "sVm")
    sXm = Sem(nc, "sXm")
    sXx = Sem(nc, "sXx")
    sXw = Sem(nc, "sXw")
    sXw2 = Sem(nc, "sXw2")
    sXa = [Sem(nc, "sXa0"), Sem(nc, "sXa1")]
    sTp = Sem(nc, "sTp")
    sTt = Sem(nc, "sTt")
    sTv = Sem(nc, "sTv")
    sTat = [Sem(nc, "sTat0"), Sem(nc, "sTat1")]
    sTa4 = Sem(nc, "sTa4")
    sTh = Sem(nc, "sTh")
    sTr = [Sem(nc, "sTr0"), Sem(nc, "sTr1")]
    sTap = [Sem(nc, "sTap0"), Sem(nc, "sTap1")]
    sGw = [Sem(nc, "sGw0"), Sem(nc, "sGw1")]
    sGf = [Sem(nc, "sGf0"), Sem(nc, "sGf1")]
    sGp = Sem(nc, "sGp")

    ev = {}

    def mark(label, s):
        ev[label] = (s, s.n)

    def wv(eng, label):
        s, n = ev[label]
        if n > 0:
            eng.wait_ge(s.h, n)

    def inc(ins, s, k=1):
        s.n += k
        ins.then_inc(s.h, k)

    f0_list = []
    for wi in range(3):
        for g in range(16):
            for q in range(4):
                f0_list.append((wi, g, q))

    def f0_dma(e, qi):
        wi, g, q = f0_list[qi]
        if qi >= 2:
            e.wait_ge(sF0.h, qi - 1)
        wt = (wg_d, wu_d, wd_d)[wi]
        nk = 4 if wi < 2 else 16
        src = wt[g, q * nk:(q + 1) * nk, :, :]
        dst = f0st[:, (qi % 2) * 2048:(qi % 2) * 2048 + 2048].rearrange("p (k c) -> p k c", k=nk)
        ins = e.dma_start(out=dst, in_=src.rearrange("k p c -> p k c"))
        inc(ins, sGf[qi % 2], 16)
        mark("f0_dma_%d" % qi, sGf[qi % 2])

    def f0_abs_s(e, qi):
        wv(e, "f0_dma_%d" % qi)
        if qi >= 8:
            e.wait_ge(sF0.h, qi - 7)
        sl = f0st[:, (qi % 2) * 2048:(qi % 2) * 2048 + 2048]
        ins = e.activation(out=sl, in_=sl, func=AF.Abs,
                           accum_out=tv2[:, qi % 8: qi % 8 + 1])
        mark("f0_abs_%d" % qi, sS)

    def f0_red(e, qi):
        wi, g, q = f0_list[qi]
        wv(e, "f0_abs_%d" % qi)
        col = wi * 16 + g
        if q == 0:
            ins = e.tensor_copy(out=partials[:, col:col + 1], in_=tv2[:, qi % 8: qi % 8 + 1])
        else:
            ins = e.tensor_add(out=partials[:, col:col + 1], in0=partials[:, col:col + 1],
                               in1=tv2[:, qi % 8: qi % 8 + 1])
        mark("f0_addv_%d" % qi, sV)
        ins2 = e.engine_nop()
        inc(ins2, sF0, 1)

    with nc.Block() as blk:
        @blk.sync
        def _(e):
            ins = e.dma_start(out=mmul[:], in_=mmulh[:, :]); inc(ins, sXm, 16)
            ins = e.dma_start(out=madd[:].rearrange("p (a c) -> p a c", a=2),
                              in_=maddh.rearrange("a p c -> p a c")); inc(ins, sXm, 16)
            ins = e.dma_start(out=pb_sb[:], in_=pbh[:, :]); inc(ins, sXm, 16)
            ins = e.dma_start(out=lnw_sb[:], in_=lnwh[:, :]); inc(ins, sXm, 16)
        mark("init_dma", sXm)

        @blk.gpsimd
        def _(e):
            e = ES(e, sP)
            ins = e.iota(iof[:], [[1, 128]], channel_multiplier=0, allow_small_or_imprecise_dtypes=True)
            ins = e.iota(iop[:], [[0, 1]], channel_multiplier=1, allow_small_or_imprecise_dtypes=True)
            e.tensor_scalar(out=identf[:], in0=iof[:], scalar1=iop[:, 0:1], scalar2=None, op0=OP.is_equal)
            e.memset(ones_bf[:], 1.0)
            e.memset(ones_f[:], 1.0)
            e.memset(ones1[:], 1.0)
            e.memset(c_round[:], CR)
            e.memset(c_nhalf[:], -0.5)
            e.memset(qm0[:], 0.0)
            e.memset(qm1[:], 0.0)
            e.memset(Mhi[:], -1.0e30)
            e.memset(Mlo[:], 1.0e30)
            e.memset(v_tok[0][:], 0.0)
            ins = e.memset(v_tok[1][:], 0.0)
        mark("init_gp", sP)

        for t in range(NT):
            @blk.sync
            def _(e, t=t):
                if t > 0:
                    wv(e, "a1_done_%d" % (t - 1))
                ins = e.dma_start(out=xtok[:], in_=xh[t * 128:(t + 1) * 128, :])
                inc(ins, sXx, 16)
            mark("a1_x_%d" % t, sXx)

            @blk.vector
            def _(e, t=t):
                e = ES(e, sV)
                if t == 0:
                    wv(e, "init_gp")
                wv(e, "a1_x_%d" % t)
                e.reduce_sum(out=tv1[:, 0:1], in_=xtok[:], axis=AX.X)
                ins = e.tensor_scalar(out=ln_mean[:, t:t + 1], in0=tv1[:, 0:1],
                                      scalar1=1.0 / D, scalar2=None, op0=OP.mult)
            mark("a1_mean_%d" % t, sV)

            @blk.scalar
            def _(e, t=t):
                e = ES(e, sS)
                if t == 0:
                    wv(e, "init_gp")
                wv(e, "a1_x_%d" % t)
                ins = e.activation(out=sq_t[:], in_=xtok[:], func=AF.Square, accum_out=tv2[:, 0:1])
            mark("a1_ssq_%d" % t, sS)

            @blk.vector
            def _(e, t=t):
                e = ES(e, sV)
                wv(e, "a1_ssq_%d" % t)
                wv(e, "a1_mean_%d" % t)
                e.tensor_scalar(out=tv2[:, 1:2], in0=tv2[:, 0:1], scalar1=1.0 / D, scalar2=1e-5,
                                op0=OP.mult, op1=OP.add)
                e.tensor_mul(out=tv2[:, 2:3], in0=ln_mean[:, t:t + 1], in1=ln_mean[:, t:t + 1])
                ins = e.tensor_sub(out=tv2[:, 3:4], in0=tv2[:, 1:2], in1=tv2[:, 2:3])
            mark("a1_var_%d" % t, sV)

            @blk.scalar
            def _(e, t=t):
                e = ES(e, sS)
                wv(e, "a1_var_%d" % t)
                e.activation(out=tv2[:, 4:5], in_=tv2[:, 3:4], func=AF.Ln)
                ins = e.activation(out=ln_rstd[:, t:t + 1], in_=tv2[:, 4:5], func=AF.Exp,
                                   scale=c_nhalf[:, 0:1])
            mark("a1_rstd_%d" % t, sS)

            @blk.vector
            def _(e, t=t):
                e = ES(e, sV)
                wv(e, "a1_rstd_%d" % t)
                ins = e.tensor_scalar(out=xn_t[:], in0=xtok[:], scalar1=ln_mean[:, t:t + 1],
                                      scalar2=ln_rstd[:, t:t + 1], op0=OP.subtract, op1=OP.mult)
                e.reduce_max(out=tv1[:, 1:2], in_=xn_t[:], axis=AX.X, apply_absolute_value=True)
                e.tensor_scalar(out=tv1[:, 2:3], in0=tv1[:, 1:2], scalar1=1e-5, scalar2=None, op0=OP.max)
                e.reciprocal(out=tv1[:, 3:4], in_=tv1[:, 2:3])
                e.tensor_scalar(out=ln_srow[:, t:t + 1], in0=tv1[:, 3:4], scalar1=127.0,
                                scalar2=None, op0=OP.mult)
                ins = e.tensor_scalar(out=tmp128[:], in0=ones_f[:],
                                      scalar1=tv1[:, 2:3], scalar2=1.0 / 127.0, op0=OP.mult, op1=OP.mult)
            mark("a1_xn_%d" % t, sV)

            @blk.tensor
            def _(e, t=t):
                wv(e, "a1_xn_%d" % t)
                if t == 0:
                    wv(e, "init_gp")
                if t > 0:
                    wv(e, "a1_isxrow_%d" % (t - 1))
                ins = e.transpose(psA[:, 3584:3584 + 128], tmp128[:], identf[:])
                inc(ins, sM, 1)
            mark("a1_isxT_%d" % t, sM)

            @blk.vector
            def _(e, t=t):
                e = ES(e, sV)
                wv(e, "a1_isxT_%d" % t)
                ins = e.tensor_copy(out=isx_row[0:1, t * 128:(t + 1) * 128],
                                    in_=psA[0:1, 3584:3584 + 128])
            mark("a1_isxrow_%d" % t, sV)

            @blk.scalar
            def _(e, t=t):
                e = ES(e, sS)
                wv(e, "a1_xn_%d" % t)
                if t > 0:
                    wv(e, "a1_done_%d" % (t - 1))
                ins = e.copy(out=xnb_t[:], in_=xn_t[:])
            mark("a1_cast_%d" % t, sS)

            @blk.scalar
            def _(e, t=t):
                e = ES(e, sS)
                wv(e, "a1_cast_%d" % t)
                for kc in range(KC):
                    ins = e.dma_start(out=xnT[:, kc * TP + t * 128: kc * TP + (t + 1) * 128],
                                      in_=xnb_t[:, kc * 128:(kc + 1) * 128], transpose=True)
                    inc(ins, sTt, 16)
            mark("a1_done_%d" % t, sTt)
        mark("a1_all", sTt)

        @blk.tensor
        def _(e):
            wv(e, "a1_isxrow_%d" % (NT - 1))
            e.matmul(psA[:, 3072:3584], ones1[:], isx_row[0:1, 8:520], start=True, stop=True)
            ins = e.matmul(psA[:, 3584:4096], ones1[:], isx_row[0:1, 520:1032], start=True, stop=True)
            inc(ins, sM, 1)
        mark("isx_mm", sM)

        @blk.vector
        def _(e):
            e = ES(e, sV)
            wv(e, "isx_mm")
            e.tensor_copy(out=ISX[:, 0:512], in_=psA[:, 3072:3584])
            ins = e.tensor_copy(out=ISX[:, 512:1024], in_=psA[:, 3584:4096])
        mark("isx_done", sV)

        f0_emitted = 0
        f0_s_emitted = 0

        def emit_wq(pr):
            wbuf = (wq_bf0, wq_bf1)[pr % 2]

            @blk.gpsimd
            def _(e, pr=pr, wbuf=wbuf):
                e = ES(e, sP)
                if pr >= 2:
                    wv(e, "qkv_mm_%d" % (pr - 2))
                ins = e.dma_start(out=wbuf[:], in_=wqkv[pr])
                inc(ins, sGw[pr % 2], 16)
            mark("wq_%d" % pr, sGw[pr % 2])

        emit_wq(0)
        def emit_qkv_pair(pr):
            wbuf = (wq_bf0, wq_bf1)[pr % 2]
            if pr + 1 < NPAIR:
                emit_wq(pr + 1)

            @blk.gpsimd
            def _(e, pr=pr):
                e = ES(e, sP)
                for k in range(12):
                    qi = pr * 12 + k
                    if qi < 192:
                        f0_dma(e, qi)

            for mi in range(3):
                for ti, (n0, nn) in enumerate(QKV_NT if mi < 2 else QKV_NT_V):
                    u = mi * 3 + ti

                    @blk.tensor
                    def _(e, pr=pr, mi=mi, ti=ti, n0=n0, nn=nn, u=u, wbuf=wbuf):
                        if u == 0:
                            wv(e, "wq_%d" % pr)
                            if pr == 0:
                                wv(e, "a1_all")
                                wv(e, "isx_done")
                        gu = pr * 9 + u
                        if gu >= 2:
                            wv(e, "qkv_ev_%d" % (gu - 2))
                        for kc in range(KC):
                            ins = e.matmul(bank(gu % 2, nn),
                                           wbuf[:, kc * 384 + mi * 128: kc * 384 + (mi + 1) * 128],
                                           xnT[:, kc * TP + n0: kc * TP + n0 + nn],
                                           start=(kc == 0), stop=(kc == KC - 1))
                        inc(ins, sM, 1)
                    mark("qkv_ps_%d_%d" % (pr, u), sM)

                    @blk.vector
                    def _(e, pr=pr, mi=mi, n0=n0, nn=nn, u=u):
                        e = ES(e, sV)
                        gu = pr * 9 + u
                        wv(e, "qkv_ps_%d_%d" % (pr, u))
                        if mi == 0:
                            s0 = max(n0, 8) - n0
                            s1 = min(n0 + nn, 8 + T) - n0
                            d0 = n0 + s0 - 8
                            ins = e.tensor_scalar(out=qm0[0:64, d0:d0 + (s1 - s0)],
                                                  in0=bank(gu % 2, nn)[0:64, s0:s1],
                                                  scalar1=0.125, scalar2=None, op0=OP.mult)
                            ins = e.tensor_scalar(out=qm1[64:128, d0:d0 + (s1 - s0)],
                                                  in0=bank(gu % 2, nn)[64:128, s0:s1],
                                                  scalar1=0.125, scalar2=None, op0=OP.mult)
                        else:
                            if mi == 2 and ti == 0 and pr > 0:
                                wv(e, "vtok_%d" % (pr - 1))
                            dst = (None, kT_p, vT_p)[mi]
                            ins = e.tensor_copy(out=dst[:, n0:n0 + nn], in_=bank(gu % 2, nn))
                    mark("qkv_ev_%d" % (pr * 9 + u), sV)
            mark("qkv_mm_%d" % pr, sM)
            mark("qkv_ev_done_%d" % pr, sV)

            @blk.scalar
            def _(e, pr=pr):
                e = ES(e, sS)
                wv(e, "qkv_ev_done_%d" % pr)
                if pr >= 2:
                    wv(e, "a3_pv_%d_1" % (pr - 2))
                for tt in range(NT):
                    ins = e.dma_start(out=v_tok[pr % 2][:, tt * 128: tt * 128 + 128],
                                      in_=vT_p[:, tt * 128: tt * 128 + 128], transpose=True)
                    inc(ins, sTv, 16)
            mark("vtok_%d" % pr, sTv)


        def emit_scores_half(pr, half):
            if True:
                hh = pr * 2 + half

                @blk.tensor
                def _(e, pr=pr, half=half, hh=hh):
                    if hh > 0:
                        wv(e, "a3_Tsb_%d" % (hh - 1))
                    for jj in range(4):
                        j = half * 4 + jj
                        for h in range(2):
                            slot = jj * 2 + h
                            off = 1024 + (slot // 3) * 512 + (slot % 3) * 144
                            qm = (qm0, qm1)[h]
                            ins = e.matmul(psA[:, off:off + 144],
                                           qm[:, j * 128: (j + 1) * 128],
                                           kT_p[:, j * 128: j * 128 + 144],
                                           start=True, stop=True)
                    off8 = 1024 + 2 * 512 + 2 * 144
                    ins = e.matmul(psA[:, off8:off8 + 144],
                                   qm0[:, half * 512: half * 512 + 128],
                                   kT_p[:, half * 512: half * 512 + 144],
                                   start=True, stop=True)
                    inc(ins, sM, 1)
                mark("a3_sc_%d" % hh, sM)

                @blk.vector
                def _(e, pr=pr, half=half, hh=hh):
                    e = ES(e, sV)
                    if hh == 0:
                        wv(e, "init_dma")
                    else:
                        wv(e, "a3_exp_%d" % (hh - 1))
                    wv(e, "a3_sc_%d" % hh)
                    psc = psA[:, 1024:1024 + 1536].rearrange("p (b s) -> p b s", b=3)[:, :, 0:432]
                    ins = e.tensor_tensor(out=T_sb[:].rearrange("p (b c) -> p b c", b=3), in0=psc,
                                          in1=mmul[:].rearrange("p (b c) -> p b c", b=3), op=OP.mult)
                    mark("a3_Tsb_%d" % hh, sV)
                    e.tensor_tensor(out=T_sb[:], in0=T_sb[:],
                                    in1=madd[:, half * TSBW:(half + 1) * TSBW], op=OP.add)
                    ins = e.reduce_max(out=negmax[:], in_=T_sb[:], axis=AX.X, negate=True)
                mark("a3_mx_%d" % hh, sV)

                @blk.scalar
                def _(e, hh=hh):
                    e = ES(e, sS)
                    wv(e, "a3_mx_%d" % hh)
                    ins = e.activation(out=E_bf[:], in_=T_sb[:], func=AF.Exp, bias=negmax[:, 0:1])
                mark("a3_exp_%d" % hh, sS)

                @blk.vector
                def _(e, hh=hh):
                    e = ES(e, sV)
                    wv(e, "a3_exp_%d" % hh)
                    if hh > 0:
                        wv(e, "a3_pt_%d" % (hh - 1))
                    e.reduce_sum(out=rsum[:], in_=E_bf[:].rearrange("p (s c) -> p s c", s=9), axis=AX.X)
                    e.reciprocal(out=rinv[:], in_=rsum[:])
                    for slot in range(9):
                        ins = e.tensor_scalar(out=P_bf[:, slot * 144:(slot + 1) * 144],
                                              in0=E_bf[:, slot * 144:(slot + 1) * 144],
                                              scalar1=rinv[:, slot:slot + 1], scalar2=None, op0=OP.mult)
                mark("a3_P_%d" % hh, sV)

                @blk.sync
                def _(e, hh=hh):
                    wv(e, "a3_P_%d" % hh)
                    if hh > 0:
                        wv(e, "a3_pv_%d_%d" % ((hh - 1) // 2, (hh - 1) % 2))
                    for slot in range(8):
                        ins = e.dma_start(out=PT_a[:, slot * 128:(slot + 1) * 128],
                                          in_=P_bf[:, slot * 144: slot * 144 + 128], transpose=True)
                        inc(ins, sTp, 16)
                        ins = e.dma_start(out=PT_b[:, slot * 128:(slot + 1) * 128],
                                          in_=P_bf[:, slot * 144 + 128: slot * 144 + 256], transpose=True)
                        inc(ins, sTp, 16)
                mark("a3_pt_%d" % hh, sTp)


        def emit_pv_half(pr, half):
            if True:
                hh = pr * 2 + half
                @blk.tensor
                def _(e, pr=pr, half=half, hh=hh):
                    wv(e, "a3_pt_%d" % hh)
                    wv(e, "vtok_%d" % pr)
                    if hh > 0:
                        wv(e, "a3_oev_%d" % (hh - 1))
                    for jj in range(4):
                        j = half * 4 + jj
                        for h in range(2):
                            slot = jj * 2 + h
                            out = psA[0:64, 2560 + slot * 128: 2560 + (slot + 1) * 128]
                            vt = v_tok[pr % 2]
                            e.matmul(out, vt[:, j * 128 + 64 * h: j * 128 + 64 * h + 64],
                                     PT_a[:, slot * 128:(slot + 1) * 128], start=True, stop=False)
                            ins = e.matmul(out,
                                           vt[0:16, (j + 1) * 128 + 64 * h: (j + 1) * 128 + 64 * h + 64],
                                           PT_b[0:16, slot * 128:(slot + 1) * 128], start=False, stop=True)
                    inc(ins, sM, 1)
                mark("a3_pv_%d_%d" % (pr, half), sM)

                @blk.vector
                def _(e, pr=pr, half=half):
                    e = ES(e, sV)
                    wv(e, "a3_pv_%d_%d" % (pr, half))
                    if pr >= 2 and half == 0:
                        wv(e, "at_spill_%d" % (pr - 2))
                    for jj in range(4):
                        for h in range(2):
                            slot = jj * 2 + h
                            dst = (ath0, ath1)[h]
                            ins = e.tensor_copy(out=dst[:, (pr % 2) * T + half * 512 + jj * 128:
                                                        (pr % 2) * T + half * 512 + (jj + 1) * 128],
                                                in_=psA[0:64, 2560 + slot * 128: 2560 + (slot + 1) * 128])
                mark("a3_oev_%d" % (pr * 2 + half), sV)

            if half == 1:
                @blk.scalar
                def _(e, pr=pr):
                    e = ES(e, sS)
                    wv(e, "a3_oev_%d" % (pr * 2 + 1))
                    ins = e.dma_start(out=atscr[pr][0:64, :], in_=ath0[:, (pr % 2) * T:(pr % 2) * T + T])
                    inc(ins, sTat[pr % 2], 16)
                    ins = e.dma_start(out=atscr[pr][64:128, :], in_=ath1[:, (pr % 2) * T:(pr % 2) * T + T])
                    inc(ins, sTat[pr % 2], 16)
                mark("at_spill_%d" % pr, sTat[pr % 2])


        for pr in range(NPAIR):
            emit_qkv_pair(pr)
            for half in range(2):
                hh = pr * 2 + half
                if hh >= 1:
                    emit_pv_half((hh - 1) // 2, (hh - 1) % 2)
                emit_scores_half(pr, half)

            @blk.scalar
            def _(e, pr=pr):
                e = ES(e, sS)
                nonlocal f0_s_emitted
                hi = min((pr + 1) * 12, 192)
                while f0_s_emitted < hi:
                    f0_abs_s(e, f0_s_emitted)
                    f0_s_emitted += 1

            @blk.vector
            def _(e, pr=pr):
                e = ES(e, sV)
                nonlocal f0_emitted
                hi = min((pr + 1) * 12, 192)
                while f0_emitted < hi:
                    f0_red(e, f0_emitted)
                    f0_emitted += 1
        emit_pv_half(NPAIR - 1, 1)
        @blk.scalar
        def _(e):
            e = ES(e, sS)
            nonlocal f0_s_emitted
            while f0_s_emitted < 192:
                f0_abs_s(e, f0_s_emitted)
                f0_s_emitted += 1

        @blk.vector
        def _(e):
            e = ES(e, sV)
            nonlocal f0_emitted
            while f0_emitted < 192:
                f0_red(e, f0_emitted)
                f0_emitted += 1
        mark("f0_red_all", sV)

        @blk.vector
        def _(e):
            e = ES(e, sV)
            wv(e, "f0_red_all")
            e.reduce_sum(out=tv1[:, 0:1], in_=partials[:, 0:16], axis=AX.X)
            e.reduce_sum(out=tv1[:, 1:2], in_=partials[:, 16:32], axis=AX.X)
            ins = e.reduce_sum(out=tv1[:, 2:3], in_=partials[:, 32:48], axis=AX.X)
        mark("f0_sums", sV)

        @blk.tensor
        def _(e):
            wv(e, "f0_sums")
            wv(e, "a3_oev_%d" % (NPAIR * 2 - 1))
            e.matmul(psA[:, 3072:3073], ones_f[:], tv1[:, 0:1], start=True, stop=True)
            e.matmul(psA[:, 3073:3074], ones_f[:], tv1[:, 1:2], start=True, stop=True)
            ins = e.matmul(psA[:, 3074:3075], ones_f[:], tv1[:, 2:3], start=True, stop=True)
            inc(ins, sM, 1)
        mark("f0_bc", sM)

        @blk.vector
        def _(e):
            e = ES(e, sV)
            wv(e, "f0_bc")
            nel = float(I * D)
            for wi, (ssb, isb) in enumerate(((swg, iswg), (swu, iswu), (swd, iswd))):
                e.tensor_scalar(out=isb[:], in0=psA[:, 3072 + wi:3073 + wi],
                                scalar1=1.0 / nel, scalar2=1e-5, op0=OP.mult, op1=OP.max)
                ins = e.reciprocal(out=ssb[:], in_=isb[:])
            ins = e.tensor_scalar(out=ISXU[:], in0=ISX[:], scalar1=iswu[:, 0:1],
                                  scalar2=None, op0=OP.mult)
        mark("f0_scales", sV)

        for g in range(4):
            @blk.gpsimd
            def _(e, g=g):
                e = ES(e, sP)
                if g == 0:
                    wv(e, "f0_red_all")
                else:
                    wv(e, "a4_mm_%d" % (g - 1))
                ins = e.dma_start(out=pw_bf[:], in_=wproj[g])
                inc(ins, sGp, 16)
            mark("a4_w_%d" % g, sGp)

            for kc in range(KC):
                @blk.sync
                def _(e, g=g, kc=kc):
                    wv(e, "at_spill_%d" % kc)
                    gk = g * KC + kc
                    if gk >= 2:
                        wv(e, "a4_mmk_%d" % (gk - 2))
                    ins = e.dma_start(out=atrb[:, (kc % 2) * T:(kc % 2) * T + T], in_=atscr[kc])
                    inc(ins, sXa[kc % 2], 16)
                mark("a4_rb_%d_%d" % (g, kc), sXa[kc % 2])

                @blk.tensor
                def _(e, g=g, kc=kc):
                    if kc == 0:
                        wv(e, "a4_w_%d" % g)
                        if g == 0:
                            wv(e, "f0_scales")
                        else:
                            wv(e, "a4_ev_%d" % (g - 1))
                    wv(e, "a4_rb_%d_%d" % (g, kc))
                    for mi in range(4):
                        for nt in range(2):
                            ins = e.matmul(bank(mi * 2 + nt),
                                           pw_bf[:, kc * 512 + mi * 128: kc * 512 + (mi + 1) * 128],
                                           atrb[:, (kc % 2) * T + nt * 512:
                                                (kc % 2) * T + (nt + 1) * 512],
                                           start=(kc == 0), stop=(kc == KC - 1))
                    inc(ins, sM, 1)
                mark("a4_mmk_%d" % (g * KC + kc), sM)
            mark("a4_mm_%d" % g, sM)

            @blk.vector
            def _(e, g=g):
                e = ES(e, sV)
                wv(e, "a4_mm_%d" % g)
                if g >= 1:
                    wv(e, "a4_spill_%d" % (g - 1))
                for mi in range(4):
                    for nt in range(2):
                        m = g * 4 + mi
                        ins = e.tensor_scalar(out=astg[:, mi * T + nt * 512: mi * T + (nt + 1) * 512],
                                              in0=bank(mi * 2 + nt), scalar1=pb_sb[:, m:m + 1],
                                              scalar2=None, op0=OP.add)
            mark("a4_ev_%d" % g, sV)

            @blk.scalar
            def _(e, g=g):
                e = ES(e, sS)
                wv(e, "a4_ev_%d" % g)
                ins = e.dma_start(out=ascr[g * 4:(g + 1) * 4].rearrange("k p c -> p k c"),
                                  in_=astg[:].rearrange("p (k c) -> p k c", k=4))
                inc(ins, sTa4, 16)
            mark("a4_spill_%d" % g, sTa4)

        for t in range(NT):
            @blk.sync
            def _(e, t=t):
                if t == 0:
                    wv(e, "a4_spill_3")
                else:
                    wv(e, "f1r_t_%d" % (t - 1))
                ins = e.dma_start(out=xtok2[:], in_=xh[t * 128:(t + 1) * 128, :])
                inc(ins, sXx, 16)
            mark("f1r_x_%d" % t, sXx)

            @blk.vector
            def _(e, t=t):
                e = ES(e, sV)
                wv(e, "f1r_x_%d" % t)
                ins = e.tensor_scalar(out=xn2[:], in0=xtok2[:], scalar1=ln_mean[:, t:t + 1],
                                      scalar2=ln_rstd[:, t:t + 1], op0=OP.subtract, op1=OP.mult)
            mark("f1r_xn_%d" % t, sV)

            @blk.scalar
            def _(e, t=t):
                e = ES(e, sS)
                wv(e, "f1r_xn_%d" % t)
                ins = e.activation(out=xn2[:], in_=xn2[:], func=AF.Identity,
                                   scale=ln_srow[:, t:t + 1], bias=c_round[:, 0:1])
            mark("f1r_y_%d" % t, sS)

            @blk.vector
            def _(e, t=t):
                e = ES(e, sV)
                wv(e, "f1r_y_%d" % t)
                ins = e.tensor_scalar(out=qx2[:], in0=xn2[:], scalar1=CR, scalar2=None, op0=OP.subtract)
            mark("f1r_qx_%d" % t, sV)

            @blk.scalar
            def _(e, t=t):
                e = ES(e, sS)
                wv(e, "f1r_qx_%d" % t)
                for kc in range(KC):
                    ins = e.dma_start(out=qxT[:, kc * TP + t * 128: kc * TP + (t + 1) * 128],
                                      in_=qx2[:, kc * 128:(kc + 1) * 128], transpose=True)
                    inc(ins, sTt, 16)
            mark("f1r_t_%d" % t, sTt)
        mark("f1r_all", sTt)

        ssq_pending = []
        for g in range(16):
            for which in range(2):
                lab = "%d_%d" % (g, which)
                warr = (wg_d, wu_d)[which]
                tdst = (tern_g, tern_u)[which]
                ssc = (swg, swu)[which]

                def _w_guards(e, h, g=g, which=which):
                    if which == 1:
                        wv(e, "f1_tern_%d_0_h%d" % (g, h))
                    elif g > 0:
                        wv(e, "f1_tern_%d_1_h%d" % (g - 1, h))
                    else:
                        wv(e, "f1r_all")
                        wv(e, "at_spill_%d" % (NPAIR - 1))
                        wv(e, "a4_mm_3")

                @blk.sync
                def _(e, g=g, warr=warr, _w_guards=_w_guards):
                    _w_guards(e, 0)
                    ins = e.dma_start(out=wstage[:, 0:8 * 512].rearrange("p (k c) -> p k c", k=8),
                                      in_=warr[g][0:8].rearrange("k p c -> p k c"))
                    inc(ins, sXw, 16)
                mark("f1_wdma_%s" % lab, sXw)

                @blk.gpsimd
                def _(e, g=g, warr=warr, _w_guards=_w_guards):
                    _w_guards(e, 1)
                    ins = e.dma_start(out=wstage[:, 8 * 512:16 * 512].rearrange("p (k c) -> p k c", k=8),
                                      in_=warr[g][8:16].rearrange("k p c -> p k c"))
                    inc(ins, sXw2, 16)
                mark("f1_wdma2_%s" % lab, sXw2)

                @blk.scalar
                def _(e, g=g, which=which, lab=lab, ssc=ssc):
                    e = ES(e, sS)
                    if g == 0 and which == 0:
                        wv(e, "f0_scales")
                    wv(e, "f1_wdma_%s" % lab)
                    ins = e.activation(out=wstage[:, 0:8 * 512], in_=wstage[:, 0:8 * 512],
                                       func=AF.Identity, scale=ssc[:, 0:1], bias=c_round[:, 0:1])
                    mark("f1_tern_s_%s_h0" % lab, sS)
                    wv(e, "f1_wdma2_%s" % lab)
                    ins = e.activation(out=wstage[:, 8 * 512:16 * 512], in_=wstage[:, 8 * 512:16 * 512],
                                       func=AF.Identity, scale=ssc[:, 0:1], bias=c_round[:, 0:1])
                mark("f1_tern_s_%s_h1" % lab, sS)

                @blk.vector
                def _(e, g=g, which=which, lab=lab, tdst=tdst):
                    e = ES(e, sV)
                    wv(e, "f1_tern_s_%s_h0" % lab)
                    if g > 0:
                        wv(e, "f1_mm_%d_%d" % (g - 1, which))
                    e.tensor_scalar(out=wstage[:, 0:8 * 512], in0=wstage[:, 0:8 * 512],
                                    scalar1=CR, scalar2=1.0, op0=OP.subtract, op1=OP.min)
                    ins = e.tensor_scalar(out=tdst[:, 0:8 * 512], in0=wstage[:, 0:8 * 512],
                                          scalar1=-1.0, scalar2=None, op0=OP.max)
                    mark("f1_tern_%s_h0" % lab, sV)
                    wv(e, "f1_tern_s_%s_h1" % lab)
                    e.tensor_scalar(out=wstage[:, 8 * 512:16 * 512], in0=wstage[:, 8 * 512:16 * 512],
                                    scalar1=CR, scalar2=1.0, op0=OP.subtract, op1=OP.min)
                    ins = e.tensor_scalar(out=tdst[:, 8 * 512:16 * 512], in0=wstage[:, 8 * 512:16 * 512],
                                          scalar1=-1.0, scalar2=None, op0=OP.max)
                mark("f1_tern_%s_h1" % lab, sV)
                mark("f1_tern_%s" % lab, sV)

            for mi in range(4):
                c = g * 4 + mi
                for nt in range(2):
                    u = mi * 2 + nt
                    gu = g * 8 + u
                    lab = "%d_%d" % (c, nt)

                    @blk.tensor
                    def _(e, g=g, mi=mi, nt=nt, u=u, gu=gu):
                        if u == 0:
                            wv(e, "f1_tern_%d_0" % g)
                            if g == 0:
                                wv(e, "f1r_all")
                                wv(e, "f1s_prev_none" if False else "f0_scales")
                        if gu >= 2:
                            wv(e, "f1_gg_ev_%d" % (gu - 2))
                        for kc in range(KC):
                            ins = e.matmul(bank(u % 2),
                                           tern_g[:, kc * 512 + mi * 128: kc * 512 + (mi + 1) * 128],
                                           qxT[:, kc * TP + 8 + nt * 512: kc * TP + 8 + (nt + 1) * 512],
                                           start=(kc == 0), stop=(kc == KC - 1))
                        inc(ins, sM, 1)
                    mark("f1_gp_%d" % gu, sM)

                    @blk.tensor
                    def _(e, g=g, mi=mi, nt=nt, u=u, gu=gu):
                        if u == 0:
                            wv(e, "f1_tern_%d_1" % g)
                        if gu >= 2:
                            wv(e, "f1_h_ev_%d" % (gu - 2))
                        for kc in range(KC):
                            ins = e.matmul(bank(2 + u % 2),
                                           tern_u[:, kc * 512 + mi * 128: kc * 512 + (mi + 1) * 128],
                                           qxT[:, kc * TP + 8 + nt * 512: kc * TP + 8 + (nt + 1) * 512],
                                           start=(kc == 0), stop=(kc == KC - 1))
                        inc(ins, sM, 1)
                    mark("f1_up_%d" % gu, sM)

                    @blk.vector
                    def _(e, u=u, nt=nt, gu=gu):
                        e = ES(e, sV)
                        wv(e, "f1_gp_%d" % gu)
                        ins = e.scalar_tensor_tensor(out=sg_t[:], in0=bank(u % 2), scalar=iswg[:, 0:1],
                                                     in1=ISX[:, nt * 512:(nt + 1) * 512],
                                                     op0=OP.mult, op1=OP.mult)
                    mark("f1_gg_%s" % lab, sV)
                    mark("f1_gg_ev_%d" % gu, sV)

                    @blk.scalar
                    def _(e, lab=lab):
                        e = ES(e, sS)
                        wv(e, "f1_gg_%s" % lab)
                        ins = e.activation(out=sg_t[:], in_=sg_t[:], func=AF.Silu)
                    mark("f1_sil_%s" % lab, sS)

                    @blk.vector
                    def _(e, nt=nt, lab=lab):
                        e = ES(e, sV)
                        wv(e, "f1_sil_%s" % lab)
                        ins = e.tensor_tensor(out=sgp_t[:], in0=sg_t[:],
                                              in1=ISXU[:, nt * 512:(nt + 1) * 512], op=OP.mult)
                    mark("f1_sgp_%s" % lab, sV)

                    @blk.vector
                    def _(e, g=g, mi=mi, nt=nt, u=u, gu=gu, lab=lab):
                        e = ES(e, sV)
                        wv(e, "f1_sgp_%s" % lab)
                        wv(e, "f1_up_%d" % gu)
                        if g > 0 and u <= 1:
                            wv(e, "f1_spill_%d" % (g - 1))
                        ins = e.tensor_tensor(out=hstage[:, mi * T + nt * 512: mi * T + (nt + 1) * 512],
                                              in0=bank(2 + u % 2), in1=sgp_t[:], op=OP.mult)
                    mark("f1_h_%s" % lab, sV)
                    mark("f1_h_ev_%d" % gu, sV)

                    @blk.vector
                    def _(e, g=g, mi=mi, nt=nt, gu=gu, lab=lab):
                        e = ES(e, sV)
                        wv(e, "f1_h_%s" % lab)
                        if gu >= 2:
                            wv(e, "f1_ssq_%d_%d" % ((gu - 2) // 2, (gu - 2) % 2))
                        hs = hstage[:, mi * T + nt * 512: mi * T + (nt + 1) * 512]
                        ins = e.tensor_tensor(out=(hsq_t, hsq_t2)[gu % 2][:], in0=hs, in1=hs, op=OP.mult)
                    mark("f1_hsq_%s" % lab, sV)

                    def emit_ssq(gu2):
                        c2, nt2 = (gu2 // 2) , gu2 % 2
                        lab2 = "%d_%d" % (c2, nt2)

                        @blk.tensor
                        def _(e, c2=c2, nt2=nt2, gu2=gu2, lab2=lab2):
                            wv(e, "f1_hsq_%s" % lab2)
                            ins = e.matmul(psA[:, 3072 + nt2 * 512: 3072 + (nt2 + 1) * 512],
                                           ones_bf[:], (hsq_t, hsq_t2)[gu2 % 2][:],
                                           start=(c2 == 0), stop=(c2 == IC - 1))
                            inc(ins, sM, 1)
                        mark("f1_ssq_%s" % lab2, sM)
                    ssq_pending.append(emit_ssq)
                    if len(ssq_pending) >= 2:
                        ssq_pending.pop(0)(gu - 1)

                    @blk.vector
                    def _(e, c=c, mi=mi, nt=nt, gu=gu, lab=lab):
                        e = ES(e, sV)
                        wv(e, "f1_h_%s" % lab)
                        hs = hstage[:, mi * T + nt * 512: mi * T + (nt + 1) * 512]
                        e.scalar_tensor_tensor(out=Mhi[:, nt * 512:(nt + 1) * 512], in0=hs,
                                               scalar=lnw_sb[:, c:c + 1],
                                               in1=Mhi[:, nt * 512:(nt + 1) * 512],
                                               op0=OP.mult, op1=OP.max)
                        ins = e.scalar_tensor_tensor(out=Mlo[:, nt * 512:(nt + 1) * 512], in0=hs,
                                                     scalar=lnw_sb[:, c:c + 1],
                                                     in1=Mlo[:, nt * 512:(nt + 1) * 512],
                                                     op0=OP.mult, op1=OP.min)
                    mark("f1_M_%s" % lab, sV)
            mark("f1_mm_%d_0" % g, sM)
            mark("f1_mm_%d_1" % g, sM)

            @blk.scalar
            def _(e, g=g):
                e = ES(e, sS)
                for mi in range(4):
                    for nt in range(2):
                        wv(e, "f1_hsq_%d_%d" % (g * 4 + mi, nt))
                        wv(e, "f1_M_%d_%d" % (g * 4 + mi, nt))
                ins = e.dma_start(out=hscr[g * 4:(g + 1) * 4].rearrange("k p c -> p k c"),
                                  in_=hstage[:].rearrange("p (k c) -> p k c", k=4))
                inc(ins, sTh, 16)
            mark("f1_spill_%d" % g, sTh)

        if ssq_pending:
            ssq_pending.pop(0)(127)

        @blk.vector
        def _(e):
            e = ES(e, sV)
            wv(e, "f1_ssq_63_1")
            for mi in range(4):
                wv(e, "f1_M_%d_1" % (60 + mi))
            e.tensor_scalar(out=r_bc[:, 0:512], in0=psA[:, 3072:3584], scalar1=1.0 / I,
                            scalar2=1e-6, op0=OP.mult, op1=OP.add)
            ins = e.tensor_scalar(out=r_bc[:, 512:1024], in0=psA[:, 3584:4096], scalar1=1.0 / I,
                                  scalar2=1e-6, op0=OP.mult, op1=OP.add)
            e.tensor_scalar(out=Mlo[:], in0=Mlo[:], scalar1=-1.0, scalar2=None, op0=OP.mult)
            ins = e.tensor_tensor(out=Mhi[:], in0=Mhi[:], in1=Mlo[:], op=OP.max)
        mark("f1s_pre", sV)

        @blk.scalar
        def _(e):
            e = ES(e, sS)
            wv(e, "f1s_pre")
            e.activation(out=r_bc[:], in_=r_bc[:], func=AF.Ln)
            ins = e.activation(out=r_bc[:], in_=r_bc[:], func=AF.Exp, scale=c_nhalf[:, 0:1])
        mark("f1s_rbc", sS)

        @blk.tensor
        def _(e):
            wv(e, "f1s_pre")
            for b in range(8):
                if b >= 2:
                    e.wait_ge(sVm.h, b - 1)
                ins = e.transpose(psA[:, 2048 + (b % 2) * 512: 2048 + (b % 2) * 512 + 128],
                                  Mhi[:, b * 128:(b + 1) * 128], identf[:])
                inc(ins, sM, 1)
                mark("f1s_mt_%d" % b, sM)

        @blk.vector
        def _(e):
            e = ES(e, sV)
            for b in range(8):
                wv(e, "f1s_mt_%d" % b)
                ins = e.reduce_max(out=amax_tok[:, b:b + 1],
                                   in_=psA[:, 2048 + (b % 2) * 512: 2048 + (b % 2) * 512 + 128],
                                   axis=AX.X)
                ins2 = e.engine_nop()
                inc(ins2, sVm, 1)
        mark("f1s_am", sV)

        @blk.tensor
        def _(e):
            wv(e, "f1s_rbc")
            wv(e, "f1s_am")
            for b in range(8):
                ins = e.transpose(psA[:, 2048 + b * 128: 2048 + b * 128 + 128],
                                  r_bc[:, b * 128:(b + 1) * 128], identf[:])
                inc(ins, sM, 1)
                mark("f1s_rt_%d" % b, sM)

        @blk.vector
        def _(e):
            e = ES(e, sV)
            wv(e, "f1s_rt_%d" % 7)
            for b in range(8):
                ins = e.tensor_copy(out=rtok[:, b:b + 1],
                                    in_=psA[:, 2048 + b * 128: 2048 + b * 128 + 1])
            mark("f1s_rtok", sV)
            e.tensor_tensor(out=mrms[:], in0=rtok[:], in1=amax_tok[:], op=OP.mult)
            e.tensor_scalar(out=mrms[:], in0=mrms[:], scalar1=1e-5, scalar2=None, op0=OP.max)
            e.reciprocal(out=shtok[:], in_=mrms[:])
            e.tensor_scalar(out=shtok[:], in0=shtok[:], scalar1=127.0, scalar2=None, op0=OP.mult)
            e.tensor_scalar(out=ivtok[:], in0=mrms[:], scalar1=1.0 / 127.0,
                            scalar2=None, op0=OP.mult)
            ins = e.tensor_scalar(out=ivtok[:], in0=ivtok[:], scalar1=iswd[:, 0:1],
                                  scalar2=None, op0=OP.mult)
        mark("f1s_tok", sV)

        for b in range(8):
            for which, (colsrc, rowdst) in enumerate(((shtok, sh_row), (ivtok, iv_row))):
                lab = "%d_%d" % (b, which)

                @blk.vector
                def _(e, b=b, which=which, colsrc=colsrc, lab=lab):
                    e = ES(e, sV)
                    wv(e, "f1s_tok")
                    if b * 2 + which >= 2:
                        wv(e, "f1s_row_%d_%d" % ((b * 2 + which - 2) // 2, (b * 2 + which - 2) % 2))
                    ins = e.tensor_scalar(out=tmp128[:], in0=ones_f[:], scalar1=colsrc[:, b:b + 1],
                                          scalar2=None, op0=OP.mult)
                mark("f1s_bcc_%s" % lab, sV)

                @blk.tensor
                def _(e, b=b, which=which, lab=lab):
                    wv(e, "f1s_bcc_%s" % lab)
                    ins = e.transpose(psA[:, 2048 + ((b * 2 + which) % 2) * 512:
                                          2048 + ((b * 2 + which) % 2) * 512 + 128],
                                      tmp128[:], identf[:])
                    inc(ins, sM, 1)
                mark("f1s_ct_%s" % lab, sM)

                @blk.vector
                def _(e, b=b, which=which, rowdst=rowdst, lab=lab):
                    e = ES(e, sV)
                    wv(e, "f1s_ct_%s" % lab)
                    ins = e.tensor_copy(out=rowdst[0:1, b * 128:(b + 1) * 128],
                                        in_=psA[0:1, 2048 + ((b * 2 + which) % 2) * 512:
                                                2048 + ((b * 2 + which) % 2) * 512 + 128])
                mark("f1s_row_%s" % lab, sV)
        mark("f1s_rows", sV)

        @blk.tensor
        def _(e):
            wv(e, "f1s_rows")
            e.matmul(psA[:, 0:512], ones1[:], sh_row[0:1, 0:512], start=True, stop=True)
            e.matmul(psA[:, 512:1024], ones1[:], sh_row[0:1, 512:1024], start=True, stop=True)
            e.matmul(psA[:, 1024:1536], ones1[:], iv_row[0:1, 0:512], start=True, stop=True)
            ins = e.matmul(psA[:, 1536:2048], ones1[:], iv_row[0:1, 512:1024], start=True, stop=True)
            inc(ins, sM, 1)
        mark("f1s_bc", sM)

        @blk.vector
        def _(e):
            e = ES(e, sV)
            wv(e, "f1s_bc")
            e.tensor_tensor(out=RS[:, 0:512], in0=r_bc[:, 0:512], in1=psA[:, 0:512], op=OP.mult)
            e.tensor_tensor(out=RS[:, 512:1024], in0=r_bc[:, 512:1024], in1=psA[:, 512:1024], op=OP.mult)
            e.tensor_copy(out=INVD[:, 0:512], in_=psA[:, 1024:1536])
            ins = e.tensor_copy(out=INVD[:, 512:1024], in_=psA[:, 1536:2048])
        mark("f1s_done", sV)

        for half in range(2):
            c0 = half * 512
            for c in range(IC):
                @blk.scalar
                def _(e, half=half, c=c, c0=c0):
                    e = ES(e, sS)
                    if half == 0 and c < 2:
                        wv(e, "f1_spill_15")
                        wv(e, "f1s_done")
                    else:
                        lab = ("f2_q_%d_%d" % (half, c - 2)) if c >= 2 else \
                              ("f2_q_%d_%d" % (half - 1, IC - 2 + c))
                        wv(e, lab)
                        if half == 1 and c < 2:
                            wv(e, "f3_mm_0_%d" % (KC - 1))
                    ins = e.dma_start(out=hrb[:, (c % 2) * 512:(c % 2) * 512 + 512],
                                      in_=hscr[c][:, c0:c0 + 512])
                    inc(ins, sTr[c % 2], 16)
                mark("f2_rb_%d_%d" % (half, c), sTr[c % 2])

                @blk.vector
                def _(e, half=half, c=c, c0=c0):
                    e = ES(e, sV)
                    wv(e, "f2_rb_%d_%d" % (half, c))
                    buf = (hl_t, sgp_t)[c % 2]
                    ins = e.tensor_scalar(out=buf[:], in0=hrb[:, (c % 2) * 512:(c % 2) * 512 + 512],
                                          scalar1=lnw_sb[:, c:c + 1], scalar2=None, op0=OP.mult)
                    ins = e.tensor_tensor(out=buf[:], in0=buf[:], in1=RS[:, c0:c0 + 512], op=OP.mult)
                    ins = e.tensor_scalar(out=hq[:, c * 512:(c + 1) * 512], in0=buf[:],
                                          scalar1=CR, scalar2=CR, op0=OP.add, op1=OP.subtract)
                mark("f2_q_%d_%d" % (half, c), sV)
            mark("f2_done_%d" % half, sV)

            for b in range(4):
                @blk.sync
                def _(e, half=half, b=b):
                    if half > 0:
                        wv(e, "out_all_%d" % (half - 1))
                    else:
                        wv(e, "f1_tern_15_1")
                    tb = half * 4 + b
                    ins = e.dma_start(out=(ot0, ot1, ot2, ot3)[b][:],
                                      in_=xh[8 + tb * 128: 8 + (tb + 1) * 128, :])
                    inc(ins, sXx, 16)
                mark("f3_x_%d_%d" % (half, b), sXx)

            for m in range(KC):
                lab = "%d_%d" % (half, m)
                hm = half * KC + m

                def _wd_guards(e, h, hm=hm):
                    if hm > 0:
                        wv(e, "f3_tern_%d_%d_h%d" % ((hm - 1) // KC, (hm - 1) % KC, h))
                    else:
                        wv(e, "f2_done_0")

                @blk.sync
                def _(e, m=m, _wd_guards=_wd_guards):
                    _wd_guards(e, 0)
                    ins = e.dma_start(out=wdstage[:, 0:32 * 128].rearrange("p (k c) -> p k c", k=32),
                                      in_=wd_d[m][0:32].rearrange("k p c -> p k c"))
                    inc(ins, sXw, 16)
                mark("f3_wdma_%s" % lab, sXw)

                @blk.gpsimd
                def _(e, m=m, _wd_guards=_wd_guards):
                    _wd_guards(e, 1)
                    ins = e.dma_start(out=wdstage[:, 32 * 128:64 * 128].rearrange("p (k c) -> p k c", k=32),
                                      in_=wd_d[m][32:64].rearrange("k p c -> p k c"))
                    inc(ins, sXw2, 16)
                mark("f3_wdma2_%s" % lab, sXw2)

                @blk.scalar
                def _(e, lab=lab, hm=hm):
                    e = ES(e, sS)
                    wv(e, "f3_wdma_%s" % lab)
                    ins = e.activation(out=wdstage[:, 0:32 * 128], in_=wdstage[:, 0:32 * 128],
                                       func=AF.Identity, scale=swd[:, 0:1], bias=c_round[:, 0:1])
                    mark("f3_tern_s_%s_h0" % lab, sS)
                    wv(e, "f3_wdma2_%s" % lab)
                    ins = e.activation(out=wdstage[:, 32 * 128:64 * 128], in_=wdstage[:, 32 * 128:64 * 128],
                                       func=AF.Identity, scale=swd[:, 0:1], bias=c_round[:, 0:1])
                mark("f3_tern_s_%s_h1" % lab, sS)

                @blk.vector
                def _(e, lab=lab, hm=hm):
                    e = ES(e, sV)
                    wv(e, "f3_tern_s_%s_h0" % lab)
                    if hm > 0:
                        wv(e, "f3_mm_%d_%d" % ((hm - 1) // KC, (hm - 1) % KC))
                    e.tensor_scalar(out=wdstage[:, 0:32 * 128], in0=wdstage[:, 0:32 * 128],
                                    scalar1=CR, scalar2=1.0, op0=OP.subtract, op1=OP.min)
                    ins = e.tensor_scalar(out=tern_d[:, 0:32 * 128], in0=wdstage[:, 0:32 * 128],
                                          scalar1=-1.0, scalar2=None, op0=OP.max)
                    mark("f3_tern_%s_h0" % lab, sV)
                    wv(e, "f3_tern_s_%s_h1" % lab)
                    e.tensor_scalar(out=wdstage[:, 32 * 128:64 * 128], in0=wdstage[:, 32 * 128:64 * 128],
                                    scalar1=CR, scalar2=1.0, op0=OP.subtract, op1=OP.min)
                    ins = e.tensor_scalar(out=tern_d[:, 32 * 128:64 * 128], in0=wdstage[:, 32 * 128:64 * 128],
                                          scalar1=-1.0, scalar2=None, op0=OP.max)
                mark("f3_tern_%s_h1" % lab, sV)
                mark("f3_tern_%s" % lab, sV)

                @blk.scalar
                def _(e, half=half, m=m, hm=hm, c0=c0):
                    e = ES(e, sS)
                    if hm == 0:
                        wv(e, "a4_spill_3")
                        wv(e, "f1s_done")
                    if hm >= 2:
                        wv(e, "f3_fin_%d_%d" % ((hm - 2) // KC, (hm - 2) % KC))
                    ins = e.dma_start(out=aprb[:, (m % 2) * 512:(m % 2) * 512 + 512],
                                      in_=ascr[m][:, c0:c0 + 512])
                    inc(ins, sTap[m % 2], 16)
                mark("f3_ap_%s" % lab, sTap[m % 2])

                @blk.tensor
                def _(e, half=half, m=m, lab=lab, hm=hm):
                    wv(e, "f3_tern_%s" % lab)
                    wv(e, "f2_done_%d" % half)
                    if hm >= 2:
                        wv(e, "f3_ev_%d_%d" % ((hm - 2) // KC, (hm - 2) % KC))
                    for kc in range(IC):
                        ins = e.matmul(bank(m % 2), tern_d[:, kc * 128:(kc + 1) * 128],
                                       hq[:, kc * 512:(kc + 1) * 512],
                                       start=(kc == 0), stop=(kc == IC - 1))
                    inc(ins, sM, 1)
                mark("f3_mm_%s" % lab, sM)

                @blk.vector
                def _(e, half=half, m=m, lab=lab, c0=c0):
                    e = ES(e, sV)
                    wv(e, "f3_mm_%s" % lab)
                    wv(e, "f3_ap_%s" % lab)
                    e.tensor_tensor(out=fin_t[:], in0=bank(m % 2), in1=INVD[:, c0:c0 + 512], op=OP.mult)
                    ins = e.tensor_tensor(out=fin_t[:], in0=fin_t[:],
                                          in1=aprb[:, (m % 2) * 512:(m % 2) * 512 + 512], op=OP.add)
                mark("f3_fin_%s" % lab, sV)
                mark("f3_ev_%s" % lab, sV)

                @blk.tensor
                def _(e, half=half, m=m, lab=lab, hm=hm):
                    wv(e, "f3_fin_%s" % lab)
                    for b in range(4):
                        gidx = hm * 4 + b
                        if gidx >= 2:
                            e.wait_ge(sVt.h, gidx - 1)
                        ins = e.transpose(psA[:, 2048 + (gidx % 2) * 512: 2048 + (gidx % 2) * 512 + 128],
                                          fin_t[:, b * 128:(b + 1) * 128], identf[:])
                        inc(ins, sM, 1)
                        mark("f3_tt_%d_%d_%d" % (half, m, b), sM)

                @blk.vector
                def _(e, half=half, m=m, hm=hm):
                    e = ES(e, sV)
                    for b in range(4):
                        gidx = hm * 4 + b
                        wv(e, "f3_tt_%d_%d_%d" % (half, m, b))
                        wv(e, "f3_x_%d_%d" % (half, 3))
                        ot = (ot0, ot1, ot2, ot3)[b]
                        ins = e.tensor_tensor(out=ot[:, m * 128:(m + 1) * 128],
                                              in0=psA[:, 2048 + (gidx % 2) * 512:
                                                      2048 + (gidx % 2) * 512 + 128],
                                              in1=ot[:, m * 128:(m + 1) * 128], op=OP.add)
                        ins2 = e.engine_nop()
                        inc(ins2, sVt, 1)
                        mark("f3_ta_%d_%d_%d" % (half, m, b), sV)

            for b in range(4):
                @blk.scalar
                def _(e, half=half, b=b):
                    e = ES(e, sS)
                    for m in range(KC):
                        wv(e, "f3_ta_%d_%d_%d" % (half, m, b))
                    tb = half * 4 + b
                    ins = e.dma_start(out=y[tb * 128:(tb + 1) * 128, :],
                                      in_=(ot0, ot1, ot2, ot3)[b][:])
                    inc(ins, sO, 16)
                mark("out_%d_%d" % (half, b), sO)
            mark("out_all_%d" % half, sO)

        @blk.scalar
        def _(e):
            e = ES(e, sS)
            e.wait_ge(sO.h, sO.n)

    return nc


_NC_CACHE = None


def _get_nc():
    global _NC_CACHE
    if _NC_CACHE is None:
        _NC_CACHE = build_nc()
    return _NC_CACHE


def _host_prep(inputs):
    x = np.ascontiguousarray(np.asarray(inputs["L0_attn_x"], dtype=np.float32))
    qkv_w = np.asarray(inputs["L0_attn_qkv_w"], dtype=np.float32)
    proj_w = np.asarray(inputs["L0_attn_proj_w"], dtype=np.float32)
    proj_b = np.asarray(inputs["L0_attn_proj_b"], dtype=np.float32)
    w_gate = np.asarray(inputs["L0_ffn_w_gate"], dtype=np.float32)
    w_up = np.asarray(inputs["L0_ffn_w_up"], dtype=np.float32)
    w_down = np.asarray(inputs["L0_ffn_w_down"], dtype=np.float32)
    lnw = np.asarray(inputs["L0_ffn_ln_weight"], dtype=np.float32)

    wqkv = np.ascontiguousarray(
        qkv_w.reshape(3, 16, 128, D).transpose(1, 3, 0, 2).reshape(NPAIR, KC, 128, 384)
        .transpose(0, 2, 1, 3).reshape(NPAIR, 128, KC * 384))
    wproj = np.ascontiguousarray(
        proj_w.T.reshape(KC, 128, 4, 512).transpose(2, 0, 1, 3)
        .transpose(0, 2, 1, 3).reshape(4, 128, KC * 512))
    wg_a = np.ascontiguousarray(w_gate.T.reshape(KC, 128, 16, 512).transpose(2, 0, 1, 3))
    wu_a = np.ascontiguousarray(w_up.T.reshape(KC, 128, 16, 512).transpose(2, 0, 1, 3))
    wd_a = np.ascontiguousarray(w_down.T.reshape(IC, 128, 16, 128).transpose(2, 0, 1, 3))
    pbh = np.ascontiguousarray(proj_b.reshape(16, 128).T)
    lnwh = np.ascontiguousarray(lnw.reshape(64, 128).T)

    pp = np.arange(128)[:, None]
    ff = np.arange(SLOTW)[None, :]
    rel = ff - pp - 8
    band = (rel >= -8) & (rel < 8)
    mband = np.where(band, (8.0 - np.abs(rel) / 2.0) / 8.0, 0.0).astype(np.float32)
    mmulh = np.zeros((128, TSBW), np.float32)
    for s in range(8):
        mmulh[:, s * SLOTW:(s + 1) * SLOTW] = mband
    mmulh = mmulh.astype(ml_dtypes.bfloat16)

    in_maps = []
    for cidx in range(8):
        b, q0 = cidx // 4, (cidx % 4) * T
        xhc = np.zeros((TP, D), np.float32)
        klo = q0 - 8
        lo = max(klo, 0)
        hi = min(klo + TP, SEQ)
        xhc[lo - klo: hi - klo] = x[b, lo:hi]

        maddh = np.full((2, 128, TSBW), NEG, np.float32)
        for half in range(2):
            for jj in range(4):
                j = half * 4 + jj
                ig = q0 + j * 128 + pp
                jg = klo + j * 128 + ff
                valid = band & (jg >= 0) & (jg < SEQ)
                Aa = np.where(valid, 0.0, NEG).astype(np.float32)
                for h in range(2):
                    s = jj * 2 + h
                    maddh[half, :, s * SLOTW:(s + 1) * SLOTW] = Aa
            maddh[half, :, 8 * SLOTW] = 0.0
        maddh = maddh.astype(ml_dtypes.bfloat16)

        in_maps.append({
            "xh": xhc, "wqkv": wqkv, "wproj": wproj, "wg": wg_a, "wu": wu_a,
            "wd": wd_a, "pbh": pbh, "lnwh": lnwh, "mmulh": mmulh, "maddh": maddh,
        })
    return in_maps


def kernel(**inputs):
    in_maps = _host_prep(inputs)
    nc = _get_nc()
    res = run_bass_kernel_spmd(nc, in_maps, core_ids=list(range(8)))
    out = np.stack([r["y"] for r in res.results], axis=0)
    return out.reshape(2, SEQ, D)
